# version 05
# baseline (speedup 1.0000x reference)
import numpy as np
from contextlib import ExitStack
import concourse.bass as bass
import concourse.mybir as mybir
from concourse.bass_utils import run_bass_kernel_spmd

F32 = mybir.dt.float32
BF16 = mybir.dt.bfloat16
AF = mybir.ActivationFunctionType
ALU = mybir.AluOpType
AX = mybir.AxisListType

D = 2048
NCORES = 8
NC = D // 128
LN_EPS = 1e-5

ENGS = ("pe", "dve", "act", "pool", "sp")
WRITE_KEYS = ("out", "accum_out", "out_max", "out_indices")
DMA_SEMS = {"sp": ["spd%d" % i for i in range(8)],
            "pool": ["pld%d" % i for i in range(4)],
            "act": ["acd%d" % i for i in range(2)],
            "dve": ["dvd%d" % i for i in range(2)]}


_SEQ = [0]


class _Op:
    __slots__ = ("eng", "meth", "args", "kwargs", "deps", "need_inc", "sem", "cnt",
                 "is_dma", "seq", "paranoid")

    def __init__(self, eng, meth, args, kwargs):
        _SEQ[0] += 1
        self.seq = _SEQ[0]
        self.paranoid = False
        self.eng = eng
        self.meth = meth
        self.args = args
        self.kwargs = kwargs
        self.deps = []
        self.need_inc = False
        self.sem = None
        self.cnt = None
        self.is_dma = meth.startswith("dma")


def _aps_of(obj):
    if isinstance(obj, bass.AP):
        return [obj]
    return []


class Ctx:

    def __init__(self, nc):
        self.nc = nc
        self.es = ExitStack()
        self.q = {e: [] for e in ENGS}
        self.sem_names = ["pe", "dve", "act", "pool"] + sum(DMA_SEMS.values(), [])
        self.sems = {s: self.es.enter_context(nc.semaphore("sem_" + s))
                     for s in self.sem_names}
        self.mem = {}
        self._dma_rr = {e: 0 for e in DMA_SEMS}
        self._dma_last = {s: None for s in sum(DMA_SEMS.values(), [])}
        self._names = set()
        self._barrier_ops = []
        self._scopes = [self.es]
        self.prefix = ""
        self.paranoid_flag = False

    def sb(self, name, shape, dt):
        name = self.prefix + name
        assert name not in self._names, name
        self._names.add(name)
        return self._scopes[-1].enter_context(self.nc.sbuf_tensor(name, shape, dt))

    def psum(self, name, shape, dt=F32):
        name = self.prefix + name
        assert name not in self._names, name
        self._names.add(name)
        return self._scopes[-1].enter_context(self.nc.psum_tensor(name, shape, dt))

    def push_scope(self):
        s = ExitStack()
        self._scopes.append(s)
        return s

    def pop_scope(self):
        self.hard_barrier()
        self._scopes.pop().close()

    def scope(self):
        return ExitStack()

    def sb_s(self, scope, name, shape, dt):
        name = self.prefix + name
        assert name not in self._names, name
        self._names.add(name)
        return scope.enter_context(self.nc.sbuf_tensor(name, shape, dt))

    def close_scope(self, scope):
        self.hard_barrier()
        scope.close()

    def hard_barrier(self):
        self._barrier_ops = [q[-1] for q in self.q.values() if q]

    def _slot(self, name):
        s = self.mem.get(name)
        if s is None:
            s = self.mem[name] = {"w": None, "r": {}}
        return s

    def op(self, eng, meth, *args, **kwargs):
        if meth == "transpose" and eng == "pe":
            meth = "matmul"
            kwargs = {"out": kwargs["out"], "lhsT": kwargs["in_"],
                      "rhs": kwargs["identity"], "start": True, "stop": True}
        op = _Op(eng, meth, args, kwargs)
        op.paranoid = self.paranoid_flag
        if op.is_dma:
            pool = DMA_SEMS[eng]
            idx = self._dma_rr[eng] % len(pool)
            self._dma_rr[eng] += 1
            op.sem = pool[idx]
            prev = self._dma_last[op.sem]
            if prev is not None:
                op.deps.append(prev)
            self._dma_last[op.sem] = op
        reads, writes = [], []
        for k, v in kwargs.items():
            for ap in _aps_of(v):
                (writes if k in WRITE_KEYS else reads).append(ap)
        for i, a in enumerate(args):
            if i == 0 and meth in ("memset", "memzero"):
                writes.extend(_aps_of(a))
            else:
                reads.extend(_aps_of(a))
        if op.meth == "matmul" and not kwargs.get("start", True):
            reads.extend(_aps_of(kwargs.get("out")))
        rd_names = {a.tensor.name for a in reads}
        wr_names = {a.tensor.name for a in writes}
        for b in self._barrier_ops:
            self._dep(op, b)
        for n in rd_names:
            s = self._slot(n)
            if s["w"] is not None:
                self._dep(op, s["w"])
        for n in wr_names:
            s = self._slot(n)
            if s["w"] is not None:
                self._dep(op, s["w"])
            for r in s["r"].values():
                self._dep(op, r)
        for n in rd_names:
            s = self._slot(n)
            key = op.sem if op.is_dma else op.eng
            s["r"][key] = op
        for n in wr_names:
            self.mem[n] = {"w": op, "r": {}}
        self.q[eng].append(op)
        return op

    def _dep(self, op, prod):
        if prod is op:
            return
        if prod.eng == "pe" and op.eng == "pe" and not prod.is_dma and not op.is_dma:
            return
        op.deps.append(prod)

    def finish(self, paranoid=False):
        import bisect
        any_par = paranoid or any(op.paranoid for e in ENGS for op in self.q[e])
        if any_par:
            for e in ENGS:
                for op in self.q[e]:
                    op.need_inc = True
        for e in ENGS:
            for op in self.q[e]:
                for d in op.deps:
                    d.need_inc = True
        counts = {s: 0 for s in self.sem_names}
        for e in ENGS:
            for op in self.q[e]:
                if op.is_dma:
                    counts[op.sem] += 16
                    op.cnt = counts[op.sem]
                elif op.need_inc:
                    op.sem = op.eng
                    counts[op.sem] += 1
                    op.cnt = counts[op.sem]
        self.final_counts = counts
        pred = {}
        if any_par:
            allops = sorted((op.seq, op) for e in ENGS for op in self.q[e])
            last = None
            for seq, op in allops:
                pred[seq] = last
                if op.sem is not None:
                    last = op
        with self.nc.Block() as block:
            for eng, attr in (("pe", "tensor"), ("dve", "vector"), ("act", "scalar"),
                              ("pool", "gpsimd"), ("sp", "sync")):
                queue = self.q[eng]

                def body(E, queue=queue, eng=eng):
                    hw = {}
                    for op in queue:
                        want = {}
                        for d in op.deps:
                            if d.sem is None:
                                continue
                            want[d.sem] = max(want.get(d.sem, 0), d.cnt)
                        if paranoid or op.paranoid:
                            p = pred.get(op.seq)
                            if p is not None:
                                want[p.sem] = max(want.get(p.sem, 0), p.cnt)
                            if op.sem is not None:
                                cap = op.cnt - (16 if op.is_dma else 1)
                                if want.get(op.sem, 0) > cap:
                                    want[op.sem] = cap
                        for s, v in sorted(want.items()):
                            if hw.get(s, 0) < v:
                                E.wait_ge(self.sems[s], v)
                                hw[s] = v
                        instr = getattr(E, op.meth)(*op.args, **op.kwargs)
                        if op.sem is not None and (op.is_dma or op.need_inc):
                            instr.then_inc(self.sems[op.sem], 16 if op.is_dma else 1)
                    for s in DMA_SEMS.get(eng, []):
                        if counts[s] > 0:
                            E.wait_ge(self.sems[s], counts[s])
                getattr(block, attr)(body)
        self.es.close()



def consts_host():
    pack = np.zeros((128, 1024), np.float32)
    pack[:, 0:128] = np.eye(128, dtype=np.float32)
    s = np.arange(128)[:, None]
    t = np.arange(128)[None, :]
    pack[:, 128:256] = (s <= t).astype(np.float32)
    pack[:, 256:384] = (s < t).astype(np.float32)
    pack[:, 512] = 1.0
    pack[:64, 515] = 1.0
    pack[64:, 516] = 1.0
    pack[:, 520] = 1e-5
    pack[:, 521] = 1.1920929e-7
    pack[:, 522] = 1e-6
    pack[:, 523] = 128.0 ** -0.5
    pack[:, 524] = 1e-12
    pack[:, 525] = 0.5
    pack[:, 526] = 1.0
    p16 = np.arange(128)[:, None] // 16
    c16 = np.arange(128)[None, :] // 16
    po = np.arange(128)[:, None] % 16
    co = np.arange(128)[None, :] % 16
    pack[:, 384:512] = ((p16 == c16) & (po < co)).astype(np.float32)
    pack[:, 896:1024] = ((p16 == c16) & (po <= co)).astype(np.float32)
    pack[0, 640:704] = 1.0
    pack[1, 704:768] = 1.0
    pack[0, 768:896] = 1.0
    return pack


def make_consts(ctx, lite=False, blk=False):
    nc = ctx.nc
    c = {}
    c["ident_f"] = ctx.sb("c_ident_f", [128, 128], F32)
    c["ident_b"] = ctx.sb("c_ident_b", [128, 128], BF16)
    c["ones_col_b"] = ctx.sb("c_ones_col_b", [128, 1], BF16)
    c["ones_col_f"] = ctx.sb("c_ones_col_f", [128, 1], F32)
    c["ones_row_f"] = ctx.sb("c_ones_row_f", [1, 128], F32)
    c["ones_row_b"] = ctx.sb("c_ones_row_b", [1, 128], BF16)
    c["mask_incl_f"] = ctx.sb("c_mask_incl_f", [128, 128], F32)
    c["cv"] = ctx.sb("c_cv", [128, 16], F32)
    dr = getattr(ctx, "_consts_dram", None)
    if dr is None:
        dr = nc.dram_tensor("consts_pack", [128, 1024], F32, kind="ExternalInput").ap()
    ctx.op("sp", "dma_start", out=c["ident_f"][:], in_=dr[:, 0:128])
    ctx.op("sp", "dma_start", out=c["mask_incl_f"][:], in_=dr[:, 128:256])
    ctx.op("sp", "dma_start", out=c["ones_col_f"][:], in_=dr[:, 512:513],
           allow_slow_non_contiguous=True)
    ctx.op("sp", "dma_start", out=c["ones_row_f"][:], in_=dr[0:1, 768:896])
    ctx.op("sp", "dma_start", out=c["cv"][:], in_=dr[:, 520:536])
    ctx.op("pool", "dma_start", out=c["ident_b"][:], in_=dr[:, 0:128])
    ctx.op("pool", "dma_start", out=c["ones_col_b"][:], in_=dr[:, 512:513],
           allow_slow_non_contiguous=True)
    ctx.op("pool", "dma_start", out=c["ones_row_b"][:], in_=dr[0:1, 768:896])
    c["_dr"] = dr
    return c


def load_ffnq_consts(ctx, c):
    dr = c["_dr"]
    g2 = ctx.sb("c_g2_b", [128, 2], BF16)
    sel = ctx.sb("c_sel_half_b", [2, 128], BF16)
    ctx.op("pool", "dma_start", out=g2[:], in_=dr[:, 515:517],
           allow_slow_non_contiguous=True)
    ctx.op("pool", "dma_start", out=sel[:], in_=dr[0:2, 640:768])
    return dict(c, g2_b=g2, sel_half_b=sel)


def load_blk_consts(ctx, c):
    dr = c["_dr"]
    tb = ctx.sb("c_tblk", [128, 128], F32)
    pb_ = ctx.sb("c_pblk", [128, 128], F32)
    ctx.op("sp", "dma_start", out=tb[:], in_=dr[:, 384:512])
    ctx.op("sp", "dma_start", out=pb_[:], in_=dr[:, 896:1024])
    return dict(c, tblk=tb, pblk=pb_)



def emit_ln(ctx, c, sc, ps_stat, ps_b0, ps_b1, X, XBF, TT,
            out_bf=None, resid=None, out_f32=None):
    sq = sc["sq"]
    row = sc["row"]
    tmp = sc["tmp"]
    m_row = row[0:1, 0:TT]
    r_row = row[0:1, TT:2 * TT]
    t_row = row[0:1, 2 * TT:3 * TT]
    for i in range(NC):
        b = i % 2
        ctx.op("act", "activation", out=sq[b][:], in_=X[i][:], func=AF.Square)
        ctx.op("pe", "matmul", out=ps_stat[0:1, 0:TT], lhsT=c["ones_col_b"][:], rhs=XBF[i][:],
               start=(i == 0), stop=(i == NC - 1))
        ctx.op("pe", "matmul", out=ps_b0[0:1, 0:TT], lhsT=c["ones_col_b"][:], rhs=sq[b][:],
               start=(i == 0), stop=(i == NC - 1))
    ctx.op("dve", "tensor_scalar_mul", out=m_row, in0=ps_stat[0:1, 0:TT], scalar1=1.0 / D)
    ctx.op("act", "activation", out=t_row, in_=m_row, func=AF.Square)
    ctx.op("dve", "scalar_tensor_tensor", out=r_row, in0=ps_b0[0:1, 0:TT], scalar=1.0 / D,
           in1=t_row, op0=ALU.mult, op1=ALU.subtract)
    ctx.op("act", "activation", out=r_row, in_=r_row, func=AF.Sqrt, bias=c["cv"][0:1, 0:1])
    ctx.op("dve", "reciprocal", out=r_row, in_=r_row)
    ctx.op("pe", "matmul", out=ps_b0[:, 0:TT], lhsT=c["ones_row_f"][:], rhs=m_row, start=True, stop=True)
    ctx.op("pe", "matmul", out=ps_b1[:, 0:TT], lhsT=c["ones_row_f"][:], rhs=r_row, start=True, stop=True)
    for i in range(NC):
        b = i % 2
        ctx.op("dve", "tensor_sub", out=tmp[b][:], in0=X[i][:], in1=ps_b0[:, 0:TT])
        if out_bf is not None:
            ctx.op("dve", "tensor_mul", out=out_bf[i][:], in0=tmp[b][:], in1=ps_b1[:, 0:TT])
        else:
            ctx.op("dve", "tensor_mul", out=tmp[b][:], in0=tmp[b][:], in1=ps_b1[:, 0:TT])
            ctx.op("dve", "tensor_add", out=out_f32[i][:], in0=resid[i][:], in1=tmp[b][:])


def emit_ln_stream(ctx, c, sc, ps_stat, ps_b0, ps_b1, src, tsl, XB2, out_bf, TT):
    sq = sc["sq"]
    row = sc["row"]
    m_row = row[0:1, 0:TT]
    r_row = row[0:1, TT:2 * TT]
    t_row = row[0:1, 2 * TT:3 * TT]
    for i in range(NC):
        b = i % 2
        ctx.op("pool", "dma_start", out=XB2[b][:], in_=src[i * 128:(i + 1) * 128, tsl])
        ctx.op("act", "activation", out=sq[b][:], in_=XB2[b][:], func=AF.Square)
        ctx.op("pe", "matmul", out=ps_stat[0:1, 0:TT], lhsT=c["ones_col_b"][:], rhs=XB2[b][:],
               start=(i == 0), stop=(i == NC - 1))
        ctx.op("pe", "matmul", out=ps_b0[0:1, 0:TT], lhsT=c["ones_col_b"][:], rhs=sq[b][:],
               start=(i == 0), stop=(i == NC - 1))
    ctx.op("dve", "tensor_scalar_mul", out=m_row, in0=ps_stat[0:1, 0:TT], scalar1=1.0 / D)
    ctx.op("act", "activation", out=t_row, in_=m_row, func=AF.Square)
    ctx.op("dve", "scalar_tensor_tensor", out=r_row, in0=ps_b0[0:1, 0:TT], scalar=1.0 / D,
           in1=t_row, op0=ALU.mult, op1=ALU.subtract)
    ctx.op("act", "activation", out=r_row, in_=r_row, func=AF.Sqrt, bias=c["cv"][0:1, 0:1])
    ctx.op("dve", "reciprocal", out=r_row, in_=r_row)
    ctx.op("pe", "matmul", out=ps_b0[:, 0:TT], lhsT=c["ones_row_f"][:], rhs=m_row, start=True, stop=True)
    ctx.op("pe", "matmul", out=ps_b1[:, 0:TT], lhsT=c["ones_row_f"][:], rhs=r_row, start=True, stop=True)
    for i in range(NC):
        b = i % 2
        ctx.op("pool", "dma_start", out=XB2[b][:], in_=src[i * 128:(i + 1) * 128, tsl])
        ctx.op("dve", "tensor_sub", out=sc["tmp"][b][:], in0=XB2[b][:], in1=ps_b0[:, 0:TT])
        ctx.op("dve", "tensor_mul", out=out_bf[i][:], in0=sc["tmp"][b][:], in1=ps_b1[:, 0:TT])


def emit_tile_load(ctx, X, XBF, src, tsl):
    for i in range(NC):
        ctx.op("sp", "dma_start", out=X[i][:], in_=src[i * 128:(i + 1) * 128, tsl])
        ctx.op("pool", "dma_start", out=XBF[i][:], in_=src[i * 128:(i + 1) * 128, tsl])


def emit_slab_mm(ctx, ps, WQ, wsrc, col0, rhs, TT, n_oc=4, ncols=None, queue="sp"):
    nk = len(rhs)
    if ncols is None:
        ncols = n_oc * 128
    for kk in range(nk):
        b = kk % 2
        ctx.op(queue, "dma_start", out=WQ[b][:, 0:ncols],
               in_=wsrc[kk * 128:(kk + 1) * 128, col0:col0 + ncols])
        for oc in range(n_oc):
            w = min(128, ncols - oc * 128)
            ctx.op("pe", "matmul", out=ps[oc][0:w if w < 128 else 128, 0:TT],
                   lhsT=WQ[b][:, oc * 128:oc * 128 + w], rhs=rhs[kk][:],
                   start=(kk == 0), stop=(kk == nk - 1))



def build_l1ffn(TT, tiles, nc=None, ctx=None, c=None, rx=None, ox_=None):
    T = sum(tiles)
    standalone = nc is None
    if standalone:
        nc = bass.Bass("TRN2", target_bir_lowering=False, debug=False)
    if rx is None:
        rx = nc.dram_tensor("rx", [D, T], F32, kind="ExternalInput").ap()
    wupT = nc.dram_tensor("W_upT", [2048, 8192], F32, kind="ExternalInput").ap()
    wgateT = nc.dram_tensor("W_gateT", [2048, 8192], F32, kind="ExternalInput").ap()
    wdownT = nc.dram_tensor("W_downT", [8192, 2048], F32, kind="ExternalInput").ap()
    if ox_ is None:
        ox = nc.dram_tensor("ox", [D, T], F32, kind="ExternalOutput").ap()
    else:
        ox = ox_
    qup = nc.dram_tensor("qup", [2048, 8192], BF16, kind="Internal").ap()
    qgate = nc.dram_tensor("qgate", [2048, 8192], BF16, kind="Internal").ap()
    qdown = nc.dram_tensor("qdown", [8192, 2048], BF16, kind="Internal").ap()
    if ctx is None:
        ctx = Ctx(nc)
        c = make_consts(ctx)
    ctx.push_scope()
    ctx.prefix = "l1f_"
    ps = [ctx.psum("ps%d" % i, [128, 512], F32) for i in range(8)]

    srow = ctx.sb("srow", [1, 8], F32)
    grow = ctx.sb("grow", [1, 16], F32)
    scol = ctx.sb("scol", [128, 8], F32)
    WQ = [ctx.sb("wq%d" % i, [128, 512], BF16) for i in range(2)]
    qsc = ctx.scope()
    WF = [ctx.sb_s(qsc, "wf%d" % i, [128, 512], F32) for i in range(2)]
    AB = [ctx.sb_s(qsc, "ab%d" % i, [128, 512], BF16) for i in range(2)]
    SG = [ctx.sb_s(qsc, "sgn%d" % i, [128, 512], F32) for i in range(2)]
    MK = [ctx.sb_s(qsc, "msk%d" % i, [128, 512], F32) for i in range(2)]

    mats = [(wupT, qup, 2048, 8192, 0), (wgateT, qgate, 2048, 8192, 1),
            (wdownT, qdown, 8192, 2048, 2)]
    sbank = [ps[7], ps[6], ps[5]]

    for (w, q, K, O, mi) in mats:
        nk, no = K // 128, O // 512
        ntile = nk * no
        ti = 0
        for ko in range(no):
            for kk in range(nk):
                b = ti % 2
                ctx.op("sp", "dma_start", out=WF[b][:],
                       in_=w[kk * 128:(kk + 1) * 128, ko * 512:(ko + 1) * 512])
                ctx.op("act", "activation", out=AB[b][:], in_=WF[b][:], func=AF.Abs)
                ctx.op("pe", "matmul", out=sbank[mi][0:1, 0:512], lhsT=c["ones_col_b"][:],
                       rhs=AB[b][:], start=(ti == 0), stop=(ti == ntile - 1))
                ti += 1
    for (w, q, K, O, mi) in mats:
        ctx.op("dve", "tensor_reduce", out=grow[0:1, mi:mi + 1], in_=sbank[mi][0:1, 0:512],
               axis=AX.X, op=ALU.add)
        ctx.op("dve", "tensor_scalar", out=srow[0:1, mi:mi + 1], in0=grow[0:1, mi:mi + 1],
               scalar1=1.0 / (K * O), scalar2=1e-8, op0=ALU.mult, op1=ALU.add)
        ctx.op("dve", "tensor_scalar_mul", out=srow[0:1, 3 + mi:4 + mi], in0=srow[0:1, mi:mi + 1],
               scalar1=0.5)
    ctx.op("pe", "matmul", out=ps[4][:, 0:6], lhsT=c["ones_row_f"][:], rhs=srow[0:1, 0:6],
           start=True, stop=True)
    ctx.op("dve", "tensor_copy", out=scol[:, 0:6], in_=ps[4][:, 0:6])

    for (w, q, K, O, mi) in mats:
        nk, no = K // 128, O // 512
        for ko in range(no):
            for kk in range(nk):
                b = kk % 2
                ctx.op("sp", "dma_start", out=WF[b][:],
                       in_=w[kk * 128:(kk + 1) * 128, ko * 512:(ko + 1) * 512])
                ctx.op("act", "activation", out=SG[b][:], in_=WF[b][:], func=AF.Sign)
                ctx.op("act", "activation", out=MK[b][:], in_=WF[b][:], func=AF.Abs)
                ctx.op("dve", "tensor_single_scalar", out=MK[b][:], in_=MK[b][:],
                       scalar=scol[:, 3 + mi:4 + mi], op=ALU.is_gt)
                ctx.op("dve", "tensor_mul", out=WQ[b][:], in0=MK[b][:], in1=SG[b][:])
                ctx.op("sp", "dma_start",
                       out=q[kk * 128:(kk + 1) * 128, ko * 512:(ko + 1) * 512],
                       in_=WQ[b][:])

    ctx.close_scope(qsc)
    X = [ctx.sb("x%d" % i, [128, TT], F32) for i in range(NC)]
    XBF = [ctx.sb("xb%d" % i, [128, TT], BF16) for i in range(NC)]
    XN = [ctx.sb("xn%d" % i, [128, TT], BF16) for i in range(NC)]
    FO = [ctx.sb("fo%d" % i, [128, TT], F32) for i in range(NC)]
    FOB = [ctx.sb("fob%d" % i, [128, TT], BF16) for i in range(NC)]
    H = [ctx.sb("h%d" % i, [128, TT], BF16) for i in range(64)]
    SU = [ctx.sb("su%d" % i, [128, TT], BF16) for i in range(4)]
    sc = {"sq": [ctx.sb("ln_sq0", [128, TT], BF16), ctx.sb("ln_sq1", [128, TT], BF16)],
          "row": ctx.sb("ln_row", [1, 3 * TT], F32),
          "tmp": [ctx.sb("ln_tmp0", [128, TT], F32), ctx.sb("ln_tmp1", [128, TT], F32)]}

    t0 = 0
    for it, TTt in enumerate(tiles):
        assert TTt == TT
        tsl = slice(t0, t0 + TTt)
        emit_tile_load(ctx, X, XBF, rx, tsl)
        emit_ln(ctx, c, sc, ps[4], ps[5], ps[6], X, XBF, TT, out_bf=XN)

        for j in range(16):
            for phase in (0, 1):
                qsrc = qup if phase == 0 else qgate
                emit_slab_mm(ctx, ps, WQ, qsrc, j * 512, XN, TT)
                if phase == 0:
                    for oc in range(4):
                        ctx.op("act", "activation", out=SU[oc][:], in_=ps[oc][:, 0:TT],
                               func=AF.Silu, scale=scol[:, 0:1])
                else:
                    for oc in range(4):
                        ctx.op("dve", "scalar_tensor_tensor", out=H[j * 4 + oc][:],
                               in0=ps[oc][:, 0:TT], scalar=scol[:, 1:2], in1=SU[oc][:],
                               op0=ALU.mult, op1=ALU.mult)

        for j in range(4):
            emit_slab_mm(ctx, ps, WQ, qdown, j * 512, H, TT)
            for oc in range(4):
                i = j * 4 + oc
                ctx.op("dve", "tensor_scalar_mul", out=FO[i][:], in0=ps[oc][:, 0:TT],
                       scalar1=scol[:, 2:3])
                ctx.op("pool", "tensor_copy", out=FOB[i][:], in_=FO[i][:])

        emit_ln(ctx, c, sc, ps[4], ps[5], ps[6], FO, FOB, TT, resid=X, out_f32=X)
        for i in range(NC):
            ctx.op("sp", "dma_start", out=ox[i * 128:(i + 1) * 128, tsl], in_=X[i][:])
        t0 += TTt

    ctx.prefix = ""
    ctx.pop_scope()
    if standalone:
        ctx.finish()
    return nc





def emit_group_quant(ctx, c, ps_gs, ps_sct, WB, AB, AV, SGV, MK, SCB, RSC, wsrc, qdst, K, O):
    nk, no = K // 128, O // 512
    for ko in range(no):
        for kk in range(nk):
            b = (ko * nk + kk) % 2
            ctx.op("pool", "dma_start", out=WB[b][:],
                   in_=wsrc[kk * 128:(kk + 1) * 128, ko * 512:(ko + 1) * 512])
            ctx.op("act", "activation", out=AB[b][:], in_=WB[b][:], func=AF.Abs)
            ctx.op("pe", "matmul", out=ps_gs[0:2, 0:512], lhsT=c["g2_b"][:], rhs=AB[b][:],
                   start=True, stop=True)
            ctx.op("dve", "tensor_scalar", out=SCB[:], in0=ps_gs[0:2, 0:512],
                   scalar1=1.0 / 64, scalar2=1e-8, op0=ALU.mult, op1=ALU.max)
            ctx.op("pe", "matmul", out=ps_sct[:, 0:512], lhsT=c["sel_half_b"][:], rhs=SCB[:],
                   start=True, stop=True)
            ctx.op("dve", "reciprocal", out=RSC[:], in_=ps_sct[:, 0:512])
            ctx.op("dve", "tensor_mul", out=AV[b][:], in0=WB[b][:], in1=RSC[:])
            ctx.op("act", "activation", out=SGV[b][:], in_=AV[b][:], func=AF.Sign)
            ctx.op("act", "activation", out=AV[b][:], in_=AV[b][:], func=AF.Abs)
            ctx.op("dve", "tensor_single_scalar", out=MK[b][:], in_=AV[b][:],
                   scalar=0.5, op=ALU.is_gt)
            ctx.op("dve", "tensor_mul", out=MK[b][:], in0=MK[b][:], in1=SGV[b][:])
            ctx.op("dve", "tensor_tensor", out=MK[b][:], in0=MK[b][:], in1=ps_sct[:, 0:512],
                   op=ALU.mult)
            ctx.op("sp", "dma_start",
                   out=qdst[kk * 128:(kk + 1) * 128, ko * 512:(ko + 1) * 512], in_=MK[b][:])


def build_l0ffn(TT, tiles, nc=None, ctx=None, c=None, rx=None, ox_=None):
    T = sum(tiles)
    standalone = nc is None
    if standalone:
        nc = bass.Bass("TRN2", target_bir_lowering=False, debug=False)
    if rx is None:
        rx = nc.dram_tensor("rx", [D, T], F32, kind="ExternalInput").ap()
    wguT = nc.dram_tensor("w_guT", [2048, 16384], F32, kind="ExternalInput").ap()
    wpT = nc.dram_tensor("w_pT", [8192, 2048], F32, kind="ExternalInput").ap()
    if ox_ is None:
        ox = nc.dram_tensor("ox", [D, T], F32, kind="ExternalOutput").ap()
    else:
        ox = ox_
    qgu = nc.dram_tensor("qgu", [2048, 16384], BF16, kind="Internal").ap()
    qp = nc.dram_tensor("qp", [8192, 2048], BF16, kind="Internal").ap()
    if ctx is None:
        ctx = Ctx(nc)
        c = make_consts(ctx)
    ctx.push_scope()
    ctx.prefix = "l0f_"
    c = load_ffnq_consts(ctx, c)
    ps = [ctx.psum("ps%d" % i, [128, 512], F32) for i in range(8)]
    WQ = [ctx.sb("wq%d" % i, [128, 512], BF16) for i in range(2)]

    qsc = ctx.scope()
    WB = [ctx.sb_s(qsc, "wb%d" % i, [128, 512], BF16) for i in range(2)]
    AB = [ctx.sb_s(qsc, "ab%d" % i, [128, 512], BF16) for i in range(2)]
    AV = [ctx.sb_s(qsc, "av%d" % i, [128, 512], BF16) for i in range(2)]
    SGV = [ctx.sb_s(qsc, "sgv%d" % i, [128, 512], BF16) for i in range(2)]
    MK = [ctx.sb_s(qsc, "mk%d" % i, [128, 512], BF16) for i in range(2)]
    SCB = ctx.sb_s(qsc, "scb", [2, 512], BF16)
    RSC = ctx.sb_s(qsc, "rsc", [128, 512], F32)
    emit_group_quant(ctx, c, ps[6], ps[7], WB, AB, AV, SGV, MK, SCB, RSC, wguT, qgu, 2048, 16384)
    emit_group_quant(ctx, c, ps[6], ps[7], WB, AB, AV, SGV, MK, SCB, RSC, wpT, qp, 8192, 2048)
    ctx.close_scope(qsc)

    X = [ctx.sb("x%d" % i, [128, TT], F32) for i in range(NC)]
    XBF = [ctx.sb("xb%d" % i, [128, TT], BF16) for i in range(NC)]
    XN = [ctx.sb("xn%d" % i, [128, TT], BF16) for i in range(NC)]
    FO = [ctx.sb("fo%d" % i, [128, TT], F32) for i in range(NC)]
    FOB = [ctx.sb("fob%d" % i, [128, TT], BF16) for i in range(NC)]
    H = [ctx.sb("h%d" % i, [128, TT], BF16) for i in range(64)]
    SU = [ctx.sb("su%d" % i, [128, TT], BF16) for i in range(4)]
    sc = {"sq": [ctx.sb("ln_sq0", [128, TT], BF16), ctx.sb("ln_sq1", [128, TT], BF16)],
          "row": ctx.sb("ln_row", [1, 3 * TT], F32),
          "tmp": [ctx.sb("ln_tmp0", [128, TT], F32), ctx.sb("ln_tmp1", [128, TT], F32)]}
    rrow = ctx.sb("rms_row", [1, TT], F32)

    t0 = 0
    for it, TTt in enumerate(tiles):
        assert TTt == TT
        tsl = slice(t0, t0 + TTt)
        emit_tile_load(ctx, X, XBF, rx, tsl)
        emit_ln(ctx, c, sc, ps[4], ps[5], ps[6], X, XBF, TT, out_bf=XN)

        for j in range(16):
            for phase in (0, 1):
                col = j * 512 + (8192 if phase == 1 else 0)
                emit_slab_mm(ctx, ps, WQ, qgu, col, XN, TT)
                if phase == 0:
                    for oc in range(4):
                        ctx.op("act", "activation", out=SU[oc][:], in_=ps[oc][:, 0:TT],
                               func=AF.Silu)
                else:
                    for oc in range(4):
                        ctx.op("dve", "tensor_mul", out=H[j * 4 + oc][:],
                               in0=ps[oc][:, 0:TT], in1=SU[oc][:])

        for i in range(64):
            b = i % 2
            ctx.op("act", "activation", out=sc["sq"][b][:], in_=H[i][:], func=AF.Square)
            ctx.op("pe", "matmul", out=ps[4][0:1, 0:TT], lhsT=c["ones_col_b"][:],
                   rhs=sc["sq"][b][:], start=(i == 0), stop=(i == 63))
        ctx.op("dve", "tensor_scalar_mul", out=rrow[0:1, 0:TT], in0=ps[4][0:1, 0:TT],
               scalar1=1.0 / 8192)
        ctx.op("act", "activation", out=rrow[0:1, 0:TT], in_=rrow[0:1, 0:TT], func=AF.Sqrt,
               bias=c["cv"][0:1, 1:2])
        ctx.op("dve", "reciprocal", out=rrow[0:1, 0:TT], in_=rrow[0:1, 0:TT])
        ctx.op("pe", "matmul", out=ps[5][:, 0:TT], lhsT=c["ones_row_f"][:], rhs=rrow[0:1, 0:TT],
               start=True, stop=True)
        for i in range(64):
            ctx.op("dve", "tensor_mul", out=H[i][:], in0=H[i][:], in1=ps[5][:, 0:TT])

        for j in range(4):
            emit_slab_mm(ctx, ps, WQ, qp, j * 512, H, TT)
            for oc in range(4):
                i = j * 4 + oc
                ctx.op("dve", "tensor_copy", out=FO[i][:], in_=ps[oc][:, 0:TT])
                ctx.op("pool", "tensor_copy", out=FOB[i][:], in_=FO[i][:])

        emit_ln(ctx, c, sc, ps[4], ps[5], ps[6], FO, FOB, TT, resid=X, out_f32=X)
        for i in range(NC):
            ctx.op("sp", "dma_start", out=ox[i * 128:(i + 1) * 128, tsl], in_=X[i][:])
        t0 += TTt

    ctx.prefix = ""
    ctx.pop_scope()
    if standalone:
        ctx.finish()
    return nc





def emit_cumsum128(ctx, c, ps_a, ps_b, src, dst, P, ncols, tmp):
    for cc in range(ncols // 128):
        cs = slice(cc * 128, (cc + 1) * 128)
        ctx.op("pe", "transpose", out=ps_a[:, 0:P], in_=src[:, cs],
               identity=c["ident_f"][0:P, 0:P])
        ctx.op("dve", "tensor_copy", out=tmp[:, 0:P], in_=ps_a[:, 0:P])
        ctx.op("pe", "matmul", out=ps_b[:, 0:P], lhsT=c["mask_incl_f"][:], rhs=tmp[:, 0:P],
               start=True, stop=True)
        ctx.op("dve", "tensor_copy", out=tmp[:, 0:P], in_=ps_b[:, 0:P])
        ctx.op("pe", "transpose", out=ps_a[0:P, 128:256], in_=tmp[:, 0:P],
               identity=c["ident_f"][:])
        ctx.op("dve", "tensor_copy", out=dst[:, cs], in_=ps_a[0:P, 128:256])



def build_l0attn(TT, tiles, first_tile_state_zero=True, upto='full',
                 nc=None, ctx=None, c=None, rx=None, ox_=None):
    import os
    skip = set(os.environ.get("L0SKIP", "").split(","))
    T = sum(tiles)
    NCH = TT // 128
    standalone = nc is None
    if standalone:
        nc = bass.Bass("TRN2", target_bir_lowering=False, debug=False)
    if rx is None:
        rx = nc.dram_tensor("rx", [D, T], F32, kind="ExternalInput").ap()
    winT = nc.dram_tensor("W_inT", [2048, 8512], F32, kind="ExternalInput").ap()
    woutT = nc.dram_tensor("W_outT", [4096, 2048], F32, kind="ExternalInput").ap()
    cw = nc.dram_tensor("conv_w", [128, 136], F32, kind="ExternalInput").ap()
    cb = nc.dram_tensor("conv_b", [128, 34], F32, kind="ExternalInput").ap()
    dtb = nc.dram_tensor("dt_bias", [64, 1], F32, kind="ExternalInput").ap()
    alog = nc.dram_tensor("A_log", [64, 1], F32, kind="ExternalInput").ap()
    dvec = nc.dram_tensor("D_vec", [128, 32], F32, kind="ExternalInput").ap()
    nrmw = nc.dram_tensor("norm_w", [128, 32], F32, kind="ExternalInput").ap()
    if ox_ is None:
        ox = nc.dram_tensor("ox", [D, T], F32, kind="ExternalOutput").ap()
    else:
        ox = ox_
    if ctx is None:
        ctx = Ctx(nc)
        c = make_consts(ctx, lite=True)
    ctx.push_scope()
    ctx.prefix = "l0a_"
    ps = [ctx.psum("ps%d" % i, [128, 512], F32) for i in range(6)]
    pb = [ctx.psum("pb%d" % i, [128, 512], F32) for i in range(2)]

    XB2 = [ctx.sb("xb2_%d" % i, [128, TT], BF16) for i in range(2)]
    XN = [ctx.sb("xn%d" % i, [128, TT], BF16) for i in range(NC)]
    XBCC = [ctx.sb("xbcc%d" % i, [128, TT], BF16) for i in range(34)]
    YG = [ctx.sb("yg%d" % i, [128, TT], BF16) for i in range(32)]
    FO = [ctx.sb("fo%d" % i, [128, TT], F32) for i in range(NC)]
    FOB = [ctx.sb("fob%d" % i, [128, TT], BF16) for i in range(NC)]
    XT = ctx.sb("xt", [128, 4096], BF16)
    YT = ctx.sb("yt", [128, 4096], BF16)
    S = ctx.sb("state", [128, 4096], F32)
    WQ = [ctx.sb("wq%d" % i, [128, 512], BF16) for i in range(2)]
    sc = {"sq": [ctx.sb("ln_sq0", [128, TT], BF16), ctx.sb("ln_sq1", [128, TT], BF16)],
          "row": ctx.sb("ln_row", [1, 3 * TT], F32),
          "tmp": [ctx.sb("ln_tmp0", [128, TT], F32), ctx.sb("ln_tmp1", [128, TT], F32)]}
    XRW = [ctx.sb("xrw%d" % i, [128, TT + 3], BF16) for i in range(2)]
    ACC = sc["tmp"]
    TAILS = ctx.sb("tails", [128, 34 * 3], BF16)
    XR2 = sc["tmp"]
    SZ = [ctx.sb("sz%d" % i, [128, TT], BF16) for i in range(2)]
    DT1 = ctx.sb("dt1", [64, TT], F32)
    DTS = ctx.sb("dts", [64, TT], F32)
    LDA = DT1
    BCum = ctx.sb("bcum", [64, TT], F32)
    T64 = [ctx.sb("t64_0", [64, TT], F32)]
    NBT = ctx.sb("nbt", [128, 64], F32)
    EBT = ctx.sb("ebt", [128, 64], F32)
    DTT = ctx.sb("dtt", [128, 64], F32)
    DTWT = ctx.sb("dtwt", [128, 64], F32)
    EBLB = ctx.sb("eblb", [128, 64], F32)
    EBLR = ctx.sb("eblr", [1, 64], F32)
    BROW = ctx.sb("brow", [1, 128], F32)
    CBM = ctx.sb("cbm", [128, 128], F32)
    BT = ctx.sb("btr", [128, 128], BF16)
    SCF = [ctx.sb("scf%d" % i, [128, 128], F32) for i in range(2)]
    SCB = [ctx.sb("scbuf%d" % i, [128, 128], BF16) for i in range(2)]
    BD = [ctx.sb("bd%d" % i, [128, 128], BF16) for i in range(2)]
    TY = [ctx.sb("ty%d" % i, [128, 64], F32) for i in range(2)]
    S0B = [ctx.sb("s0b%d" % i, [128, 64], BF16) for i in range(2)]
    CWS = ctx.sb("cws", [128, 34 * 4], F32)
    CBS = ctx.sb("cbs", [128, 34], F32)
    DTB = ctx.sb("dtbs", [64, 1], F32)
    NEA = ctx.sb("nea", [64, 1], F32)
    DVC = ctx.sb("dvc", [128, 32], F32)
    NRM = ctx.sb("nrm", [128, 32], F32)
    rrow = ctx.sb("rms_row", [1, TT], F32)

    ctx.op("sp", "dma_start", out=CWS[:], in_=cw)
    ctx.op("sp", "dma_start", out=CBS[:], in_=cb)
    ctx.op("sp", "dma_start", out=DTB[:], in_=dtb)
    ctx.op("sp", "dma_start", out=DVC[:], in_=dvec)
    ctx.op("sp", "dma_start", out=NRM[:], in_=nrmw)
    AL = ctx.sb("als", [64, 1], F32)
    ctx.op("sp", "dma_start", out=AL[:], in_=alog)
    ctx.op("act", "activation", out=NEA[:], in_=AL[:], func=AF.Exp)
    ctx.op("dve", "tensor_scalar_mul", out=NEA[:], in0=NEA[:], scalar1=-1.0)
    if first_tile_state_zero:
        ctx.op("pool", "memset", S[:], 0.0)
        ctx.op("dve", "memset", TAILS[:], 0.0)

    t0 = 0
    for it, TTt in enumerate(tiles):
        assert TTt == TT
        tsl = slice(t0, t0 + TTt)
        emit_ln_stream(ctx, c, sc, ps[4], ps[5], ps[3], rx, tsl, XB2, XN, TT)

        slabs = [] if upto == "u1" else [(4096 + 512 * j, 4) for j in range(8)] + [(8192, 2)]
        for (col0, noc) in slabs:
            emit_slab_mm(ctx, ps, WQ, winT, col0, XN, TT, n_oc=noc, queue="pool")
            for oc in range(noc):
                ch = (col0 - 4096) // 128 + oc
                b = ch % 2
                ctx.op("dve", "tensor_copy", out=XRW[b][:, 0:3], in_=TAILS[:, ch * 3:ch * 3 + 3])
                ctx.op("dve", "tensor_copy", out=XRW[b][:, 3:3 + TT], in_=ps[oc][:, 0:TT])
                ctx.op("dve", "tensor_copy", out=TAILS[:, ch * 3:ch * 3 + 3], in_=XRW[b][:, TT:TT + 3])
                ctx.op("dve", "tensor_scalar_mul", out=ACC[b][:], in0=XRW[b][:, 0:TT],
                       scalar1=CWS[:, ch * 4:ch * 4 + 1])
                for j in (1, 2, 3):
                    ctx.op("dve", "scalar_tensor_tensor", out=ACC[b][:], in0=XRW[b][:, j:j + TT],
                           scalar=CWS[:, ch * 4 + j:ch * 4 + j + 1], in1=ACC[b][:],
                           op0=ALU.mult, op1=ALU.add)
                ctx.op("act", "activation", out=XBCC[ch][:], in_=ACC[b][:], func=AF.Silu,
                       bias=CBS[:, ch:ch + 1])
        if upto == "u1":
            for i in range(NC):
                ctx.op("pool", "dma_start", out=ox[i * 128:(i + 1) * 128, tsl], in_=XN[i][:])
            t0 += TTt
            continue
        emit_slab_mm(ctx, ps, WQ, winT, 8448, XN, TT, n_oc=1, ncols=64, queue="pool")
        ctx.op("dve", "tensor_scalar_add", out=DT1[:], in0=ps[0][0:64, 0:TT], scalar1=DTB[:, 0:1])
        ctx.op("act", "activation", out=DTS[:], in_=DT1[:], func=AF.Relu)
        ctx.op("act", "activation", out=T64[0][:], in_=DT1[:], func=AF.Abs)
        ctx.op("act", "activation", out=T64[0][:], in_=T64[0][:], func=AF.Exp, scale=-1.0)
        ctx.op("act", "activation", out=T64[0][:], in_=T64[0][:], func=AF.Ln, bias=c["cv"][0:64, 6:7])
        ctx.op("dve", "tensor_add", out=DTS[:], in0=T64[0][:], in1=DTS[:])
        ctx.op("dve", "tensor_scalar_mul", out=LDA[:], in0=DTS[:], scalar1=NEA[:, 0:1])
        emit_cumsum128(ctx, c, ps[4], ps[5], LDA, BCum, 64, TT, CBM)

        if upto in ("u1", "u2"):
            for i in range(NC):
                ctx.op("pool", "dma_start", out=ox[i * 128:(i + 1) * 128, tsl], in_=XN[i][:])
            t0 += TTt
            continue
        ctx.paranoid_flag = True
        for cc in range(NCH):
            cs = slice(cc * 128, (cc + 1) * 128)
            bL = BCum[:, cc * 128 + 127:cc * 128 + 128]
            if "t1" not in skip:
                ctx.op("pe", "transpose", out=ps[4][:, 0:64], in_=BCum[:, cs],
                       identity=c["ident_f"][0:64, 0:64])
                ctx.op("dve", "tensor_scalar_mul", out=NBT[:], in0=ps[4][:, 0:64], scalar1=-1.0)
                ctx.op("act", "activation", out=EBT[:], in_=ps[4][:, 0:64], func=AF.Exp)
                ctx.op("pe", "transpose", out=ps[4][:, 64:128], in_=DTS[:, cs],
                       identity=c["ident_f"][0:64, 0:64])
                ctx.op("dve", "tensor_copy", out=DTT[:], in_=ps[4][:, 64:128])
            if "t2" not in skip:
                ctx.op("dve", "tensor_scalar", out=T64[0][:, 0:128], in0=BCum[:, cs], scalar1=bL,
                       scalar2=-1.0, op0=ALU.subtract, op1=ALU.mult)
                ctx.op("act", "activation", out=T64[0][:, 0:128], in_=T64[0][:, 0:128], func=AF.Exp)
                ctx.op("dve", "tensor_mul", out=T64[0][:, 0:128], in0=T64[0][:, 0:128], in1=DTS[:, cs])
                ctx.op("pe", "transpose", out=ps[4][:, 128:192], in_=T64[0][:, 0:128],
                       identity=c["ident_f"][0:64, 0:64])
                ctx.op("dve", "tensor_copy", out=DTWT[:], in_=ps[4][:, 128:192])
            if "t3" not in skip:
                ctx.op("act", "activation", out=T64[0][:, 128:129], in_=bL, func=AF.Exp)
                ctx.op("pe", "matmul", out=ps[4][0:1, 192:256], lhsT=T64[0][:, 128:129],
                       rhs=c["ident_f"][0:64, 0:64], start=True, stop=True)
                ctx.op("dve", "tensor_copy", out=EBLR[:], in_=ps[4][0:1, 192:256])
                ctx.op("pe", "matmul", out=ps[4][:, 256:320], lhsT=c["ones_row_f"][:], rhs=EBLR[:],
                       start=True, stop=True)
                ctx.op("dve", "tensor_copy", out=EBLB[:], in_=ps[4][:, 256:320])
            if "t4" not in skip:
                ctx.op("pe", "matmul", out=ps[5][:, 0:128], lhsT=XBCC[32][:, cs], rhs=XBCC[33][:, cs],
                       start=True, stop=True)
                ctx.op("dve", "tensor_mul", out=CBM[:], in0=ps[5][:, 0:128], in1=c["mask_incl_f"][:])
            if "t5" not in skip:
                ctx.op("pe", "transpose", out=pb[0][:, 0:128], in_=XBCC[32][:, cs], identity=c["ident_b"][:])
                ctx.op("dve", "tensor_copy", out=BT[:], in_=pb[0][:, 0:128])
            if "t6" not in skip:
                for d in range(32):
                    bb = d % 2
                    ctx.op("pe", "transpose", out=pb[bb][:, 128:256], in_=XBCC[d][:, cs],
                           identity=c["ident_b"][:])
                    ctx.op("dve", "tensor_copy", out=XT[:, d * 128:(d + 1) * 128], in_=pb[bb][:, 128:256])
            if upto == "c0":
                continue
            for h in range(64):
                b = h % 2
                hs = slice(h * 64, (h + 1) * 64)
                psc = [ps[0], ps[2]][b]
                py = [ps[1], ps[3]][b]
                pct = [ps[4], ps[5]][b]
                ctx.op("pe", "matmul", out=psc[0:1, 128:256], lhsT=c["ident_f"][0:64, h:h + 1],
                       rhs=BCum[:, cs], start=True, stop=True)
                ctx.op("dve", "tensor_copy", out=BROW[:], in_=psc[0:1, 128:256])
                ctx.op("pe", "matmul", out=psc[:, 0:128], lhsT=c["ones_row_f"][:], rhs=BROW[:],
                       start=True, stop=True)
                ctx.op("dve", "tensor_scalar", out=SCF[b][:], in0=psc[:, 0:128],
                       scalar1=NBT[:, h:h + 1], scalar2=0.0, op0=ALU.add, op1=ALU.min)
                ctx.op("act", "activation", out=SCF[b][:], in_=SCF[b][:], func=AF.Exp)
                ctx.op("dve", "tensor_mul", out=SCF[b][:], in0=SCF[b][:], in1=CBM[:])
                ctx.op("dve", "tensor_scalar_mul", out=SCB[b][:], in0=SCF[b][:], scalar1=DTT[:, h:h + 1])
                ctx.op("dve", "tensor_copy", out=S0B[b][:], in_=S[:, hs])
                ctx.op("pe", "matmul", out=py[:, 0:64], lhsT=SCB[b][:], rhs=XT[:, h * 64:h * 64 + 64],
                       start=True, stop=True)
                ctx.op("pe", "matmul", out=py[:, 64:128], lhsT=XBCC[33][:, cs], rhs=S0B[b][:],
                       start=True, stop=True)
                ctx.op("dve", "tensor_scalar_mul", out=TY[b][:], in0=py[:, 64:128], scalar1=EBT[:, h:h + 1])
                ctx.op("dve", "tensor_add", out=YT[:, hs], in0=py[:, 0:64], in1=TY[b][:])
                ctx.op("dve", "tensor_scalar_mul", out=BD[b][:], in0=BT[:], scalar1=DTWT[:, h:h + 1])
                ctx.op("pe", "matmul", out=pct[:, 320:384], lhsT=BD[b][:], rhs=XT[:, h * 64:h * 64 + 64],
                       start=True, stop=True)
                ctx.op("dve", "scalar_tensor_tensor", out=S[:, hs], in0=S[:, hs],
                       scalar=EBLB[:, h:h + 1], in1=pct[:, 320:384], op0=ALU.mult, op1=ALU.add)
            if upto == "c1":
                continue
            for d in range(32):
                bb = d % 2
                ctx.op("pe", "transpose", out=pb[bb][:, 256:384], in_=YT[:, d * 128:(d + 1) * 128],
                       identity=c["ident_b"][:])
                ctx.op("dve", "scalar_tensor_tensor", out=YG[d][:, cs], in0=XBCC[d][:, cs],
                       scalar=DVC[:, d:d + 1], in1=pb[bb][:, 256:384], op0=ALU.mult, op1=ALU.add)

        if upto in ("u3", "c0", "c1"):
            for i in range(NC):
                ctx.op("pool", "dma_start", out=ox[i * 128:(i + 1) * 128, tsl], in_=XN[i][:])
            t0 += TTt
            continue
        for j in range(8):
            emit_slab_mm(ctx, ps, WQ, winT, j * 512, XN, TT, queue="pool")
            for oc in range(4):
                zi = j * 4 + oc
                ctx.op("act", "activation", out=SZ[oc % 2][:], in_=ps[oc][:, 0:TT], func=AF.Silu)
                ctx.op("dve", "tensor_mul", out=YG[zi][:], in0=YG[zi][:], in1=SZ[oc % 2][:])

        for i in range(32):
            b = i % 2
            ctx.op("act", "activation", out=sc["sq"][b][:], in_=YG[i][:], func=AF.Square)
            ctx.op("pe", "matmul", out=ps[4][0:1, 0:TT], lhsT=c["ones_col_b"][:],
                   rhs=sc["sq"][b][:], start=(i == 0), stop=(i == 31))
        ctx.op("dve", "tensor_scalar_mul", out=rrow[0:1, 0:TT], in0=ps[4][0:1, 0:TT],
               scalar1=1.0 / 4096)
        ctx.op("act", "activation", out=rrow[0:1, 0:TT], in_=rrow[0:1, 0:TT], func=AF.Sqrt,
               bias=c["cv"][0:1, 0:1])
        ctx.op("dve", "reciprocal", out=rrow[0:1, 0:TT], in_=rrow[0:1, 0:TT])
        ctx.op("pe", "matmul", out=ps[5][:, 0:TT], lhsT=c["ones_row_f"][:], rhs=rrow[0:1, 0:TT],
               start=True, stop=True)
        for i in range(32):
            ctx.op("dve", "scalar_tensor_tensor", out=YG[i][:], in0=YG[i][:],
                   scalar=NRM[:, i:i + 1], in1=ps[5][:, 0:TT], op0=ALU.mult, op1=ALU.mult)

        if upto == "u4":
            for i in range(NC):
                ctx.op("pool", "dma_start", out=ox[i * 128:(i + 1) * 128, tsl], in_=XN[i][:])
            t0 += TTt
            continue
        for j in range(4):
            emit_slab_mm(ctx, ps, WQ, woutT, j * 512, YG, TT, queue="pool")
            for oc in range(4):
                i = j * 4 + oc
                ctx.op("dve", "tensor_copy", out=FO[i][:], in_=ps[oc][:, 0:TT])
                ctx.op("pool", "tensor_copy", out=FOB[i][:], in_=FO[i][:])

        emit_ln_stream_add(ctx, c, sc, ps[4], ps[5], ps[3], FO, FOB, XR2, rx, ox, tsl, TT)
        t0 += TTt

    ctx.prefix = ""
    ctx.pop_scope()
    if standalone:
        ctx.finish()
    return nc


def emit_ln_stream_add(ctx, c, sc, ps_stat, ps_b0, ps_b1, FO, FOB, XR2, rx, ox, tsl, TT):
    sq = sc["sq"]
    row = sc["row"]
    m_row = row[0:1, 0:TT]
    r_row = row[0:1, TT:2 * TT]
    t_row = row[0:1, 2 * TT:3 * TT]
    for i in range(NC):
        b = i % 2
        ctx.op("act", "activation", out=sq[b][:], in_=FO[i][:], func=AF.Square)
        ctx.op("pe", "matmul", out=ps_stat[0:1, 0:TT], lhsT=c["ones_col_b"][:], rhs=FOB[i][:],
               start=(i == 0), stop=(i == NC - 1))
        ctx.op("pe", "matmul", out=ps_b0[0:1, 0:TT], lhsT=c["ones_col_b"][:], rhs=sq[b][:],
               start=(i == 0), stop=(i == NC - 1))
    ctx.op("dve", "tensor_scalar_mul", out=m_row, in0=ps_stat[0:1, 0:TT], scalar1=1.0 / D)
    ctx.op("act", "activation", out=t_row, in_=m_row, func=AF.Square)
    ctx.op("dve", "scalar_tensor_tensor", out=r_row, in0=ps_b0[0:1, 0:TT], scalar=1.0 / D,
           in1=t_row, op0=ALU.mult, op1=ALU.subtract)
    ctx.op("act", "activation", out=r_row, in_=r_row, func=AF.Sqrt, bias=c["cv"][0:1, 0:1])
    ctx.op("dve", "reciprocal", out=r_row, in_=r_row)
    ctx.op("pe", "matmul", out=ps_b0[:, 0:TT], lhsT=c["ones_row_f"][:], rhs=m_row, start=True, stop=True)
    ctx.op("pe", "matmul", out=ps_b1[:, 0:TT], lhsT=c["ones_row_f"][:], rhs=r_row, start=True, stop=True)
    for i in range(NC):
        b = i % 2
        ctx.op("sp", "dma_start", out=XR2[b][:], in_=rx[i * 128:(i + 1) * 128, tsl])
        ctx.op("dve", "tensor_sub", out=FO[i][:], in0=FO[i][:], in1=ps_b0[:, 0:TT])
        ctx.op("dve", "tensor_mul", out=FO[i][:], in0=FO[i][:], in1=ps_b1[:, 0:TT])
        ctx.op("dve", "tensor_add", out=FO[i][:], in0=FO[i][:], in1=XR2[b][:])
        ctx.op("sp", "dma_start", out=ox[i * 128:(i + 1) * 128, tsl], in_=FO[i][:])





def build_l1attn(TT, tiles, dbg=False, nc=None, ctx=None, c=None, rx=None, ox_=None):
    T = sum(tiles)
    NCH = TT // 128
    standalone = nc is None
    if standalone:
        nc = bass.Bass("TRN2", target_bir_lowering=False, debug=False)
    if rx is None:
        rx = nc.dram_tensor("rx", [D, T], F32, kind="ExternalInput").ap()
    wqT = nc.dram_tensor("WqT", [2048, 2048], F32, kind="ExternalInput").ap()
    wkT = nc.dram_tensor("WkT", [2048, 2048], F32, kind="ExternalInput").ap()
    wvT = nc.dram_tensor("WvT", [2048, 2048], F32, kind="ExternalInput").ap()
    convq = nc.dram_tensor("conv_q", [128, 64], F32, kind="ExternalInput").ap()
    convk = nc.dram_tensor("conv_k", [128, 64], F32, kind="ExternalInput").ap()
    convv = nc.dram_tensor("conv_v", [128, 64], F32, kind="ExternalInput").ap()
    aexp = nc.dram_tensor("A_exp", [128, 16], F32, kind="ExternalInput").ap()
    dtbias = nc.dram_tensor("dt_bias1", [128, 16], F32, kind="ExternalInput").ap()
    wsvT = nc.dram_tensor("WsvT", [128, 128], F32, kind="ExternalInput").ap()
    mlpw = nc.dram_tensor("mlp_w", [1, 4 * 96 + 80], F32, kind="ExternalInput").ap()
    mb2w = nc.dram_tensor("mb2_w", [32, 2], F32, kind="ExternalInput").ap()
    bb1w = nc.dram_tensor("bb1_w", [32, 2], F32, kind="ExternalInput").ap()
    wg1T = nc.dram_tensor("Wg1T", [2048, 128], F32, kind="ExternalInput").ap()
    wg2T = nc.dram_tensor("Wg2T", [128, 2048], F32, kind="ExternalInput").ap()
    bg2 = nc.dram_tensor("bg2", [128, 16], F32, kind="ExternalInput").ap()
    nwrow = nc.dram_tensor("norm_w_row", [1, 128], F32, kind="ExternalInput").ap()
    woT = nc.dram_tensor("WoT", [2048, 2048], F32, kind="ExternalInput").ap()
    if ox_ is None:
        ox = nc.dram_tensor("ox", [D, T], F32, kind="ExternalOutput").ap()
    else:
        ox = ox_
    if dbg:
        dK = nc.dram_tensor("dK", [2048, T], F32, kind="ExternalOutput").ap()
        dQ = nc.dram_tensor("dQ", [2048, T], F32, kind="ExternalOutput").ap()
        dV = nc.dram_tensor("dV", [2048, T], F32, kind="ExternalOutput").ap()
        dG = nc.dram_tensor("dG", [2048, T], F32, kind="ExternalOutput").ap()
        dBC = nc.dram_tensor("dBC", [128, 64], F32, kind="ExternalOutput").ap()
        dBQ = nc.dram_tensor("dBQ", [128, T], F32, kind="ExternalOutput").ap()
        dTS = nc.dram_tensor("dTS", [128, 128], F32, kind="ExternalOutput").ap()
        dPS = nc.dram_tensor("dPS", [128, 128], F32, kind="ExternalOutput").ap()
        dXD = nc.dram_tensor("dXD", [128, 128], F32, kind="ExternalOutput").ap()
        dSS = nc.dram_tensor("dSS", [128, 512], F32, kind="ExternalOutput").ap()
        dXD4 = nc.dram_tensor("dXD4", [128, 512], F32, kind="ExternalOutput").ap()
        dRH = nc.dram_tensor("dRH", [128, 512], F32, kind="ExternalOutput").ap()
        dO4 = nc.dram_tensor("dO4", [128, 512], F32, kind="ExternalOutput").ap()
        dTS4 = nc.dram_tensor("dTS4", [128, 512], F32, kind="ExternalOutput").ap()
        dKN = nc.dram_tensor("dKN", [128, 512], F32, kind="ExternalOutput").ap()
        dQN = nc.dram_tensor("dQN", [128, 512], F32, kind="ExternalOutput").ap()
        dOD = nc.dram_tensor("dOD", [2048, T], F32, kind="ExternalOutput").ap()

    if ctx is None:
        ctx = Ctx(nc)
        c = make_consts(ctx, lite=True, blk=True)
    ctx.push_scope()
    ctx.prefix = "l1a_"
    c = load_blk_consts(ctx, c)
    psf = [ctx.psum("psf%d" % i, [128, 512], F32) for i in range(6)]
    psb = [ctx.psum("psb%d" % i, [128, 512], F32) for i in range(2)]

    XB2 = [ctx.sb("xb2_%d" % i, [128, TT], BF16) for i in range(2)]
    XN = [ctx.sb("xn%d" % i, [128, TT], BF16) for i in range(NC)]
    QC = [ctx.sb("qc%d" % i, [128, TT], BF16) for i in range(16)]
    KC = [ctx.sb("kc%d" % i, [128, TT], BF16) for i in range(16)]
    VC = [ctx.sb("vc%d" % i, [128, TT], BF16) for i in range(16)]
    G1GATE = ctx.sb("g1gate", [128, TT], BF16)
    SGT = ctx.sb("sgt", [128, TT], BF16)
    FO = [ctx.sb("fo%d" % i, [128, TT], F32) for i in range(NC)]
    S = ctx.sb("state", [128, 2048], F32)
    OD = XN
    WQ = [ctx.sb("wq%d" % i, [128, 512], BF16) for i in range(2)]
    sc = {"sq": [ctx.sb("ln_sq0", [128, TT], BF16), ctx.sb("ln_sq1", [128, TT], BF16)],
          "row": ctx.sb("ln_row", [1, 3 * TT], F32),
          "tmp": [ctx.sb("ln_tmp0", [128, TT], F32), ctx.sb("ln_tmp1", [128, TT], F32)]}
    XRW = [ctx.sb("xrw%d" % i, [128, TT + 3], BF16) for i in range(2)]
    TLS = [ctx.sb("tls%d" % i, [128, 3 * 16], BF16) for i in range(3)]
    EHF = None
    G1 = ctx.sb("g1f", [128, TT], F32)
    MBALL = G1
    BQ = ctx.sb("bq", [128, TT], F32)
    SEXP = ctx.sb("sexp", [128, TT], F32)
    DPAIR = [ctx.sb("dpair%d" % i, [128, 2048], F32) for i in range(1)]
    TPB = ctx.sb("tpb", [128, 2048], F32)
    PPB = ctx.sb("ppb", [128, 2048], F32)
    KS1 = ctx.sb("ks1", [128, 2048], F32)
    UQ = ctx.sb("uqbuf", [128, 2 * TT], F32)
    UC = ctx.sb("ucbuf", [128, TT], F32)
    QCC = ctx.sb("qccbuf", [128, TT], F32)
    WP = ctx.sb("wpbuf", [128, TT], F32)
    WBL = [ctx.sb("wbl%d" % i, [128, 4 * 112], F32) for i in range(1)]
    TROW = ctx.sb("trow", [1, 2048], F32)
    TD = ctx.sb("tdbuf", [128, 16], F32)
    PD = ctx.sb("pdbuf", [128, 16], F32)
    TS = ctx.sb("tsbuf", [128, 128], F32)
    PS_ = ctx.sb("psbuf", [128, 128], F32)
    AB_ = ctx.sb("abuf", [128, 128], F32)
    ABT = ctx.sb("abtbuf", [128, 128], F32)
    APW = [ctx.sb("apw%d" % i, [128, 128], F32) for i in range(2)]
    APWT = [ctx.sb("apwt%d" % i, [128, 128], F32) for i in range(2)]
    XDF = ctx.sb("xdf", [128, 128], F32)
    VTS = ctx.sb("vts", [128, 128], F32)
    R1 = ctx.sb("r1f", [128, 128], F32)
    ONF = R1
    ONB = ctx.sb("onb", [128, 128], BF16)
    OCOL = ctx.sb("ocol", [128, 4], F32)
    BCOL = ctx.sb("bcol", [128, 64], F32)
    ROWS = ctx.sb("rows_s", [1, 5 * TT], F32)
    H1B = ctx.sb("h1b", [32, TT], BF16)
    NW = ctx.sb("nwb", [128, 128], F32)
    DTBC = ctx.sb("dtbc", [128, 16], F32)
    NEAC = ctx.sb("neac", [128, 16], F32)
    BG2C = ctx.sb("bg2c", [128, 16], F32)
    CWS = [ctx.sb("cws%d" % i, [128, 64], F32) for i in range(3)]
    MLPR = ctx.sb("mlpr", [1, 4 * 96 + 80], F32)
    WSV = ctx.sb("wsv", [128, 128], BF16)
    FSQ = None
    EHF = sc["tmp"][1]

    ctx.op("sp", "dma_start", out=MLPR[:], in_=mlpw)
    ctx.op("pool", "dma_start", out=WSV[:], in_=wsvT)
    for i, cv_ in enumerate((convq, convk, convv)):
        ctx.op("sp", "dma_start", out=CWS[i][:], in_=cv_)
    ctx.op("sp", "dma_start", out=DTBC[:], in_=dtbias)
    ctx.op("sp", "dma_start", out=NEAC[:], in_=aexp)
    ctx.op("sp", "dma_start", out=BG2C[:], in_=bg2)
    ctx.op("act", "activation", out=NEAC[:], in_=NEAC[:], func=AF.Exp)
    ctx.op("dve", "tensor_scalar_mul", out=NEAC[:], in0=NEAC[:], scalar1=-1.0)
    ctx.op("sp", "dma_start", out=ROWS[0:1, 0:128], in_=nwrow)
    ctx.op("pe", "matmul", out=psf[0][:, 0:128], lhsT=c["ones_row_f"][:], rhs=ROWS[0:1, 0:128],
           start=True, stop=True)
    ctx.op("dve", "tensor_copy", out=NW[:], in_=psf[0][:, 0:128])
    ctx.op("pool", "memset", S[:], 0.0)
    for i in range(3):
        ctx.op("dve", "memset", TLS[i][:], 0.0)
    W1B = MLPR[0:1, 0:96]
    W1A = MLPR[0:1, 96:192]
    MB2 = ctx.sb("mb2", [32, 2], BF16)
    ctx.op("pool", "dma_start", out=MB2[:], in_=mb2w)
    BB1 = ctx.sb("bb1c", [32, 2], F32)
    ctx.op("sp", "dma_start", out=BB1[:], in_=bb1w)

    t0 = 0
    for it, TTt in enumerate(tiles):
        assert TTt == TT
        tsl = slice(t0, t0 + TTt)
        emit_ln_stream(ctx, c, sc, psf[4], psf[5], psf[3], rx, tsl, XB2, XN, TT)

        for mi, (wT, dst) in enumerate(((wqT, QC), (wkT, KC), (wvT, VC))):
            for j in range(4):
                emit_slab_mm(ctx, psf, WQ, wT, j * 512, XN, TT, queue="pool")
                for oc in range(4):
                    ch = j * 4 + oc
                    b = ch % 2
                    ctx.op("dve", "tensor_copy", out=XRW[b][:, 0:3], in_=TLS[mi][:, ch * 3:ch * 3 + 3])
                    ctx.op("dve", "tensor_copy", out=XRW[b][:, 3:3 + TT], in_=psf[oc][:, 0:TT])
                    ctx.op("dve", "tensor_copy", out=TLS[mi][:, ch * 3:ch * 3 + 3], in_=XRW[b][:, TT:TT + 3])
                    ctx.op("dve", "tensor_scalar_mul", out=sc["tmp"][b][:], in0=XRW[b][:, 0:TT],
                           scalar1=CWS[mi][:, ch * 4:ch * 4 + 1])
                    for jj in (1, 2, 3):
                        ctx.op("dve", "scalar_tensor_tensor", out=sc["tmp"][b][:], in0=XRW[b][:, jj:jj + TT],
                               scalar=CWS[mi][:, ch * 4 + jj:ch * 4 + jj + 1], in1=sc["tmp"][b][:],
                               op0=ALU.mult, op1=ALU.add)
                    ctx.op("act", "activation", out=dst[ch][:], in_=sc["tmp"][b][:], func=AF.Silu)

        for kk in range(16):
            b = kk % 2
            ctx.op("pool", "dma_start", out=WQ[b][:, 0:128], in_=wg1T[kk * 128:(kk + 1) * 128, :])
            ctx.op("pe", "matmul", out=psf[0][:, 0:TT], lhsT=WQ[b][:, 0:128], rhs=XN[kk][:],
                   start=(kk == 0), stop=(kk == 15))
        ctx.op("dve", "tensor_copy", out=G1GATE[:], in_=psf[0][:, 0:TT])

        if dbg and it == 0:
            for ch in range(16):
                ctx.op("pool", "dma_start", out=dK[ch * 128:(ch + 1) * 128, tsl], in_=KC[ch][:])
                ctx.op("pool", "dma_start", out=dQ[ch * 128:(ch + 1) * 128, tsl], in_=QC[ch][:])
                ctx.op("pool", "dma_start", out=dV[ch * 128:(ch + 1) * 128, tsl], in_=VC[ch][:])
        for h in range(16):
            ehr = slice(0, TT)
            r_l2 = ROWS[0:1, 0:TT]
            r_l1 = ROWS[0:1, TT:2 * TT]
            r_cos = ROWS[0:1, 2 * TT:3 * TT]
            r_t1 = ROWS[0:1, 3 * TT:4 * TT]
            r_t2 = ROWS[0:1, 4 * TT:5 * TT]
            r_beta = r_t1
            r_amp = r_t2
            r_t3 = r_l2
            ctx.op("pe", "matmul", out=psf[5][:, 0:TT], lhsT=WSV[:], rhs=KC[h][:], start=True, stop=True)
            ctx.op("dve", "tensor_sub", out=EHF[:], in0=psf[5][:, 0:TT], in1=VC[h][:])
            ctx.op("act", "activation", out=SEXP[:], in_=EHF[:], func=AF.Square)
            ctx.op("pe", "matmul", out=psf[0][0:1, 0:TT], lhsT=c["ones_col_f"][:], rhs=SEXP[:], start=True, stop=True)
            ctx.op("act", "activation", out=SEXP[:], in_=EHF[:], func=AF.Abs)
            ctx.op("pe", "matmul", out=psf[1][0:1, 0:TT], lhsT=c["ones_col_f"][:], rhs=SEXP[:], start=True, stop=True)
            ctx.op("dve", "tensor_mul", out=SEXP[:], in0=psf[5][:, 0:TT], in1=VC[h][:])
            ctx.op("pe", "matmul", out=psf[2][0:1, 0:TT], lhsT=c["ones_col_f"][:], rhs=SEXP[:], start=True, stop=True)
            ctx.op("act", "activation", out=SEXP[:], in_=psf[5][:, 0:TT], func=AF.Square)
            ctx.op("pe", "matmul", out=psf[3][0:1, 0:TT], lhsT=c["ones_col_f"][:], rhs=SEXP[:], start=True, stop=True)
            ctx.op("act", "activation", out=SEXP[:], in_=VC[h][:], func=AF.Square)
            ctx.op("pe", "matmul", out=psf[4][0:1, 0:TT], lhsT=c["ones_col_f"][:], rhs=SEXP[:], start=True, stop=True)
            ctx.op("act", "activation", out=r_l2, in_=psf[0][0:1, 0:TT], func=AF.Sqrt, bias=c["cv"][0:1, 2:3])
            ctx.op("dve", "tensor_copy", out=r_l1, in_=psf[1][0:1, 0:TT])
            ctx.op("act", "activation", out=r_t1, in_=psf[3][0:1, 0:TT], func=AF.Sqrt, bias=c["cv"][0:1, 2:3])
            ctx.op("act", "activation", out=r_t2, in_=psf[4][0:1, 0:TT], func=AF.Sqrt, bias=c["cv"][0:1, 2:3])
            ctx.op("dve", "tensor_mul", out=r_t1, in0=r_t1, in1=r_t2)
            ctx.op("dve", "tensor_scalar_add", out=r_t1, in0=r_t1, scalar1=1e-6)
            ctx.op("dve", "reciprocal", out=r_t1, in_=r_t1)
            ctx.op("dve", "tensor_tensor", out=r_cos, in0=psf[2][0:1, 0:TT], in1=r_t1, op=ALU.mult)
            ctx.op("dve", "tensor_scalar", out=r_cos, in0=r_cos, scalar1=-1.0, scalar2=1.0,
                   op0=ALU.mult, op1=ALU.add)
            for mi2, (w1, w2col, bcol, outr, actf) in enumerate(
                    ((W1B, 0, 0, r_beta, AF.Sigmoid), (W1A, 1, 1, r_amp, AF.Identity))):
                for f, srow_ in enumerate((r_l2, r_l1, r_cos)):
                    ctx.op("pe", "matmul", out=psf[5][0:32, 0:TT], lhsT=w1[0:1, f * 32:(f + 1) * 32],
                           rhs=srow_, start=(f == 0), stop=(f == 2))
                ctx.op("act", "activation", out=H1B[:], in_=psf[5][0:32, 0:TT], func=AF.Silu,
                       bias=BB1[:, bcol:bcol + 1])
                ctx.op("pe", "matmul", out=psf[5][0:1, 0:TT], lhsT=MB2[:, w2col:w2col + 1], rhs=H1B[:],
                       start=True, stop=True)
                ctx.op("act", "activation", out=outr, in_=psf[5][0:1, 0:TT], func=actf,
                       bias=MLPR[0:1, 384 + w2col:385 + w2col])
            for cc2 in range(NCH):
                ctx.op("pe", "transpose", out=psf[5][:, 0:1], in_=r_beta[0:1, cc2 * 128:(cc2 + 1) * 128],
                       identity=c["ident_f"][0:1, 0:1])
                ctx.op("dve", "tensor_copy", out=BCOL[:, h * 4 + cc2:h * 4 + cc2 + 1], in_=psf[5][:, 0:1])
            ctx.op("act", "activation", out=SEXP[:], in_=KC[h][:], func=AF.Square)
            ctx.op("pe", "matmul", out=psf[0][0:1, 0:TT], lhsT=c["ones_col_f"][:], rhs=SEXP[:], start=True, stop=True)
            ctx.op("act", "activation", out=r_t1, in_=psf[0][0:1, 0:TT], func=AF.Sqrt)
            ctx.op("dve", "tensor_scalar_max", out=r_t1, in0=r_t1, scalar1=1e-12)
            ctx.op("dve", "reciprocal", out=r_t1, in_=r_t1)
            ctx.op("dve", "tensor_mul", out=r_t2, in0=r_t1, in1=r_amp)
            ctx.op("pe", "matmul", out=psf[1][:, 0:TT], lhsT=c["ones_row_f"][:], rhs=r_t2, start=True, stop=True)
            ctx.op("dve", "tensor_mul", out=G1[:], in0=KC[h][:], in1=psf[1][:, 0:TT])
            ctx.op("act", "activation", out=r_t3, in_=psf[0][0:1, 0:TT], func=AF.Sqrt, bias=c["cv"][0:1, 2:3])
            ctx.op("dve", "reciprocal", out=r_t3, in_=r_t3)
            ctx.op("pe", "matmul", out=psf[1][:, 0:TT], lhsT=c["ones_row_f"][:], rhs=r_t3, start=True, stop=True)
            ctx.op("dve", "tensor_mul", out=KC[h][:], in0=KC[h][:], in1=psf[1][:, 0:TT])
            ctx.op("act", "activation", out=SEXP[:], in_=QC[h][:], func=AF.Square)
            ctx.op("pe", "matmul", out=psf[0][0:1, 0:TT], lhsT=c["ones_col_f"][:], rhs=SEXP[:], start=True, stop=True)
            ctx.op("act", "activation", out=r_t3, in_=psf[0][0:1, 0:TT], func=AF.Sqrt, bias=c["cv"][0:1, 2:3])
            ctx.op("dve", "reciprocal", out=r_t3, in_=r_t3)
            ctx.op("dve", "tensor_scalar_mul", out=r_t3, in0=r_t3, scalar1=128.0 ** -0.5)
            ctx.op("pe", "matmul", out=psf[1][:, 0:TT], lhsT=c["ones_row_f"][:], rhs=r_t3, start=True, stop=True)
            ctx.op("dve", "tensor_mul", out=QC[h][:], in0=QC[h][:], in1=psf[1][:, 0:TT])
            ctx.op("dve", "tensor_scalar_add", out=G1[:], in0=G1[:], scalar1=DTBC[:, h:h + 1])
            ctx.op("act", "activation", out=SEXP[:], in_=G1[:], func=AF.Relu)
            ctx.op("act", "activation", out=G1[:], in_=G1[:], func=AF.Abs)
            ctx.op("act", "activation", out=G1[:], in_=G1[:], func=AF.Exp, scale=-1.0)
            ctx.op("act", "activation", out=G1[:], in_=G1[:], func=AF.Ln, bias=c["cv"][:, 6:7])
            ctx.op("dve", "tensor_add", out=G1[:], in0=G1[:], in1=SEXP[:])
            ctx.op("dve", "tensor_scalar_mul", out=G1[:], in0=G1[:], scalar1=NEAC[:, h:h + 1])
            emit_cumsum128(ctx, c, psf[2], psf[5], G1, BQ, 128, TT, R1)
            bq4 = BQ[:].rearrange("p (c x) -> p c x", c=NCH)
            ctx.op("dve", "memset", MBALL[:].rearrange("p (c x) -> p c x", c=NCH)[:, :, 0:16], 0.0)
            bq4i = BQ[:].rearrange("p (c i t) -> p c i t", c=NCH, i=8)
            ctx.op("dve", "tensor_copy",
                   out=MBALL[:].rearrange("p (c i t) -> p c i t", c=NCH, i=8)[:, :, 1:8, :],
                   in_=bq4i[:, :, 0:7, 15:16].broadcast_to([128, NCH, 7, 16]))
            ctx.op("dve", "tensor_sub", out=SEXP[:], in0=BQ[:], in1=MBALL[:])
            ctx.op("act", "activation", out=SEXP[:], in_=SEXP[:], func=AF.Exp)
            ctx.op("dve", "tensor_mul", out=UQ[:, 0:TT], in0=SEXP[:], in1=KC[h][:])
            ctx.op("dve", "tensor_mul", out=UQ[:, TT:2 * TT], in0=SEXP[:], in1=QC[h][:])
            ctx.op("act", "activation", out=SEXP[:], in_=BQ[:], func=AF.Exp)
            ctx.op("dve", "tensor_mul", out=UC[:], in0=SEXP[:], in1=KC[h][:])
            ctx.op("dve", "tensor_mul", out=QCC[:], in0=SEXP[:], in1=QC[h][:])
            ctx.op("dve", "tensor_tensor", out=SEXP[:],
                   in0=bq4[:, :, 127:128].broadcast_to([128, NCH, 128]), in1=BQ[:].rearrange("p (c x) -> p c x", c=NCH),
                   op=ALU.subtract)
            ctx.op("act", "activation", out=SEXP[:], in_=SEXP[:], func=AF.Exp)
            ctx.op("dve", "tensor_mul", out=WP[:], in0=SEXP[:], in1=KC[h][:])
            tpp = [psf[4][:, 0:256], psf[4][:, 256:512], psf[3][:, 0:256], psf[3][:, 256:512]]
            ctx.op("dve", "memset", psf[4][:, 0:512], 0.0)
            ctx.op("dve", "memset", psf[3][:, 0:512], 0.0)
            for I in range(1, 8):
                w = 16 * I
                ctx.op("dve", "tensor_tensor", out=sc["tmp"][0][:, 0:NCH * w].rearrange("p (c x) -> p c x", c=NCH),
                       in0=bq4[:, :, w - 1:w].broadcast_to([128, NCH, w]), in1=bq4[:, :, 0:w],
                       op=ALU.subtract)
                ctx.op("act", "activation", out=sc["tmp"][0][:, 0:NCH * w], in_=sc["tmp"][0][:, 0:NCH * w], func=AF.Exp)
                ctx.op("dve", "tensor_tensor", out=WBL[0][:, 0:NCH * w].rearrange("p (c x) -> p c x", c=NCH),
                       in0=sc["tmp"][0][:, 0:NCH * w].rearrange("p (c x) -> p c x", c=NCH),
                       in1=KC[h][:].rearrange("p (c x) -> p c x", c=NCH)[:, :, 0:w], op=ALU.mult)
                for cc in range(NCH):
                    ctx.op("pe", "matmul", out=tpp[cc][0:w, I * 32:I * 32 + 32],
                           lhsT=WBL[0][:, 0:NCH * w].rearrange("p (c x) -> p c x", c=NCH)[:, cc, :],
                           rhs=UQ[:].rearrange("p (u x) -> p u x", u=2)[:, :, cc * 128 + w:cc * 128 + w + 16],
                           start=True, stop=True)
            if dbg and it == 0 and h == 0:
                ctx.op("sp", "dma_start", out=dBQ[:, tsl], in_=BQ[:])
                ctx.op("pool", "dma_start", out=dKN[:], in_=KC[h][:])
                ctx.op("pool", "dma_start", out=dQN[:], in_=QC[h][:])
            for cc in range(NCH):
                cs = slice(cc * 128, (cc + 1) * 128)
                bqc = BQ[:, cs].rearrange("p (i t) -> p i t", i=8)
                kcc = KC[h][:, cs].rearrange("p (i t) -> p i t", i=8)
                qcc2 = QC[h][:, cs].rearrange("p (i t) -> p i t", i=8)
                ctx.op("dve", "tensor_tensor", out=DPAIR[0][:].rearrange("p (i s t) -> p i s t", i=8, s=16),
                       in0=bqc.unsqueeze(2).broadcast_to([128, 8, 16, 16]),
                       in1=bqc.unsqueeze(3).broadcast_to([128, 8, 16, 16]), op=ALU.subtract)
                ctx.op("dve", "tensor_scalar_min", out=DPAIR[0][:], in0=DPAIR[0][:], scalar1=0.0)
                ctx.op("act", "activation", out=DPAIR[0][:], in_=DPAIR[0][:], func=AF.Exp)
                ctx.op("dve", "tensor_tensor", out=KS1[:].rearrange("p (i s t) -> p i s t", i=8, s=16),
                       in0=DPAIR[0][:].rearrange("p (i s t) -> p i s t", i=8, s=16),
                       in1=kcc.unsqueeze(3).broadcast_to([128, 8, 16, 16]), op=ALU.mult)
                ctx.op("dve", "tensor_tensor", out=TPB[:].rearrange("p (i s t) -> p i s t", i=8, s=16),
                       in0=KS1[:].rearrange("p (i s t) -> p i s t", i=8, s=16),
                       in1=kcc.unsqueeze(2).broadcast_to([128, 8, 16, 16]), op=ALU.mult)
                ctx.op("dve", "tensor_tensor", out=PPB[:].rearrange("p (i s t) -> p i s t", i=8, s=16),
                       in0=KS1[:].rearrange("p (i s t) -> p i s t", i=8, s=16),
                       in1=qcc2.unsqueeze(2).broadcast_to([128, 8, 16, 16]), op=ALU.mult)
                for src_, dst_ in ((TPB, TD), (PPB, PD)):
                    for seg in range(4):
                        ctx.op("pe", "matmul", out=psf[seg % 2][0:1, 0:512], lhsT=c["ones_col_f"][:],
                               rhs=src_[:, seg * 512:(seg + 1) * 512], start=True, stop=True)
                        ctx.op("dve", "tensor_copy", out=TROW[0:1, seg * 512:(seg + 1) * 512],
                               in_=psf[seg % 2][0:1, 0:512])
                    ctx.op("act", "dma_start", out=dst_[:], in_=TROW[:])
                tdm = TD[:].unsqueeze(1).broadcast_to([128, 8, 16])
                pdm = PD[:].unsqueeze(1).broadcast_to([128, 8, 16])
                ctx.op("dve", "tensor_mul", out=R1[:].rearrange("p (i t) -> p i t", i=8),
                       in0=tdm, in1=c["tblk"][:].rearrange("p (i t) -> p i t", i=8))
                ctx.op("dve", "tensor_tensor", out=TS[:].rearrange("p (i t) -> p i t", i=8),
                       in0=R1[:].rearrange("p (i t) -> p i t", i=8),
                       in1=tpp[cc].rearrange("p (i u t) -> p i u t", i=8, u=2)[:, :, 0, :],
                       op=ALU.add)
                ctx.op("dve", "tensor_mul", out=R1[:].rearrange("p (i t) -> p i t", i=8),
                       in0=pdm, in1=c["pblk"][:].rearrange("p (i t) -> p i t", i=8))
                ctx.op("dve", "tensor_tensor", out=PS_[:].rearrange("p (i t) -> p i t", i=8),
                       in0=R1[:].rearrange("p (i t) -> p i t", i=8),
                       in1=tpp[cc].rearrange("p (i u t) -> p i u t", i=8, u=2)[:, :, 1, :],
                       op=ALU.add)
                if dbg and it == 0 and h == 0 and cc == 0:
                    ctx.op("sp", "dma_start", out=dTS[:], in_=TS[:])
                    ctx.op("sp", "dma_start", out=dPS[:], in_=PS_[:])
                if dbg and it == 0 and h == 0:
                    ctx.op("sp", "dma_start", out=dTS4[:, cc * 128:(cc + 1) * 128], in_=TS[:])
                ctx.op("pe", "transpose", out=psf[2][:, 0:128], in_=TS[:], identity=c["ident_f"][:])
                ctx.op("dve", "tensor_scalar", out=AB_[:], in0=psf[2][:, 0:128],
                       scalar1=BCOL[:, h * 4 + cc:h * 4 + cc + 1], scalar2=-1.0,
                       op0=ALU.mult, op1=ALU.mult)
                ctx.op("pe", "transpose", out=psf[2][:, 128:256], in_=AB_[:], identity=c["ident_f"][:])
                ctx.op("dve", "tensor_copy", out=ABT[:], in_=psf[2][:, 128:256])
                ctx.op("pe", "matmul", out=psf[5][:, 0:128], lhsT=UC[:, cs],
                       rhs=S[:, h * 128:(h + 1) * 128], start=True, stop=True)
                ctx.op("pe", "transpose", out=psb[0][:, 256:384], in_=VC[h][:, cs], identity=c["ident_b"][:])
                ctx.op("dve", "tensor_copy", out=VTS[:], in_=psb[0][:, 256:384])
                ctx.op("dve", "tensor_sub", out=R1[:], in0=VTS[:], in1=psf[5][:, 0:128])
                ctx.op("dve", "tensor_scalar_mul", out=XDF[:], in0=R1[:],
                       scalar1=BCOL[:, h * 4 + cc:h * 4 + cc + 1])
                if dbg and it == 0 and h == 0:
                    ctx.op("sp", "dma_start", out=dRH[:, cc * 128:(cc + 1) * 128], in_=XDF[:])
                apw, apwt = AB_, ABT
                for lev in range(7):
                    ctx.op("pe", "matmul", out=psf[5][:, 128:256], lhsT=apwt[:], rhs=XDF[:],
                           start=True, stop=True)
                    ctx.op("dve", "tensor_add", out=XDF[:], in0=XDF[:], in1=psf[5][:, 128:256])
                    if lev < 6:
                        nb = APW[lev % 2]
                        nbt = APWT[lev % 2]
                        ctx.op("pe", "matmul", out=psf[5][:, 256:384], lhsT=apwt[:], rhs=apw[:],
                               start=True, stop=True)
                        ctx.op("dve", "tensor_copy", out=nb[:], in_=psf[5][:, 256:384])
                        ctx.op("pe", "transpose", out=psf[2][:, 256:384], in_=nb[:], identity=c["ident_f"][:])
                        ctx.op("dve", "tensor_copy", out=nbt[:], in_=psf[2][:, 256:384])
                        apw, apwt = nb, nbt
                if dbg and it == 0 and h == 0 and cc == 0:
                    ctx.op("sp", "dma_start", out=dXD[:], in_=XDF[:])
                ctx.op("pe", "matmul", out=psf[5][:, 0:128], lhsT=QCC[:, cs],
                       rhs=S[:, h * 128:(h + 1) * 128], start=True, stop=False)
                ctx.op("pe", "matmul", out=psf[5][:, 0:128], lhsT=PS_[:], rhs=XDF[:],
                       start=False, stop=True)
                if dbg and it == 0 and h == 0:
                    ctx.op("dve", "tensor_copy", out=ONF[:], in_=psf[5][:, 0:128])
                    ctx.op("sp", "dma_start", out=dO4[:, cc * 128:(cc + 1) * 128], in_=ONF[:])
                ctx.op("act", "activation", out=ONB[:], in_=psf[5][:, 0:128], func=AF.Square)
                ctx.op("dve", "tensor_reduce", out=OCOL[:, 0:1], in_=ONB[:], axis=AX.X, op=ALU.add)
                ctx.op("act", "activation", out=OCOL[:, 1:2], in_=OCOL[:, 0:1], func=AF.Sqrt,
                       scale=1.0 / 128, bias=c["cv"][:, 0:1])
                ctx.op("dve", "reciprocal", out=OCOL[:, 1:2], in_=OCOL[:, 1:2])
                ctx.op("dve", "tensor_scalar_mul", out=ONF[:], in0=psf[5][:, 0:128],
                       scalar1=OCOL[:, 1:2])
                ctx.op("dve", "tensor_mul", out=ONB[:], in0=ONF[:], in1=NW[:])
                ctx.op("pe", "transpose", out=psb[1][:, 256:384], in_=ONB[:], identity=c["ident_b"][:])
                ctx.op("dve", "tensor_copy", out=OD[h][:, cs], in_=psb[1][:, 256:384])
                ctx.op("pe", "transpose", out=psf[2][:, 384:512], in_=WP[:, cs], identity=c["ident_f"][:])
                ctx.op("dve", "tensor_copy", out=VTS[:], in_=psf[2][:, 384:512])
                ctx.op("pe", "matmul", out=psf[5][:, 128:256], lhsT=VTS[:], rhs=XDF[:],
                       start=True, stop=True)
                ctx.op("act", "activation", out=OCOL[:, 2:3], in_=BQ[:, cc * 128 + 127:cc * 128 + 128],
                       func=AF.Exp)
                ctx.op("dve", "scalar_tensor_tensor", out=S[:, h * 128:(h + 1) * 128],
                       in0=S[:, h * 128:(h + 1) * 128], scalar=OCOL[:, 2:3],
                       in1=psf[5][:, 128:256], op0=ALU.mult, op1=ALU.add)
                if dbg and it == 0 and h == 0:
                    ctx.op("sp", "dma_start", out=dSS[:, cc * 128:(cc + 1) * 128],
                           in_=S[:, 0:128])
                    ctx.op("sp", "dma_start", out=dXD4[:, cc * 128:(cc + 1) * 128], in_=XDF[:])

        if dbg and it == 0:
            ctx.op("sp", "dma_start", out=dBC[:], in_=BCOL[:])
            for ch in range(16):
                ctx.op("pool", "dma_start", out=dOD[ch * 128:(ch + 1) * 128, tsl], in_=OD[ch][:])
        for j in range(4):
            wt = WQ[j % 2]
            ctx.op("pool", "dma_start", out=wt[:], in_=wg2T[:, j * 512:(j + 1) * 512])
            for oc in range(4):
                ch = j * 4 + oc
                ctx.op("pe", "matmul", out=psf[1 + (ch % 2)][:, 0:TT], lhsT=wt[:, oc * 128:(oc + 1) * 128],
                       rhs=G1GATE[:], start=True, stop=True)
                ctx.op("act", "activation", out=SGT[:], in_=psf[1 + (ch % 2)][:, 0:TT],
                       func=AF.Sigmoid, bias=BG2C[:, ch:ch + 1])
                ctx.op("dve", "tensor_mul", out=OD[ch][:], in0=OD[ch][:], in1=SGT[:])

        for j in range(4):
            emit_slab_mm(ctx, psf, WQ, woT, j * 512, OD, TT, queue="pool")
            for oc in range(4):
                i = j * 4 + oc
                ctx.op("dve", "tensor_copy", out=FO[i][:], in_=psf[oc][:, 0:TT])

        emit_ln_stream_add_f32(ctx, c, sc, psf[4], psf[5], psf[3], FO, [SEXP, G1], sc["tmp"], rx, ox, tsl, TT)
        t0 += TTt

    ctx.prefix = ""
    ctx.pop_scope()
    if standalone:
        ctx.finish()
    return nc


def emit_ln_stream_add_f32(ctx, c, sc, ps_stat, ps_b0, ps_b1, FO, FSQ, XR2, rx, ox, tsl, TT):
    row = sc["row"]
    m_row = row[0:1, 0:TT]
    r_row = row[0:1, TT:2 * TT]
    t_row = row[0:1, 2 * TT:3 * TT]
    for i in range(NC):
        b = i % 2
        ctx.op("act", "activation", out=FSQ[b][:], in_=FO[i][:], func=AF.Square)
        ctx.op("pe", "matmul", out=ps_stat[0:1, 0:TT], lhsT=c["ones_col_f"][:], rhs=FO[i][:],
               start=(i == 0), stop=(i == NC - 1))
        ctx.op("pe", "matmul", out=ps_b0[0:1, 0:TT], lhsT=c["ones_col_f"][:], rhs=FSQ[b][:],
               start=(i == 0), stop=(i == NC - 1))
    ctx.op("dve", "tensor_scalar_mul", out=m_row, in0=ps_stat[0:1, 0:TT], scalar1=1.0 / D)
    ctx.op("act", "activation", out=t_row, in_=m_row, func=AF.Square)
    ctx.op("dve", "scalar_tensor_tensor", out=r_row, in0=ps_b0[0:1, 0:TT], scalar=1.0 / D,
           in1=t_row, op0=ALU.mult, op1=ALU.subtract)
    ctx.op("act", "activation", out=r_row, in_=r_row, func=AF.Sqrt, bias=c["cv"][0:1, 0:1])
    ctx.op("dve", "reciprocal", out=r_row, in_=r_row)
    ctx.op("pe", "matmul", out=ps_b0[:, 0:TT], lhsT=c["ones_row_f"][:], rhs=m_row, start=True, stop=True)
    ctx.op("pe", "matmul", out=ps_b1[:, 0:TT], lhsT=c["ones_row_f"][:], rhs=r_row, start=True, stop=True)
    for i in range(NC):
        b = i % 2
        ctx.op("sp", "dma_start", out=XR2[b][:], in_=rx[i * 128:(i + 1) * 128, tsl])
        ctx.op("dve", "tensor_sub", out=FO[i][:], in0=FO[i][:], in1=ps_b0[:, 0:TT])
        ctx.op("dve", "tensor_mul", out=FO[i][:], in0=FO[i][:], in1=ps_b1[:, 0:TT])
        ctx.op("dve", "tensor_add", out=FO[i][:], in0=FO[i][:], in1=XR2[b][:])
        ctx.op("sp", "dma_start", out=ox[i * 128:(i + 1) * 128, tsl], in_=FO[i][:])


def build_full(TT, tiles):
    nc = bass.Bass("TRN2", target_bir_lowering=False, debug=False)
    T = sum(tiles)
    rx = nc.dram_tensor("rx", [D, T], F32, kind="ExternalInput").ap()
    r1 = nc.dram_tensor("resid1", [D, T], F32, kind="Internal").ap()
    r2 = nc.dram_tensor("resid2", [D, T], F32, kind="Internal").ap()
    r3 = nc.dram_tensor("resid3", [D, T], F32, kind="Internal").ap()
    ox = nc.dram_tensor("ox", [D, T], F32, kind="ExternalOutput").ap()
    ctx = Ctx(nc)
    c = make_consts(ctx)
    build_l0attn(TT, tiles, nc=nc, ctx=ctx, c=c, rx=rx, ox_=r1)
    build_l0ffn(TT, tiles, nc=nc, ctx=ctx, c=c, rx=r1, ox_=r2)
    build_l1attn(TT, tiles, nc=nc, ctx=ctx, c=c, rx=r2, ox_=r3)
    build_l1ffn(TT, tiles, nc=nc, ctx=ctx, c=c, rx=r3, ox_=ox)
    ctx.finish()
    return nc



def _dmajor(x):
    return np.ascontiguousarray(x.T)


_PROGS = {}


def _progs(TT, tiles):
    key = (TT, tuple(tiles))
    if key not in _PROGS:
        _PROGS[key] = build_full(TT, tiles)
    return _PROGS[key]


def kernel(**inputs):
    TT = 512
    tiles = [TT] * 4
    ncf = _progs(TT, tiles)
    cons = consts_host()
    ct = lambda a: np.ascontiguousarray(a.astype(np.float32))
    g = lambda k: inputs[k]

    w0 = {"W_inT": ct(g("L0_attn_W_in").T),
          "W_outT": ct(g("L0_attn_W_out").T),
          "conv_w": ct(g("L0_attn_conv_w")[:, 0, :].reshape(34, 128, 4).transpose(1, 0, 2).reshape(128, 136)),
          "conv_b": ct(g("L0_attn_conv_b").reshape(34, 128).T),
          "dt_bias": ct(g("L0_attn_dt_bias")[:, None]),
          "A_log": ct(g("L0_attn_A_log")[:, None]),
          "D_vec": ct(np.repeat(g("L0_attn_D"), 64).reshape(32, 128).T),
          "norm_w": ct(g("L0_attn_norm_w").reshape(32, 128).T),
          "consts_pack": cons}
    w1 = {"w_guT": ct(g("L0_ffn_w_gate_up").T),
          "w_pT": ct(g("L0_ffn_w_proj").T),
          "consts_pack": cons}
    mlp = np.zeros((1, 464), np.float32)
    mlp[0, 0:96] = g("L1_attn_Wb1").T.reshape(-1)
    mlp[0, 96:192] = g("L1_attn_Wa1").T.reshape(-1)
    mlp[0, 192:224] = g("L1_attn_Wb2")[0]
    mlp[0, 224:256] = g("L1_attn_Wa2")[0]
    mlp[0, 256:288] = g("L1_attn_bb1")
    mlp[0, 288:320] = g("L1_attn_ba1")
    mlp[0, 384] = g("L1_attn_bb2")[0]
    mlp[0, 385] = g("L1_attn_ba2")[0]
    w2 = {"WqT": ct(g("L1_attn_Wq").T), "WkT": ct(g("L1_attn_Wk").T),
          "WvT": ct(g("L1_attn_Wv").T), "WoT": ct(g("L1_attn_Wo").T),
          "Wg1T": ct(g("L1_attn_Wg1").T), "Wg2T": ct(g("L1_attn_Wg2").T),
          "WsvT": ct(g("L1_attn_Wsv").T),
          "conv_q": ct(g("L1_attn_conv_q").reshape(16, 128, 4).transpose(1, 0, 2).reshape(128, 64)),
          "conv_k": ct(g("L1_attn_conv_k").reshape(16, 128, 4).transpose(1, 0, 2).reshape(128, 64)),
          "conv_v": ct(g("L1_attn_conv_v").reshape(16, 128, 4).transpose(1, 0, 2).reshape(128, 64)),
          "A_exp": ct(np.repeat(g("L1_attn_A_log"), 128).reshape(16, 128).T),
          "dt_bias1": ct(g("L1_attn_dt_bias").reshape(16, 128).T),
          "bg2": ct(g("L1_attn_bg2").reshape(16, 128).T),
          "norm_w_row": ct(g("L1_attn_norm_w")[None, :]),
          "mlp_w": mlp,
          "mb2_w": ct(np.stack([g("L1_attn_Wb2")[0], g("L1_attn_Wa2")[0]], 1)),
          "bb1_w": ct(np.stack([g("L1_attn_bb1"), g("L1_attn_ba1")], 1)),
          "consts_pack": cons}
    w3 = {"W_upT": ct(g("L1_ffn_W_up").T), "W_gateT": ct(g("L1_ffn_W_gate").T),
          "W_downT": ct(g("L1_ffn_W_down").T), "consts_pack": cons}

    x = g("L0_attn_u")
    wall = {}
    for wm in (w0, w1, w2, w3):
        wall.update(wm)
    in_maps = [dict(wall, rx=_dmajor(x[i])) for i in range(NCORES)]
    res = run_bass_kernel_spmd(ncf, in_maps, core_ids=list(range(NCORES)))
    return np.stack([res.results[i]["ox"].T for i in range(NCORES)]).astype(np.float32)


# version 06
# speedup vs baseline: 1.2627x; 1.2627x over previous
import numpy as np
from contextlib import ExitStack
import concourse.bass as bass
import concourse.mybir as mybir
from concourse.bass_utils import run_bass_kernel_spmd

F32 = mybir.dt.float32
BF16 = mybir.dt.bfloat16
AF = mybir.ActivationFunctionType
ALU = mybir.AluOpType
AX = mybir.AxisListType

D = 2048
NCORES = 8
NC = D // 128
LN_EPS = 1e-5

ENGS = ("pe", "dve", "act", "pool", "sp")
WRITE_KEYS = ("out", "accum_out", "out_max", "out_indices")
DMA_SEMS = {"sp": ["spd%d" % i for i in range(8)],
            "pool": ["pld%d" % i for i in range(4)],
            "act": ["acd%d" % i for i in range(2)],
            "dve": ["dvd%d" % i for i in range(2)]}


_SEQ = [0]


class _Op:
    __slots__ = ("eng", "meth", "args", "kwargs", "deps", "need_inc", "sem", "cnt",
                 "is_dma", "seq", "paranoid")

    def __init__(self, eng, meth, args, kwargs):
        _SEQ[0] += 1
        self.seq = _SEQ[0]
        self.paranoid = False
        self.eng = eng
        self.meth = meth
        self.args = args
        self.kwargs = kwargs
        self.deps = []
        self.need_inc = False
        self.sem = None
        self.cnt = None
        self.is_dma = meth.startswith("dma")


def _aps_of(obj):
    if isinstance(obj, bass.AP):
        return [obj]
    return []


class Ctx:

    def __init__(self, nc):
        self.nc = nc
        self.es = ExitStack()
        self.q = {e: [] for e in ENGS}
        self.sem_names = ["pe", "dve", "act", "pool"] + sum(DMA_SEMS.values(), [])
        self.sems = {s: self.es.enter_context(nc.semaphore("sem_" + s))
                     for s in self.sem_names}
        self.mem = {}
        self._dma_rr = {e: 0 for e in DMA_SEMS}
        self._dma_last = {s: None for s in sum(DMA_SEMS.values(), [])}
        self._names = set()
        self._barrier_ops = []
        self._scopes = [self.es]
        self.prefix = ""
        self.paranoid_flag = False

    def sb(self, name, shape, dt):
        name = self.prefix + name
        assert name not in self._names, name
        self._names.add(name)
        return self._scopes[-1].enter_context(self.nc.sbuf_tensor(name, shape, dt))

    def psum(self, name, shape, dt=F32):
        name = self.prefix + name
        assert name not in self._names, name
        self._names.add(name)
        return self._scopes[-1].enter_context(self.nc.psum_tensor(name, shape, dt))

    def push_scope(self):
        s = ExitStack()
        self._scopes.append(s)
        return s

    def pop_scope(self):
        self.hard_barrier()
        self._scopes.pop().close()

    def scope(self):
        return ExitStack()

    def sb_s(self, scope, name, shape, dt):
        name = self.prefix + name
        assert name not in self._names, name
        self._names.add(name)
        return scope.enter_context(self.nc.sbuf_tensor(name, shape, dt))

    def close_scope(self, scope):
        self.hard_barrier()
        scope.close()

    def hard_barrier(self):
        self._barrier_ops = [q[-1] for q in self.q.values() if q]

    def _slot(self, name):
        s = self.mem.get(name)
        if s is None:
            s = self.mem[name] = {"w": None, "r": {}}
        return s

    def op(self, eng, meth, *args, **kwargs):
        if meth == "transpose" and eng == "pe":
            meth = "matmul"
            kwargs = {"out": kwargs["out"], "lhsT": kwargs["in_"],
                      "rhs": kwargs["identity"], "start": True, "stop": True}
        op = _Op(eng, meth, args, kwargs)
        op.paranoid = self.paranoid_flag
        if op.is_dma:
            pool = DMA_SEMS[eng]
            idx = self._dma_rr[eng] % len(pool)
            self._dma_rr[eng] += 1
            op.sem = pool[idx]
            prev = self._dma_last[op.sem]
            if prev is not None:
                op.deps.append(prev)
            self._dma_last[op.sem] = op
        reads, writes = [], []
        for k, v in kwargs.items():
            for ap in _aps_of(v):
                (writes if k in WRITE_KEYS else reads).append(ap)
        for i, a in enumerate(args):
            if i == 0 and meth in ("memset", "memzero"):
                writes.extend(_aps_of(a))
            else:
                reads.extend(_aps_of(a))
        if op.meth == "matmul" and not kwargs.get("start", True):
            reads.extend(_aps_of(kwargs.get("out")))
        rd_names = {a.tensor.name for a in reads}
        wr_names = {a.tensor.name for a in writes}
        for b in self._barrier_ops:
            self._dep(op, b)
        for n in rd_names:
            s = self._slot(n)
            if s["w"] is not None:
                self._dep(op, s["w"])
        for n in wr_names:
            s = self._slot(n)
            if s["w"] is not None:
                self._dep(op, s["w"])
            for r in s["r"].values():
                self._dep(op, r)
        for n in rd_names:
            s = self._slot(n)
            key = op.sem if op.is_dma else op.eng
            s["r"][key] = op
        for n in wr_names:
            self.mem[n] = {"w": op, "r": {}}
        self.q[eng].append(op)
        return op

    def _dep(self, op, prod):
        if prod is op:
            return
        if prod.eng == "pe" and op.eng == "pe" and not prod.is_dma and not op.is_dma:
            return
        op.deps.append(prod)

    def finish(self, paranoid=False):
        import bisect
        any_par = paranoid or any(op.paranoid for e in ENGS for op in self.q[e])
        if any_par:
            for e in ENGS:
                for op in self.q[e]:
                    op.need_inc = True
        for e in ENGS:
            for op in self.q[e]:
                for d in op.deps:
                    d.need_inc = True
        counts = {s: 0 for s in self.sem_names}
        for e in ENGS:
            for op in self.q[e]:
                if op.is_dma:
                    counts[op.sem] += 16
                    op.cnt = counts[op.sem]
                elif op.need_inc:
                    op.sem = op.eng
                    counts[op.sem] += 1
                    op.cnt = counts[op.sem]
        self.final_counts = counts
        pred = {}
        if any_par:
            allops = sorted((op.seq, op) for e in ENGS for op in self.q[e])
            lastc = {}
            for seq, op in allops:
                pred[seq] = [o for e, o in lastc.items() if e != op.eng]
                if op.sem is not None and not op.is_dma:
                    lastc[op.eng] = op
        with self.nc.Block() as block:
            for eng, attr in (("pe", "tensor"), ("dve", "vector"), ("act", "scalar"),
                              ("pool", "gpsimd"), ("sp", "sync")):
                queue = self.q[eng]

                def body(E, queue=queue, eng=eng):
                    hw = {}
                    for op in queue:
                        want = {}
                        for d in op.deps:
                            if d.sem is None:
                                continue
                            want[d.sem] = max(want.get(d.sem, 0), d.cnt)
                        if paranoid or op.paranoid:
                            for p in pred.get(op.seq, ()):
                                want[p.sem] = max(want.get(p.sem, 0), p.cnt)
                            if op.sem is not None:
                                cap = op.cnt - (16 if op.is_dma else 1)
                                if want.get(op.sem, 0) > cap:
                                    want[op.sem] = cap
                        for s, v in sorted(want.items()):
                            if hw.get(s, 0) < v:
                                E.wait_ge(self.sems[s], v)
                                hw[s] = v
                        instr = getattr(E, op.meth)(*op.args, **op.kwargs)
                        if op.sem is not None and (op.is_dma or op.need_inc):
                            instr.then_inc(self.sems[op.sem], 16 if op.is_dma else 1)
                    for s in DMA_SEMS.get(eng, []):
                        if counts[s] > 0:
                            E.wait_ge(self.sems[s], counts[s])
                getattr(block, attr)(body)
        self.es.close()



def consts_host():
    pack = np.zeros((128, 1024), np.float32)
    pack[:, 0:128] = np.eye(128, dtype=np.float32)
    s = np.arange(128)[:, None]
    t = np.arange(128)[None, :]
    pack[:, 128:256] = (s <= t).astype(np.float32)
    pack[:, 256:384] = (s < t).astype(np.float32)
    pack[:, 512] = 1.0
    pack[:64, 515] = 1.0
    pack[64:, 516] = 1.0
    pack[:, 520] = 1e-5
    pack[:, 521] = 1.1920929e-7
    pack[:, 522] = 1e-6
    pack[:, 523] = 128.0 ** -0.5
    pack[:, 524] = 1e-12
    pack[:, 525] = 0.5
    pack[:, 526] = 1.0
    p16 = np.arange(128)[:, None] // 16
    c16 = np.arange(128)[None, :] // 16
    po = np.arange(128)[:, None] % 16
    co = np.arange(128)[None, :] % 16
    pack[:, 384:512] = ((p16 == c16) & (po < co)).astype(np.float32)
    pack[:, 896:1024] = ((p16 == c16) & (po <= co)).astype(np.float32)
    pack[0, 640:704] = 1.0
    pack[1, 704:768] = 1.0
    pack[0, 768:896] = 1.0
    return pack


def make_consts(ctx, lite=False, blk=False):
    nc = ctx.nc
    c = {}
    c["ident_f"] = ctx.sb("c_ident_f", [128, 128], F32)
    c["ident_b"] = ctx.sb("c_ident_b", [128, 128], BF16)
    c["ones_col_b"] = ctx.sb("c_ones_col_b", [128, 1], BF16)
    c["ones_col_f"] = ctx.sb("c_ones_col_f", [128, 1], F32)
    c["ones_row_f"] = ctx.sb("c_ones_row_f", [1, 128], F32)
    c["ones_row_b"] = ctx.sb("c_ones_row_b", [1, 128], BF16)
    c["mask_incl_f"] = ctx.sb("c_mask_incl_f", [128, 128], F32)
    c["cv"] = ctx.sb("c_cv", [128, 16], F32)
    dr = getattr(ctx, "_consts_dram", None)
    if dr is None:
        dr = nc.dram_tensor("consts_pack", [128, 1024], F32, kind="ExternalInput").ap()
    ctx.op("sp", "dma_start", out=c["ident_f"][:], in_=dr[:, 0:128])
    ctx.op("sp", "dma_start", out=c["mask_incl_f"][:], in_=dr[:, 128:256])
    ctx.op("sp", "dma_start", out=c["ones_col_f"][:], in_=dr[:, 512:513],
           allow_slow_non_contiguous=True)
    ctx.op("sp", "dma_start", out=c["ones_row_f"][:], in_=dr[0:1, 768:896])
    ctx.op("sp", "dma_start", out=c["cv"][:], in_=dr[:, 520:536])
    ctx.op("pool", "dma_start", out=c["ident_b"][:], in_=dr[:, 0:128])
    ctx.op("pool", "dma_start", out=c["ones_col_b"][:], in_=dr[:, 512:513],
           allow_slow_non_contiguous=True)
    ctx.op("pool", "dma_start", out=c["ones_row_b"][:], in_=dr[0:1, 768:896])
    c["_dr"] = dr
    return c


def load_ffnq_consts(ctx, c):
    dr = c["_dr"]
    g2 = ctx.sb("c_g2_b", [128, 2], BF16)
    sel = ctx.sb("c_sel_half_b", [2, 128], BF16)
    ctx.op("pool", "dma_start", out=g2[:], in_=dr[:, 515:517],
           allow_slow_non_contiguous=True)
    ctx.op("pool", "dma_start", out=sel[:], in_=dr[0:2, 640:768])
    return dict(c, g2_b=g2, sel_half_b=sel)


def load_blk_consts(ctx, c):
    dr = c["_dr"]
    tb = ctx.sb("c_tblk", [128, 128], F32)
    pb_ = ctx.sb("c_pblk", [128, 128], F32)
    ctx.op("sp", "dma_start", out=tb[:], in_=dr[:, 384:512])
    ctx.op("sp", "dma_start", out=pb_[:], in_=dr[:, 896:1024])
    return dict(c, tblk=tb, pblk=pb_)



def emit_ln(ctx, c, sc, ps_stat, ps_b0, ps_b1, X, XBF, TT,
            out_bf=None, resid=None, out_f32=None):
    sq = sc["sq"]
    row = sc["row"]
    tmp = sc["tmp"]
    m_row = row[0:1, 0:TT]
    r_row = row[0:1, TT:2 * TT]
    t_row = row[0:1, 2 * TT:3 * TT]
    for i in range(NC):
        b = i % 2
        ctx.op("act", "activation", out=sq[b][:], in_=X[i][:], func=AF.Square)
        ctx.op("pe", "matmul", out=ps_stat[0:1, 0:TT], lhsT=c["ones_col_b"][:], rhs=XBF[i][:],
               start=(i == 0), stop=(i == NC - 1))
        ctx.op("pe", "matmul", out=ps_b0[0:1, 0:TT], lhsT=c["ones_col_b"][:], rhs=sq[b][:],
               start=(i == 0), stop=(i == NC - 1))
    ctx.op("dve", "tensor_scalar_mul", out=m_row, in0=ps_stat[0:1, 0:TT], scalar1=1.0 / D)
    ctx.op("act", "activation", out=t_row, in_=m_row, func=AF.Square)
    ctx.op("dve", "scalar_tensor_tensor", out=r_row, in0=ps_b0[0:1, 0:TT], scalar=1.0 / D,
           in1=t_row, op0=ALU.mult, op1=ALU.subtract)
    ctx.op("act", "activation", out=r_row, in_=r_row, func=AF.Sqrt, bias=c["cv"][0:1, 0:1])
    ctx.op("dve", "reciprocal", out=r_row, in_=r_row)
    ctx.op("pe", "matmul", out=ps_b0[:, 0:TT], lhsT=c["ones_row_f"][:], rhs=m_row, start=True, stop=True)
    ctx.op("pe", "matmul", out=ps_b1[:, 0:TT], lhsT=c["ones_row_f"][:], rhs=r_row, start=True, stop=True)
    for i in range(NC):
        b = i % 2
        ctx.op("dve", "tensor_sub", out=tmp[b][:], in0=X[i][:], in1=ps_b0[:, 0:TT])
        if out_bf is not None:
            ctx.op("dve", "tensor_mul", out=out_bf[i][:], in0=tmp[b][:], in1=ps_b1[:, 0:TT])
        else:
            ctx.op("dve", "tensor_mul", out=tmp[b][:], in0=tmp[b][:], in1=ps_b1[:, 0:TT])
            ctx.op("dve", "tensor_add", out=out_f32[i][:], in0=resid[i][:], in1=tmp[b][:])


def emit_ln_stream(ctx, c, sc, ps_stat, ps_b0, ps_b1, src, tsl, XB2, out_bf, TT):
    sq = sc["sq"]
    row = sc["row"]
    m_row = row[0:1, 0:TT]
    r_row = row[0:1, TT:2 * TT]
    t_row = row[0:1, 2 * TT:3 * TT]
    for i in range(NC):
        b = i % 2
        ctx.op("pool", "dma_start", out=XB2[b][:], in_=src[i * 128:(i + 1) * 128, tsl])
        ctx.op("act", "activation", out=sq[b][:], in_=XB2[b][:], func=AF.Square)
        ctx.op("pe", "matmul", out=ps_stat[0:1, 0:TT], lhsT=c["ones_col_b"][:], rhs=XB2[b][:],
               start=(i == 0), stop=(i == NC - 1))
        ctx.op("pe", "matmul", out=ps_b0[0:1, 0:TT], lhsT=c["ones_col_b"][:], rhs=sq[b][:],
               start=(i == 0), stop=(i == NC - 1))
    ctx.op("dve", "tensor_scalar_mul", out=m_row, in0=ps_stat[0:1, 0:TT], scalar1=1.0 / D)
    ctx.op("act", "activation", out=t_row, in_=m_row, func=AF.Square)
    ctx.op("dve", "scalar_tensor_tensor", out=r_row, in0=ps_b0[0:1, 0:TT], scalar=1.0 / D,
           in1=t_row, op0=ALU.mult, op1=ALU.subtract)
    ctx.op("act", "activation", out=r_row, in_=r_row, func=AF.Sqrt, bias=c["cv"][0:1, 0:1])
    ctx.op("dve", "reciprocal", out=r_row, in_=r_row)
    ctx.op("pe", "matmul", out=ps_b0[:, 0:TT], lhsT=c["ones_row_f"][:], rhs=m_row, start=True, stop=True)
    ctx.op("pe", "matmul", out=ps_b1[:, 0:TT], lhsT=c["ones_row_f"][:], rhs=r_row, start=True, stop=True)
    for i in range(NC):
        b = i % 2
        ctx.op("pool", "dma_start", out=XB2[b][:], in_=src[i * 128:(i + 1) * 128, tsl])
        ctx.op("dve", "tensor_sub", out=sc["tmp"][b][:], in0=XB2[b][:], in1=ps_b0[:, 0:TT])
        ctx.op("dve", "tensor_mul", out=out_bf[i][:], in0=sc["tmp"][b][:], in1=ps_b1[:, 0:TT])


def emit_tile_load(ctx, X, XBF, src, tsl):
    for i in range(NC):
        ctx.op("sp", "dma_start", out=X[i][:], in_=src[i * 128:(i + 1) * 128, tsl])
        ctx.op("pool", "dma_start", out=XBF[i][:], in_=src[i * 128:(i + 1) * 128, tsl])


def emit_slab_mm(ctx, ps, WQ, wsrc, col0, rhs, TT, n_oc=4, ncols=None, queue="sp"):
    nk = len(rhs)
    if ncols is None:
        ncols = n_oc * 128
    for kk in range(nk):
        b = kk % 2
        ctx.op(queue, "dma_start", out=WQ[b][:, 0:ncols],
               in_=wsrc[kk * 128:(kk + 1) * 128, col0:col0 + ncols])
        for oc in range(n_oc):
            w = min(128, ncols - oc * 128)
            ctx.op("pe", "matmul", out=ps[oc][0:w if w < 128 else 128, 0:TT],
                   lhsT=WQ[b][:, oc * 128:oc * 128 + w], rhs=rhs[kk][:],
                   start=(kk == 0), stop=(kk == nk - 1))



def build_l1ffn(TT, tiles, nc=None, ctx=None, c=None, rx=None, ox_=None):
    T = sum(tiles)
    standalone = nc is None
    if standalone:
        nc = bass.Bass("TRN2", target_bir_lowering=False, debug=False)
    if rx is None:
        rx = nc.dram_tensor("rx", [D, T], F32, kind="ExternalInput").ap()
    wupT = nc.dram_tensor("W_upT", [2048, 8192], F32, kind="ExternalInput").ap()
    wgateT = nc.dram_tensor("W_gateT", [2048, 8192], F32, kind="ExternalInput").ap()
    wdownT = nc.dram_tensor("W_downT", [8192, 2048], F32, kind="ExternalInput").ap()
    if ox_ is None:
        ox = nc.dram_tensor("ox", [D, T], F32, kind="ExternalOutput").ap()
    else:
        ox = ox_
    qup = nc.dram_tensor("qup", [2048, 8192], BF16, kind="Internal").ap()
    qgate = nc.dram_tensor("qgate", [2048, 8192], BF16, kind="Internal").ap()
    qdown = nc.dram_tensor("qdown", [8192, 2048], BF16, kind="Internal").ap()
    if ctx is None:
        ctx = Ctx(nc)
        c = make_consts(ctx)
    ctx.push_scope()
    ctx.prefix = "l1f_"
    ps = [ctx.psum("ps%d" % i, [128, 512], F32) for i in range(8)]

    srow = ctx.sb("srow", [1, 8], F32)
    grow = ctx.sb("grow", [1, 16], F32)
    scol = ctx.sb("scol", [128, 8], F32)
    WQ = [ctx.sb("wq%d" % i, [128, 512], BF16) for i in range(2)]
    qsc = ctx.scope()
    WF = [ctx.sb_s(qsc, "wf%d" % i, [128, 512], F32) for i in range(2)]
    AB = [ctx.sb_s(qsc, "ab%d" % i, [128, 512], BF16) for i in range(2)]
    SG = [ctx.sb_s(qsc, "sgn%d" % i, [128, 512], F32) for i in range(2)]
    MK = [ctx.sb_s(qsc, "msk%d" % i, [128, 512], F32) for i in range(2)]

    mats = [(wupT, qup, 2048, 8192, 0), (wgateT, qgate, 2048, 8192, 1),
            (wdownT, qdown, 8192, 2048, 2)]
    sbank = [ps[7], ps[6], ps[5]]

    for (w, q, K, O, mi) in mats:
        nk, no = K // 128, O // 512
        ntile = nk * no
        ti = 0
        for ko in range(no):
            for kk in range(nk):
                b = ti % 2
                ctx.op("sp", "dma_start", out=WF[b][:],
                       in_=w[kk * 128:(kk + 1) * 128, ko * 512:(ko + 1) * 512])
                ctx.op("act", "activation", out=AB[b][:], in_=WF[b][:], func=AF.Abs)
                ctx.op("pe", "matmul", out=sbank[mi][0:1, 0:512], lhsT=c["ones_col_b"][:],
                       rhs=AB[b][:], start=(ti == 0), stop=(ti == ntile - 1))
                ti += 1
    for (w, q, K, O, mi) in mats:
        ctx.op("dve", "tensor_reduce", out=grow[0:1, mi:mi + 1], in_=sbank[mi][0:1, 0:512],
               axis=AX.X, op=ALU.add)
        ctx.op("dve", "tensor_scalar", out=srow[0:1, mi:mi + 1], in0=grow[0:1, mi:mi + 1],
               scalar1=1.0 / (K * O), scalar2=1e-8, op0=ALU.mult, op1=ALU.add)
        ctx.op("dve", "tensor_scalar_mul", out=srow[0:1, 3 + mi:4 + mi], in0=srow[0:1, mi:mi + 1],
               scalar1=0.5)
    ctx.op("pe", "matmul", out=ps[4][:, 0:6], lhsT=c["ones_row_f"][:], rhs=srow[0:1, 0:6],
           start=True, stop=True)
    ctx.op("dve", "tensor_copy", out=scol[:, 0:6], in_=ps[4][:, 0:6])

    for (w, q, K, O, mi) in mats:
        nk, no = K // 128, O // 512
        for ko in range(no):
            for kk in range(nk):
                b = kk % 2
                ctx.op("sp", "dma_start", out=WF[b][:],
                       in_=w[kk * 128:(kk + 1) * 128, ko * 512:(ko + 1) * 512])
                ctx.op("act", "activation", out=SG[b][:], in_=WF[b][:], func=AF.Sign)
                ctx.op("act", "activation", out=MK[b][:], in_=WF[b][:], func=AF.Abs)
                ctx.op("dve", "tensor_single_scalar", out=MK[b][:], in_=MK[b][:],
                       scalar=scol[:, 3 + mi:4 + mi], op=ALU.is_gt)
                ctx.op("dve", "tensor_mul", out=WQ[b][:], in0=MK[b][:], in1=SG[b][:])
                ctx.op("sp", "dma_start",
                       out=q[kk * 128:(kk + 1) * 128, ko * 512:(ko + 1) * 512],
                       in_=WQ[b][:])

    ctx.close_scope(qsc)
    X = [ctx.sb("x%d" % i, [128, TT], F32) for i in range(NC)]
    XBF = [ctx.sb("xb%d" % i, [128, TT], BF16) for i in range(NC)]
    XN = [ctx.sb("xn%d" % i, [128, TT], BF16) for i in range(NC)]
    FO = [ctx.sb("fo%d" % i, [128, TT], F32) for i in range(NC)]
    FOB = [ctx.sb("fob%d" % i, [128, TT], BF16) for i in range(NC)]
    H = [ctx.sb("h%d" % i, [128, TT], BF16) for i in range(64)]
    SU = [ctx.sb("su%d" % i, [128, TT], BF16) for i in range(4)]
    sc = {"sq": [ctx.sb("ln_sq0", [128, TT], BF16), ctx.sb("ln_sq1", [128, TT], BF16)],
          "row": ctx.sb("ln_row", [1, 3 * TT], F32),
          "tmp": [ctx.sb("ln_tmp0", [128, TT], F32), ctx.sb("ln_tmp1", [128, TT], F32)]}

    t0 = 0
    for it, TTt in enumerate(tiles):
        assert TTt == TT
        tsl = slice(t0, t0 + TTt)
        emit_tile_load(ctx, X, XBF, rx, tsl)
        emit_ln(ctx, c, sc, ps[4], ps[5], ps[6], X, XBF, TT, out_bf=XN)

        for j in range(16):
            for phase in (0, 1):
                qsrc = qup if phase == 0 else qgate
                emit_slab_mm(ctx, ps, WQ, qsrc, j * 512, XN, TT)
                if phase == 0:
                    for oc in range(4):
                        ctx.op("act", "activation", out=SU[oc][:], in_=ps[oc][:, 0:TT],
                               func=AF.Silu, scale=scol[:, 0:1])
                else:
                    for oc in range(4):
                        ctx.op("dve", "scalar_tensor_tensor", out=H[j * 4 + oc][:],
                               in0=ps[oc][:, 0:TT], scalar=scol[:, 1:2], in1=SU[oc][:],
                               op0=ALU.mult, op1=ALU.mult)

        for j in range(4):
            emit_slab_mm(ctx, ps, WQ, qdown, j * 512, H, TT)
            for oc in range(4):
                i = j * 4 + oc
                ctx.op("dve", "tensor_scalar_mul", out=FO[i][:], in0=ps[oc][:, 0:TT],
                       scalar1=scol[:, 2:3])
                ctx.op("pool", "tensor_copy", out=FOB[i][:], in_=FO[i][:])

        emit_ln(ctx, c, sc, ps[4], ps[5], ps[6], FO, FOB, TT, resid=X, out_f32=X)
        for i in range(NC):
            ctx.op("sp", "dma_start", out=ox[i * 128:(i + 1) * 128, tsl], in_=X[i][:])
        t0 += TTt

    ctx.prefix = ""
    ctx.pop_scope()
    if standalone:
        ctx.finish()
    return nc





def emit_group_quant(ctx, c, ps_gs, ps_sct, WB, AB, AV, SGV, MK, SCB, RSC, wsrc, qdst, K, O):
    nk, no = K // 128, O // 512
    for ko in range(no):
        for kk in range(nk):
            b = (ko * nk + kk) % 2
            ctx.op("pool", "dma_start", out=WB[b][:],
                   in_=wsrc[kk * 128:(kk + 1) * 128, ko * 512:(ko + 1) * 512])
            ctx.op("act", "activation", out=AB[b][:], in_=WB[b][:], func=AF.Abs)
            ctx.op("pe", "matmul", out=ps_gs[0:2, 0:512], lhsT=c["g2_b"][:], rhs=AB[b][:],
                   start=True, stop=True)
            ctx.op("dve", "tensor_scalar", out=SCB[:], in0=ps_gs[0:2, 0:512],
                   scalar1=1.0 / 64, scalar2=1e-8, op0=ALU.mult, op1=ALU.max)
            ctx.op("pe", "matmul", out=ps_sct[:, 0:512], lhsT=c["sel_half_b"][:], rhs=SCB[:],
                   start=True, stop=True)
            ctx.op("dve", "reciprocal", out=RSC[:], in_=ps_sct[:, 0:512])
            ctx.op("dve", "tensor_mul", out=AV[b][:], in0=WB[b][:], in1=RSC[:])
            ctx.op("act", "activation", out=SGV[b][:], in_=AV[b][:], func=AF.Sign)
            ctx.op("act", "activation", out=AV[b][:], in_=AV[b][:], func=AF.Abs)
            ctx.op("dve", "tensor_single_scalar", out=MK[b][:], in_=AV[b][:],
                   scalar=0.5, op=ALU.is_gt)
            ctx.op("dve", "tensor_mul", out=MK[b][:], in0=MK[b][:], in1=SGV[b][:])
            ctx.op("dve", "tensor_tensor", out=MK[b][:], in0=MK[b][:], in1=ps_sct[:, 0:512],
                   op=ALU.mult)
            ctx.op("sp", "dma_start",
                   out=qdst[kk * 128:(kk + 1) * 128, ko * 512:(ko + 1) * 512], in_=MK[b][:])


def build_l0ffn(TT, tiles, nc=None, ctx=None, c=None, rx=None, ox_=None):
    T = sum(tiles)
    standalone = nc is None
    if standalone:
        nc = bass.Bass("TRN2", target_bir_lowering=False, debug=False)
    if rx is None:
        rx = nc.dram_tensor("rx", [D, T], F32, kind="ExternalInput").ap()
    wguT = nc.dram_tensor("w_guT", [2048, 16384], F32, kind="ExternalInput").ap()
    wpT = nc.dram_tensor("w_pT", [8192, 2048], F32, kind="ExternalInput").ap()
    if ox_ is None:
        ox = nc.dram_tensor("ox", [D, T], F32, kind="ExternalOutput").ap()
    else:
        ox = ox_
    qgu = nc.dram_tensor("qgu", [2048, 16384], BF16, kind="Internal").ap()
    qp = nc.dram_tensor("qp", [8192, 2048], BF16, kind="Internal").ap()
    if ctx is None:
        ctx = Ctx(nc)
        c = make_consts(ctx)
    ctx.push_scope()
    ctx.prefix = "l0f_"
    c = load_ffnq_consts(ctx, c)
    ps = [ctx.psum("ps%d" % i, [128, 512], F32) for i in range(8)]
    WQ = [ctx.sb("wq%d" % i, [128, 512], BF16) for i in range(2)]

    qsc = ctx.scope()
    WB = [ctx.sb_s(qsc, "wb%d" % i, [128, 512], BF16) for i in range(2)]
    AB = [ctx.sb_s(qsc, "ab%d" % i, [128, 512], BF16) for i in range(2)]
    AV = [ctx.sb_s(qsc, "av%d" % i, [128, 512], BF16) for i in range(2)]
    SGV = [ctx.sb_s(qsc, "sgv%d" % i, [128, 512], BF16) for i in range(2)]
    MK = [ctx.sb_s(qsc, "mk%d" % i, [128, 512], BF16) for i in range(2)]
    SCB = ctx.sb_s(qsc, "scb", [2, 512], BF16)
    RSC = ctx.sb_s(qsc, "rsc", [128, 512], F32)
    emit_group_quant(ctx, c, ps[6], ps[7], WB, AB, AV, SGV, MK, SCB, RSC, wguT, qgu, 2048, 16384)
    emit_group_quant(ctx, c, ps[6], ps[7], WB, AB, AV, SGV, MK, SCB, RSC, wpT, qp, 8192, 2048)
    ctx.close_scope(qsc)

    X = [ctx.sb("x%d" % i, [128, TT], F32) for i in range(NC)]
    XBF = [ctx.sb("xb%d" % i, [128, TT], BF16) for i in range(NC)]
    XN = [ctx.sb("xn%d" % i, [128, TT], BF16) for i in range(NC)]
    FO = [ctx.sb("fo%d" % i, [128, TT], F32) for i in range(NC)]
    FOB = [ctx.sb("fob%d" % i, [128, TT], BF16) for i in range(NC)]
    H = [ctx.sb("h%d" % i, [128, TT], BF16) for i in range(64)]
    SU = [ctx.sb("su%d" % i, [128, TT], BF16) for i in range(4)]
    sc = {"sq": [ctx.sb("ln_sq0", [128, TT], BF16), ctx.sb("ln_sq1", [128, TT], BF16)],
          "row": ctx.sb("ln_row", [1, 3 * TT], F32),
          "tmp": [ctx.sb("ln_tmp0", [128, TT], F32), ctx.sb("ln_tmp1", [128, TT], F32)]}
    rrow = ctx.sb("rms_row", [1, TT], F32)

    t0 = 0
    for it, TTt in enumerate(tiles):
        assert TTt == TT
        tsl = slice(t0, t0 + TTt)
        emit_tile_load(ctx, X, XBF, rx, tsl)
        emit_ln(ctx, c, sc, ps[4], ps[5], ps[6], X, XBF, TT, out_bf=XN)

        for j in range(16):
            for phase in (0, 1):
                col = j * 512 + (8192 if phase == 1 else 0)
                emit_slab_mm(ctx, ps, WQ, qgu, col, XN, TT)
                if phase == 0:
                    for oc in range(4):
                        ctx.op("act", "activation", out=SU[oc][:], in_=ps[oc][:, 0:TT],
                               func=AF.Silu)
                else:
                    for oc in range(4):
                        ctx.op("dve", "tensor_mul", out=H[j * 4 + oc][:],
                               in0=ps[oc][:, 0:TT], in1=SU[oc][:])

        for i in range(64):
            b = i % 2
            ctx.op("act", "activation", out=sc["sq"][b][:], in_=H[i][:], func=AF.Square)
            ctx.op("pe", "matmul", out=ps[4][0:1, 0:TT], lhsT=c["ones_col_b"][:],
                   rhs=sc["sq"][b][:], start=(i == 0), stop=(i == 63))
        ctx.op("dve", "tensor_scalar_mul", out=rrow[0:1, 0:TT], in0=ps[4][0:1, 0:TT],
               scalar1=1.0 / 8192)
        ctx.op("act", "activation", out=rrow[0:1, 0:TT], in_=rrow[0:1, 0:TT], func=AF.Sqrt,
               bias=c["cv"][0:1, 1:2])
        ctx.op("dve", "reciprocal", out=rrow[0:1, 0:TT], in_=rrow[0:1, 0:TT])
        ctx.op("pe", "matmul", out=ps[5][:, 0:TT], lhsT=c["ones_row_f"][:], rhs=rrow[0:1, 0:TT],
               start=True, stop=True)
        for i in range(64):
            ctx.op("dve", "tensor_mul", out=H[i][:], in0=H[i][:], in1=ps[5][:, 0:TT])

        for j in range(4):
            emit_slab_mm(ctx, ps, WQ, qp, j * 512, H, TT)
            for oc in range(4):
                i = j * 4 + oc
                ctx.op("dve", "tensor_copy", out=FO[i][:], in_=ps[oc][:, 0:TT])
                ctx.op("pool", "tensor_copy", out=FOB[i][:], in_=FO[i][:])

        emit_ln(ctx, c, sc, ps[4], ps[5], ps[6], FO, FOB, TT, resid=X, out_f32=X)
        for i in range(NC):
            ctx.op("sp", "dma_start", out=ox[i * 128:(i + 1) * 128, tsl], in_=X[i][:])
        t0 += TTt

    ctx.prefix = ""
    ctx.pop_scope()
    if standalone:
        ctx.finish()
    return nc





def emit_cumsum128(ctx, c, ps_a, ps_b, src, dst, P, ncols, tmp):
    for cc in range(ncols // 128):
        cs = slice(cc * 128, (cc + 1) * 128)
        ctx.op("pe", "transpose", out=ps_a[:, 0:P], in_=src[:, cs],
               identity=c["ident_f"][0:P, 0:P])
        ctx.op("dve", "tensor_copy", out=tmp[:, 0:P], in_=ps_a[:, 0:P])
        ctx.op("pe", "matmul", out=ps_b[:, 0:P], lhsT=c["mask_incl_f"][:], rhs=tmp[:, 0:P],
               start=True, stop=True)
        ctx.op("dve", "tensor_copy", out=tmp[:, 0:P], in_=ps_b[:, 0:P])
        ctx.op("pe", "transpose", out=ps_a[0:P, 128:256], in_=tmp[:, 0:P],
               identity=c["ident_f"][:])
        ctx.op("dve", "tensor_copy", out=dst[:, cs], in_=ps_a[0:P, 128:256])



def build_l0attn(TT, tiles, first_tile_state_zero=True, upto='full',
                 nc=None, ctx=None, c=None, rx=None, ox_=None):
    import os
    skip = set(os.environ.get("L0SKIP", "").split(","))
    T = sum(tiles)
    NCH = TT // 128
    standalone = nc is None
    if standalone:
        nc = bass.Bass("TRN2", target_bir_lowering=False, debug=False)
    if rx is None:
        rx = nc.dram_tensor("rx", [D, T], F32, kind="ExternalInput").ap()
    winT = nc.dram_tensor("W_inT", [2048, 8512], F32, kind="ExternalInput").ap()
    woutT = nc.dram_tensor("W_outT", [4096, 2048], F32, kind="ExternalInput").ap()
    cw = nc.dram_tensor("conv_w", [128, 136], F32, kind="ExternalInput").ap()
    cb = nc.dram_tensor("conv_b", [128, 34], F32, kind="ExternalInput").ap()
    dtb = nc.dram_tensor("dt_bias", [64, 1], F32, kind="ExternalInput").ap()
    alog = nc.dram_tensor("A_log", [64, 1], F32, kind="ExternalInput").ap()
    dvec = nc.dram_tensor("D_vec", [128, 32], F32, kind="ExternalInput").ap()
    nrmw = nc.dram_tensor("norm_w", [128, 32], F32, kind="ExternalInput").ap()
    if ox_ is None:
        ox = nc.dram_tensor("ox", [D, T], F32, kind="ExternalOutput").ap()
    else:
        ox = ox_
    if ctx is None:
        ctx = Ctx(nc)
        c = make_consts(ctx, lite=True)
    ctx.push_scope()
    ctx.prefix = "l0a_"
    ps = [ctx.psum("ps%d" % i, [128, 512], F32) for i in range(6)]
    pb = [ctx.psum("pb%d" % i, [128, 512], F32) for i in range(2)]

    XB2 = [ctx.sb("xb2_%d" % i, [128, TT], BF16) for i in range(2)]
    XN = [ctx.sb("xn%d" % i, [128, TT], BF16) for i in range(NC)]
    XBCC = [ctx.sb("xbcc%d" % i, [128, TT], BF16) for i in range(34)]
    YG = [ctx.sb("yg%d" % i, [128, TT], BF16) for i in range(32)]
    FO = [ctx.sb("fo%d" % i, [128, TT], F32) for i in range(NC)]
    FOB = [ctx.sb("fob%d" % i, [128, TT], BF16) for i in range(NC)]
    XT = ctx.sb("xt", [128, 4096], BF16)
    YT = ctx.sb("yt", [128, 4096], BF16)
    S = ctx.sb("state", [128, 4096], F32)
    WQ = [ctx.sb("wq%d" % i, [128, 512], BF16) for i in range(2)]
    sc = {"sq": [ctx.sb("ln_sq0", [128, TT], BF16), ctx.sb("ln_sq1", [128, TT], BF16)],
          "row": ctx.sb("ln_row", [1, 3 * TT], F32),
          "tmp": [ctx.sb("ln_tmp0", [128, TT], F32), ctx.sb("ln_tmp1", [128, TT], F32)]}
    XRW = [ctx.sb("xrw%d" % i, [128, TT + 3], BF16) for i in range(2)]
    ACC = sc["tmp"]
    TAILS = ctx.sb("tails", [128, 34 * 3], BF16)
    XR2 = sc["tmp"]
    SZ = [ctx.sb("sz%d" % i, [128, TT], BF16) for i in range(2)]
    DT1 = ctx.sb("dt1", [64, TT], F32)
    DTS = ctx.sb("dts", [64, TT], F32)
    LDA = DT1
    BCum = ctx.sb("bcum", [64, TT], F32)
    T64 = [ctx.sb("t64_0", [64, TT], F32)]
    NBT = ctx.sb("nbt", [128, 64], F32)
    EBT = ctx.sb("ebt", [128, 64], F32)
    DTT = ctx.sb("dtt", [128, 64], F32)
    DTWT = ctx.sb("dtwt", [128, 64], F32)
    EBLB = ctx.sb("eblb", [128, 64], F32)
    EBLR = ctx.sb("eblr", [1, 64], F32)
    BROW = ctx.sb("brow", [1, 128], F32)
    CBM = ctx.sb("cbm", [128, 128], F32)
    BT = ctx.sb("btr", [128, 128], BF16)
    SCF = [ctx.sb("scf%d" % i, [128, 128], F32) for i in range(2)]
    SCB = [ctx.sb("scbuf%d" % i, [128, 128], BF16) for i in range(2)]
    BD = [ctx.sb("bd%d" % i, [128, 128], BF16) for i in range(2)]
    TY = [ctx.sb("ty%d" % i, [128, 64], F32) for i in range(2)]
    S0B = [ctx.sb("s0b%d" % i, [128, 64], BF16) for i in range(2)]
    CWS = ctx.sb("cws", [128, 34 * 4], F32)
    CBS = ctx.sb("cbs", [128, 34], F32)
    DTB = ctx.sb("dtbs", [64, 1], F32)
    NEA = ctx.sb("nea", [64, 1], F32)
    DVC = ctx.sb("dvc", [128, 32], F32)
    NRM = ctx.sb("nrm", [128, 32], F32)
    rrow = ctx.sb("rms_row", [1, TT], F32)

    ctx.op("sp", "dma_start", out=CWS[:], in_=cw)
    ctx.op("sp", "dma_start", out=CBS[:], in_=cb)
    ctx.op("sp", "dma_start", out=DTB[:], in_=dtb)
    ctx.op("sp", "dma_start", out=DVC[:], in_=dvec)
    ctx.op("sp", "dma_start", out=NRM[:], in_=nrmw)
    AL = ctx.sb("als", [64, 1], F32)
    ctx.op("sp", "dma_start", out=AL[:], in_=alog)
    ctx.op("act", "activation", out=NEA[:], in_=AL[:], func=AF.Exp)
    ctx.op("dve", "tensor_scalar_mul", out=NEA[:], in0=NEA[:], scalar1=-1.0)
    if first_tile_state_zero:
        ctx.op("pool", "memset", S[:], 0.0)
        ctx.op("dve", "memset", TAILS[:], 0.0)

    t0 = 0
    for it, TTt in enumerate(tiles):
        assert TTt == TT
        tsl = slice(t0, t0 + TTt)
        emit_ln_stream(ctx, c, sc, ps[4], ps[5], ps[3], rx, tsl, XB2, XN, TT)

        slabs = [] if upto == "u1" else [(4096 + 512 * j, 4) for j in range(8)] + [(8192, 2)]
        for (col0, noc) in slabs:
            emit_slab_mm(ctx, ps, WQ, winT, col0, XN, TT, n_oc=noc, queue="pool")
            for oc in range(noc):
                ch = (col0 - 4096) // 128 + oc
                b = ch % 2
                ctx.op("dve", "tensor_copy", out=XRW[b][:, 0:3], in_=TAILS[:, ch * 3:ch * 3 + 3])
                ctx.op("dve", "tensor_copy", out=XRW[b][:, 3:3 + TT], in_=ps[oc][:, 0:TT])
                ctx.op("dve", "tensor_copy", out=TAILS[:, ch * 3:ch * 3 + 3], in_=XRW[b][:, TT:TT + 3])
                ctx.op("dve", "tensor_scalar_mul", out=ACC[b][:], in0=XRW[b][:, 0:TT],
                       scalar1=CWS[:, ch * 4:ch * 4 + 1])
                for j in (1, 2, 3):
                    ctx.op("dve", "scalar_tensor_tensor", out=ACC[b][:], in0=XRW[b][:, j:j + TT],
                           scalar=CWS[:, ch * 4 + j:ch * 4 + j + 1], in1=ACC[b][:],
                           op0=ALU.mult, op1=ALU.add)
                ctx.op("act", "activation", out=XBCC[ch][:], in_=ACC[b][:], func=AF.Silu,
                       bias=CBS[:, ch:ch + 1])
        if upto == "u1":
            for i in range(NC):
                ctx.op("pool", "dma_start", out=ox[i * 128:(i + 1) * 128, tsl], in_=XN[i][:])
            t0 += TTt
            continue
        emit_slab_mm(ctx, ps, WQ, winT, 8448, XN, TT, n_oc=1, ncols=64, queue="pool")
        ctx.op("dve", "tensor_scalar_add", out=DT1[:], in0=ps[0][0:64, 0:TT], scalar1=DTB[:, 0:1])
        ctx.op("act", "activation", out=DTS[:], in_=DT1[:], func=AF.Relu)
        ctx.op("act", "activation", out=T64[0][:], in_=DT1[:], func=AF.Abs)
        ctx.op("act", "activation", out=T64[0][:], in_=T64[0][:], func=AF.Exp, scale=-1.0)
        ctx.op("act", "activation", out=T64[0][:], in_=T64[0][:], func=AF.Ln, bias=c["cv"][0:64, 6:7])
        ctx.op("dve", "tensor_add", out=DTS[:], in0=T64[0][:], in1=DTS[:])
        ctx.op("dve", "tensor_scalar_mul", out=LDA[:], in0=DTS[:], scalar1=NEA[:, 0:1])
        emit_cumsum128(ctx, c, ps[4], ps[5], LDA, BCum, 64, TT, CBM)

        if upto in ("u1", "u2"):
            for i in range(NC):
                ctx.op("pool", "dma_start", out=ox[i * 128:(i + 1) * 128, tsl], in_=XN[i][:])
            t0 += TTt
            continue
        ctx.paranoid_flag = True
        for cc in range(NCH):
            cs = slice(cc * 128, (cc + 1) * 128)
            bL = BCum[:, cc * 128 + 127:cc * 128 + 128]
            if "t1" not in skip:
                ctx.op("pe", "transpose", out=ps[4][:, 0:64], in_=BCum[:, cs],
                       identity=c["ident_f"][0:64, 0:64])
                ctx.op("dve", "tensor_scalar_mul", out=NBT[:], in0=ps[4][:, 0:64], scalar1=-1.0)
                ctx.op("act", "activation", out=EBT[:], in_=ps[4][:, 0:64], func=AF.Exp)
                ctx.op("pe", "transpose", out=ps[4][:, 64:128], in_=DTS[:, cs],
                       identity=c["ident_f"][0:64, 0:64])
                ctx.op("dve", "tensor_copy", out=DTT[:], in_=ps[4][:, 64:128])
            if "t2" not in skip:
                ctx.op("dve", "tensor_scalar", out=T64[0][:, 0:128], in0=BCum[:, cs], scalar1=bL,
                       scalar2=-1.0, op0=ALU.subtract, op1=ALU.mult)
                ctx.op("act", "activation", out=T64[0][:, 0:128], in_=T64[0][:, 0:128], func=AF.Exp)
                ctx.op("dve", "tensor_mul", out=T64[0][:, 0:128], in0=T64[0][:, 0:128], in1=DTS[:, cs])
                ctx.op("pe", "transpose", out=ps[4][:, 128:192], in_=T64[0][:, 0:128],
                       identity=c["ident_f"][0:64, 0:64])
                ctx.op("dve", "tensor_copy", out=DTWT[:], in_=ps[4][:, 128:192])
            if "t3" not in skip:
                ctx.op("act", "activation", out=T64[0][:, 128:129], in_=bL, func=AF.Exp)
                ctx.op("pe", "matmul", out=ps[4][0:1, 192:256], lhsT=T64[0][:, 128:129],
                       rhs=c["ident_f"][0:64, 0:64], start=True, stop=True)
                ctx.op("dve", "tensor_copy", out=EBLR[:], in_=ps[4][0:1, 192:256])
                ctx.op("pe", "matmul", out=ps[4][:, 256:320], lhsT=c["ones_row_f"][:], rhs=EBLR[:],
                       start=True, stop=True)
                ctx.op("dve", "tensor_copy", out=EBLB[:], in_=ps[4][:, 256:320])
            if "t4" not in skip:
                ctx.op("pe", "matmul", out=ps[5][:, 0:128], lhsT=XBCC[32][:, cs], rhs=XBCC[33][:, cs],
                       start=True, stop=True)
                ctx.op("dve", "tensor_mul", out=CBM[:], in0=ps[5][:, 0:128], in1=c["mask_incl_f"][:])
            if "t5" not in skip:
                ctx.op("pe", "transpose", out=pb[0][:, 0:128], in_=XBCC[32][:, cs], identity=c["ident_b"][:])
                ctx.op("dve", "tensor_copy", out=BT[:], in_=pb[0][:, 0:128])
            if "t6" not in skip:
                for d in range(32):
                    bb = d % 2
                    ctx.op("pe", "transpose", out=pb[bb][:, 128:256], in_=XBCC[d][:, cs],
                           identity=c["ident_b"][:])
                    ctx.op("dve", "tensor_copy", out=XT[:, d * 128:(d + 1) * 128], in_=pb[bb][:, 128:256])
            if upto == "c0":
                continue
            for h in range(64):
                b = h % 2
                hs = slice(h * 64, (h + 1) * 64)
                psc = [ps[0], ps[2]][b]
                py = [ps[1], ps[3]][b]
                pct = [ps[4], ps[5]][b]
                ctx.op("pe", "matmul", out=psc[0:1, 128:256], lhsT=c["ident_f"][0:64, h:h + 1],
                       rhs=BCum[:, cs], start=True, stop=True)
                ctx.op("dve", "tensor_copy", out=BROW[:], in_=psc[0:1, 128:256])
                ctx.op("pe", "matmul", out=psc[:, 0:128], lhsT=c["ones_row_f"][:], rhs=BROW[:],
                       start=True, stop=True)
                ctx.op("dve", "tensor_scalar", out=SCF[b][:], in0=psc[:, 0:128],
                       scalar1=NBT[:, h:h + 1], scalar2=0.0, op0=ALU.add, op1=ALU.min)
                ctx.op("act", "activation", out=SCF[b][:], in_=SCF[b][:], func=AF.Exp)
                ctx.op("dve", "tensor_mul", out=SCF[b][:], in0=SCF[b][:], in1=CBM[:])
                ctx.op("dve", "tensor_scalar_mul", out=SCB[b][:], in0=SCF[b][:], scalar1=DTT[:, h:h + 1])
                ctx.op("dve", "tensor_copy", out=S0B[b][:], in_=S[:, hs])
                ctx.op("pe", "matmul", out=py[:, 0:64], lhsT=SCB[b][:], rhs=XT[:, h * 64:h * 64 + 64],
                       start=True, stop=True)
                ctx.op("pe", "matmul", out=py[:, 64:128], lhsT=XBCC[33][:, cs], rhs=S0B[b][:],
                       start=True, stop=True)
                ctx.op("dve", "tensor_scalar_mul", out=TY[b][:], in0=py[:, 64:128], scalar1=EBT[:, h:h + 1])
                ctx.op("dve", "tensor_add", out=YT[:, hs], in0=py[:, 0:64], in1=TY[b][:])
                ctx.op("dve", "tensor_scalar_mul", out=BD[b][:], in0=BT[:], scalar1=DTWT[:, h:h + 1])
                ctx.op("pe", "matmul", out=pct[:, 320:384], lhsT=BD[b][:], rhs=XT[:, h * 64:h * 64 + 64],
                       start=True, stop=True)
                ctx.op("dve", "scalar_tensor_tensor", out=S[:, hs], in0=S[:, hs],
                       scalar=EBLB[:, h:h + 1], in1=pct[:, 320:384], op0=ALU.mult, op1=ALU.add)
            if upto == "c1":
                continue
            for d in range(32):
                bb = d % 2
                ctx.op("pe", "transpose", out=pb[bb][:, 256:384], in_=YT[:, d * 128:(d + 1) * 128],
                       identity=c["ident_b"][:])
                ctx.op("dve", "scalar_tensor_tensor", out=YG[d][:, cs], in0=XBCC[d][:, cs],
                       scalar=DVC[:, d:d + 1], in1=pb[bb][:, 256:384], op0=ALU.mult, op1=ALU.add)

        if upto in ("u3", "c0", "c1"):
            for i in range(NC):
                ctx.op("pool", "dma_start", out=ox[i * 128:(i + 1) * 128, tsl], in_=XN[i][:])
            t0 += TTt
            continue
        for j in range(8):
            emit_slab_mm(ctx, ps, WQ, winT, j * 512, XN, TT, queue="pool")
            for oc in range(4):
                zi = j * 4 + oc
                ctx.op("act", "activation", out=SZ[oc % 2][:], in_=ps[oc][:, 0:TT], func=AF.Silu)
                ctx.op("dve", "tensor_mul", out=YG[zi][:], in0=YG[zi][:], in1=SZ[oc % 2][:])

        for i in range(32):
            b = i % 2
            ctx.op("act", "activation", out=sc["sq"][b][:], in_=YG[i][:], func=AF.Square)
            ctx.op("pe", "matmul", out=ps[4][0:1, 0:TT], lhsT=c["ones_col_b"][:],
                   rhs=sc["sq"][b][:], start=(i == 0), stop=(i == 31))
        ctx.op("dve", "tensor_scalar_mul", out=rrow[0:1, 0:TT], in0=ps[4][0:1, 0:TT],
               scalar1=1.0 / 4096)
        ctx.op("act", "activation", out=rrow[0:1, 0:TT], in_=rrow[0:1, 0:TT], func=AF.Sqrt,
               bias=c["cv"][0:1, 0:1])
        ctx.op("dve", "reciprocal", out=rrow[0:1, 0:TT], in_=rrow[0:1, 0:TT])
        ctx.op("pe", "matmul", out=ps[5][:, 0:TT], lhsT=c["ones_row_f"][:], rhs=rrow[0:1, 0:TT],
               start=True, stop=True)
        for i in range(32):
            ctx.op("dve", "scalar_tensor_tensor", out=YG[i][:], in0=YG[i][:],
                   scalar=NRM[:, i:i + 1], in1=ps[5][:, 0:TT], op0=ALU.mult, op1=ALU.mult)

        if upto == "u4":
            for i in range(NC):
                ctx.op("pool", "dma_start", out=ox[i * 128:(i + 1) * 128, tsl], in_=XN[i][:])
            t0 += TTt
            continue
        for j in range(4):
            emit_slab_mm(ctx, ps, WQ, woutT, j * 512, YG, TT, queue="pool")
            for oc in range(4):
                i = j * 4 + oc
                ctx.op("dve", "tensor_copy", out=FO[i][:], in_=ps[oc][:, 0:TT])
                ctx.op("pool", "tensor_copy", out=FOB[i][:], in_=FO[i][:])

        emit_ln_stream_add(ctx, c, sc, ps[4], ps[5], ps[3], FO, FOB, XR2, rx, ox, tsl, TT)
        t0 += TTt

    ctx.prefix = ""
    ctx.pop_scope()
    if standalone:
        ctx.finish()
    return nc


def emit_ln_stream_add(ctx, c, sc, ps_stat, ps_b0, ps_b1, FO, FOB, XR2, rx, ox, tsl, TT):
    sq = sc["sq"]
    row = sc["row"]
    m_row = row[0:1, 0:TT]
    r_row = row[0:1, TT:2 * TT]
    t_row = row[0:1, 2 * TT:3 * TT]
    for i in range(NC):
        b = i % 2
        ctx.op("act", "activation", out=sq[b][:], in_=FO[i][:], func=AF.Square)
        ctx.op("pe", "matmul", out=ps_stat[0:1, 0:TT], lhsT=c["ones_col_b"][:], rhs=FOB[i][:],
               start=(i == 0), stop=(i == NC - 1))
        ctx.op("pe", "matmul", out=ps_b0[0:1, 0:TT], lhsT=c["ones_col_b"][:], rhs=sq[b][:],
               start=(i == 0), stop=(i == NC - 1))
    ctx.op("dve", "tensor_scalar_mul", out=m_row, in0=ps_stat[0:1, 0:TT], scalar1=1.0 / D)
    ctx.op("act", "activation", out=t_row, in_=m_row, func=AF.Square)
    ctx.op("dve", "scalar_tensor_tensor", out=r_row, in0=ps_b0[0:1, 0:TT], scalar=1.0 / D,
           in1=t_row, op0=ALU.mult, op1=ALU.subtract)
    ctx.op("act", "activation", out=r_row, in_=r_row, func=AF.Sqrt, bias=c["cv"][0:1, 0:1])
    ctx.op("dve", "reciprocal", out=r_row, in_=r_row)
    ctx.op("pe", "matmul", out=ps_b0[:, 0:TT], lhsT=c["ones_row_f"][:], rhs=m_row, start=True, stop=True)
    ctx.op("pe", "matmul", out=ps_b1[:, 0:TT], lhsT=c["ones_row_f"][:], rhs=r_row, start=True, stop=True)
    for i in range(NC):
        b = i % 2
        ctx.op("sp", "dma_start", out=XR2[b][:], in_=rx[i * 128:(i + 1) * 128, tsl])
        ctx.op("dve", "tensor_sub", out=FO[i][:], in0=FO[i][:], in1=ps_b0[:, 0:TT])
        ctx.op("dve", "tensor_mul", out=FO[i][:], in0=FO[i][:], in1=ps_b1[:, 0:TT])
        ctx.op("dve", "tensor_add", out=FO[i][:], in0=FO[i][:], in1=XR2[b][:])
        ctx.op("sp", "dma_start", out=ox[i * 128:(i + 1) * 128, tsl], in_=FO[i][:])





def build_l1attn(TT, tiles, dbg=False, nc=None, ctx=None, c=None, rx=None, ox_=None):
    T = sum(tiles)
    NCH = TT // 128
    standalone = nc is None
    if standalone:
        nc = bass.Bass("TRN2", target_bir_lowering=False, debug=False)
    if rx is None:
        rx = nc.dram_tensor("rx", [D, T], F32, kind="ExternalInput").ap()
    wqT = nc.dram_tensor("WqT", [2048, 2048], F32, kind="ExternalInput").ap()
    wkT = nc.dram_tensor("WkT", [2048, 2048], F32, kind="ExternalInput").ap()
    wvT = nc.dram_tensor("WvT", [2048, 2048], F32, kind="ExternalInput").ap()
    convq = nc.dram_tensor("conv_q", [128, 64], F32, kind="ExternalInput").ap()
    convk = nc.dram_tensor("conv_k", [128, 64], F32, kind="ExternalInput").ap()
    convv = nc.dram_tensor("conv_v", [128, 64], F32, kind="ExternalInput").ap()
    aexp = nc.dram_tensor("A_exp", [128, 16], F32, kind="ExternalInput").ap()
    dtbias = nc.dram_tensor("dt_bias1", [128, 16], F32, kind="ExternalInput").ap()
    wsvT = nc.dram_tensor("WsvT", [128, 128], F32, kind="ExternalInput").ap()
    mlpw = nc.dram_tensor("mlp_w", [1, 4 * 96 + 80], F32, kind="ExternalInput").ap()
    mb2w = nc.dram_tensor("mb2_w", [32, 2], F32, kind="ExternalInput").ap()
    bb1w = nc.dram_tensor("bb1_w", [32, 2], F32, kind="ExternalInput").ap()
    wg1T = nc.dram_tensor("Wg1T", [2048, 128], F32, kind="ExternalInput").ap()
    wg2T = nc.dram_tensor("Wg2T", [128, 2048], F32, kind="ExternalInput").ap()
    bg2 = nc.dram_tensor("bg2", [128, 16], F32, kind="ExternalInput").ap()
    nwrow = nc.dram_tensor("norm_w_row", [1, 128], F32, kind="ExternalInput").ap()
    woT = nc.dram_tensor("WoT", [2048, 2048], F32, kind="ExternalInput").ap()
    if ox_ is None:
        ox = nc.dram_tensor("ox", [D, T], F32, kind="ExternalOutput").ap()
    else:
        ox = ox_
    if dbg:
        dK = nc.dram_tensor("dK", [2048, T], F32, kind="ExternalOutput").ap()
        dQ = nc.dram_tensor("dQ", [2048, T], F32, kind="ExternalOutput").ap()
        dV = nc.dram_tensor("dV", [2048, T], F32, kind="ExternalOutput").ap()
        dG = nc.dram_tensor("dG", [2048, T], F32, kind="ExternalOutput").ap()
        dBC = nc.dram_tensor("dBC", [128, 64], F32, kind="ExternalOutput").ap()
        dBQ = nc.dram_tensor("dBQ", [128, T], F32, kind="ExternalOutput").ap()
        dTS = nc.dram_tensor("dTS", [128, 128], F32, kind="ExternalOutput").ap()
        dPS = nc.dram_tensor("dPS", [128, 128], F32, kind="ExternalOutput").ap()
        dXD = nc.dram_tensor("dXD", [128, 128], F32, kind="ExternalOutput").ap()
        dSS = nc.dram_tensor("dSS", [128, 512], F32, kind="ExternalOutput").ap()
        dXD4 = nc.dram_tensor("dXD4", [128, 512], F32, kind="ExternalOutput").ap()
        dRH = nc.dram_tensor("dRH", [128, 512], F32, kind="ExternalOutput").ap()
        dO4 = nc.dram_tensor("dO4", [128, 512], F32, kind="ExternalOutput").ap()
        dTS4 = nc.dram_tensor("dTS4", [128, 512], F32, kind="ExternalOutput").ap()
        dKN = nc.dram_tensor("dKN", [128, 512], F32, kind="ExternalOutput").ap()
        dQN = nc.dram_tensor("dQN", [128, 512], F32, kind="ExternalOutput").ap()
        dOD = nc.dram_tensor("dOD", [2048, T], F32, kind="ExternalOutput").ap()

    if ctx is None:
        ctx = Ctx(nc)
        c = make_consts(ctx, lite=True, blk=True)
    ctx.push_scope()
    ctx.prefix = "l1a_"
    ctx.paranoid_flag = True
    c = load_blk_consts(ctx, c)
    psf = [ctx.psum("psf%d" % i, [128, 512], F32) for i in range(6)]
    psb = [ctx.psum("psb%d" % i, [128, 512], F32) for i in range(2)]

    XB2 = [ctx.sb("xb2_%d" % i, [128, TT], BF16) for i in range(2)]
    XN = [ctx.sb("xn%d" % i, [128, TT], BF16) for i in range(NC)]
    QC = [ctx.sb("qc%d" % i, [128, TT], BF16) for i in range(16)]
    KC = [ctx.sb("kc%d" % i, [128, TT], BF16) for i in range(16)]
    VC = [ctx.sb("vc%d" % i, [128, TT], BF16) for i in range(16)]
    G1GATE = ctx.sb("g1gate", [128, TT], BF16)
    SGT = ctx.sb("sgt", [128, TT], BF16)
    FO = [ctx.sb("fo%d" % i, [128, TT], F32) for i in range(NC)]
    S = ctx.sb("state", [128, 2048], F32)
    OD = XN
    WQ = [ctx.sb("wq%d" % i, [128, 512], BF16) for i in range(2)]
    sc = {"sq": [ctx.sb("ln_sq0", [128, TT], BF16), ctx.sb("ln_sq1", [128, TT], BF16)],
          "row": ctx.sb("ln_row", [1, 3 * TT], F32),
          "tmp": [ctx.sb("ln_tmp0", [128, TT], F32), ctx.sb("ln_tmp1", [128, TT], F32)]}
    XRW = [ctx.sb("xrw%d" % i, [128, TT + 3], BF16) for i in range(2)]
    TLS = [ctx.sb("tls%d" % i, [128, 3 * 16], BF16) for i in range(3)]
    EHF = None
    G1 = ctx.sb("g1f", [128, TT], F32)
    MBALL = G1
    BQ = ctx.sb("bq", [128, TT], F32)
    SEXP = ctx.sb("sexp", [128, TT], F32)
    DPAIR = [ctx.sb("dpair%d" % i, [128, 2048], F32) for i in range(1)]
    TPB = ctx.sb("tpb", [128, 2048], F32)
    PPB = ctx.sb("ppb", [128, 2048], F32)
    KS1 = ctx.sb("ks1", [128, 2048], F32)
    UQ = ctx.sb("uqbuf", [128, 2 * TT], F32)
    UC = ctx.sb("ucbuf", [128, TT], F32)
    QCC = ctx.sb("qccbuf", [128, TT], F32)
    WP = ctx.sb("wpbuf", [128, TT], F32)
    WBL = [ctx.sb("wbl%d" % i, [128, 4 * 112], F32) for i in range(1)]
    TROW = ctx.sb("trow", [1, 2048], F32)
    TD = ctx.sb("tdbuf", [128, 16], F32)
    PD = ctx.sb("pdbuf", [128, 16], F32)
    TS = ctx.sb("tsbuf", [128, 128], F32)
    PS_ = ctx.sb("psbuf", [128, 128], F32)
    AB_ = ctx.sb("abuf", [128, 128], F32)
    ABT = ctx.sb("abtbuf", [128, 128], F32)
    APW = [ctx.sb("apw%d" % i, [128, 128], F32) for i in range(2)]
    APWT = [ctx.sb("apwt%d" % i, [128, 128], F32) for i in range(2)]
    XDF = ctx.sb("xdf", [128, 128], F32)
    VTS = ctx.sb("vts", [128, 128], F32)
    R1 = ctx.sb("r1f", [128, 128], F32)
    ONF = R1
    ONB = ctx.sb("onb", [128, 128], BF16)
    OCOL = ctx.sb("ocol", [128, 4], F32)
    BCOL = ctx.sb("bcol", [128, 64], F32)
    ROWS = ctx.sb("rows_s", [1, 5 * TT], F32)
    H1B = ctx.sb("h1b", [32, TT], BF16)
    NW = ctx.sb("nwb", [128, 128], F32)
    DTBC = ctx.sb("dtbc", [128, 16], F32)
    NEAC = ctx.sb("neac", [128, 16], F32)
    BG2C = ctx.sb("bg2c", [128, 16], F32)
    CWS = [ctx.sb("cws%d" % i, [128, 64], F32) for i in range(3)]
    MLPR = ctx.sb("mlpr", [1, 4 * 96 + 80], F32)
    WSV = ctx.sb("wsv", [128, 128], BF16)
    FSQ = None
    EHF = sc["tmp"][1]

    ctx.op("sp", "dma_start", out=MLPR[:], in_=mlpw)
    ctx.op("pool", "dma_start", out=WSV[:], in_=wsvT)
    for i, cv_ in enumerate((convq, convk, convv)):
        ctx.op("sp", "dma_start", out=CWS[i][:], in_=cv_)
    ctx.op("sp", "dma_start", out=DTBC[:], in_=dtbias)
    ctx.op("sp", "dma_start", out=NEAC[:], in_=aexp)
    ctx.op("sp", "dma_start", out=BG2C[:], in_=bg2)
    ctx.op("act", "activation", out=NEAC[:], in_=NEAC[:], func=AF.Exp)
    ctx.op("dve", "tensor_scalar_mul", out=NEAC[:], in0=NEAC[:], scalar1=-1.0)
    ctx.op("sp", "dma_start", out=ROWS[0:1, 0:128], in_=nwrow)
    ctx.op("pe", "matmul", out=psf[0][:, 0:128], lhsT=c["ones_row_f"][:], rhs=ROWS[0:1, 0:128],
           start=True, stop=True)
    ctx.op("dve", "tensor_copy", out=NW[:], in_=psf[0][:, 0:128])
    ctx.op("pool", "memset", S[:], 0.0)
    for i in range(3):
        ctx.op("dve", "memset", TLS[i][:], 0.0)
    W1B = MLPR[0:1, 0:96]
    W1A = MLPR[0:1, 96:192]
    MB2 = ctx.sb("mb2", [32, 2], BF16)
    ctx.op("pool", "dma_start", out=MB2[:], in_=mb2w)
    BB1 = ctx.sb("bb1c", [32, 2], F32)
    ctx.op("sp", "dma_start", out=BB1[:], in_=bb1w)

    t0 = 0
    for it, TTt in enumerate(tiles):
        assert TTt == TT
        tsl = slice(t0, t0 + TTt)
        emit_ln_stream(ctx, c, sc, psf[4], psf[5], psf[3], rx, tsl, XB2, XN, TT)

        for mi, (wT, dst) in enumerate(((wqT, QC), (wkT, KC), (wvT, VC))):
            for j in range(4):
                emit_slab_mm(ctx, psf, WQ, wT, j * 512, XN, TT, queue="pool")
                for oc in range(4):
                    ch = j * 4 + oc
                    b = ch % 2
                    ctx.op("dve", "tensor_copy", out=XRW[b][:, 0:3], in_=TLS[mi][:, ch * 3:ch * 3 + 3])
                    ctx.op("dve", "tensor_copy", out=XRW[b][:, 3:3 + TT], in_=psf[oc][:, 0:TT])
                    ctx.op("dve", "tensor_copy", out=TLS[mi][:, ch * 3:ch * 3 + 3], in_=XRW[b][:, TT:TT + 3])
                    ctx.op("dve", "tensor_scalar_mul", out=sc["tmp"][b][:], in0=XRW[b][:, 0:TT],
                           scalar1=CWS[mi][:, ch * 4:ch * 4 + 1])
                    for jj in (1, 2, 3):
                        ctx.op("dve", "scalar_tensor_tensor", out=sc["tmp"][b][:], in0=XRW[b][:, jj:jj + TT],
                               scalar=CWS[mi][:, ch * 4 + jj:ch * 4 + jj + 1], in1=sc["tmp"][b][:],
                               op0=ALU.mult, op1=ALU.add)
                    ctx.op("act", "activation", out=dst[ch][:], in_=sc["tmp"][b][:], func=AF.Silu)

        for kk in range(16):
            b = kk % 2
            ctx.op("pool", "dma_start", out=WQ[b][:, 0:128], in_=wg1T[kk * 128:(kk + 1) * 128, :])
            ctx.op("pe", "matmul", out=psf[0][:, 0:TT], lhsT=WQ[b][:, 0:128], rhs=XN[kk][:],
                   start=(kk == 0), stop=(kk == 15))
        ctx.op("dve", "tensor_copy", out=G1GATE[:], in_=psf[0][:, 0:TT])

        if dbg and it == 0:
            for ch in range(16):
                ctx.op("pool", "dma_start", out=dK[ch * 128:(ch + 1) * 128, tsl], in_=KC[ch][:])
                ctx.op("pool", "dma_start", out=dQ[ch * 128:(ch + 1) * 128, tsl], in_=QC[ch][:])
                ctx.op("pool", "dma_start", out=dV[ch * 128:(ch + 1) * 128, tsl], in_=VC[ch][:])
        for h in range(16):
            ehr = slice(0, TT)
            r_l2 = ROWS[0:1, 0:TT]
            r_l1 = ROWS[0:1, TT:2 * TT]
            r_cos = ROWS[0:1, 2 * TT:3 * TT]
            r_t1 = ROWS[0:1, 3 * TT:4 * TT]
            r_t2 = ROWS[0:1, 4 * TT:5 * TT]
            r_beta = r_t1
            r_amp = r_t2
            r_t3 = r_l2
            ctx.op("pe", "matmul", out=psf[5][:, 0:TT], lhsT=WSV[:], rhs=KC[h][:], start=True, stop=True)
            ctx.op("dve", "tensor_sub", out=EHF[:], in0=psf[5][:, 0:TT], in1=VC[h][:])
            ctx.op("act", "activation", out=SEXP[:], in_=EHF[:], func=AF.Square)
            ctx.op("pe", "matmul", out=psf[0][0:1, 0:TT], lhsT=c["ones_col_f"][:], rhs=SEXP[:], start=True, stop=True)
            ctx.op("act", "activation", out=SEXP[:], in_=EHF[:], func=AF.Abs)
            ctx.op("pe", "matmul", out=psf[1][0:1, 0:TT], lhsT=c["ones_col_f"][:], rhs=SEXP[:], start=True, stop=True)
            ctx.op("dve", "tensor_mul", out=SEXP[:], in0=psf[5][:, 0:TT], in1=VC[h][:])
            ctx.op("pe", "matmul", out=psf[2][0:1, 0:TT], lhsT=c["ones_col_f"][:], rhs=SEXP[:], start=True, stop=True)
            ctx.op("act", "activation", out=SEXP[:], in_=psf[5][:, 0:TT], func=AF.Square)
            ctx.op("pe", "matmul", out=psf[3][0:1, 0:TT], lhsT=c["ones_col_f"][:], rhs=SEXP[:], start=True, stop=True)
            ctx.op("act", "activation", out=SEXP[:], in_=VC[h][:], func=AF.Square)
            ctx.op("pe", "matmul", out=psf[4][0:1, 0:TT], lhsT=c["ones_col_f"][:], rhs=SEXP[:], start=True, stop=True)
            ctx.op("act", "activation", out=r_l2, in_=psf[0][0:1, 0:TT], func=AF.Sqrt, bias=c["cv"][0:1, 2:3])
            ctx.op("dve", "tensor_copy", out=r_l1, in_=psf[1][0:1, 0:TT])
            ctx.op("act", "activation", out=r_t1, in_=psf[3][0:1, 0:TT], func=AF.Sqrt, bias=c["cv"][0:1, 2:3])
            ctx.op("act", "activation", out=r_t2, in_=psf[4][0:1, 0:TT], func=AF.Sqrt, bias=c["cv"][0:1, 2:3])
            ctx.op("dve", "tensor_mul", out=r_t1, in0=r_t1, in1=r_t2)
            ctx.op("dve", "tensor_scalar_add", out=r_t1, in0=r_t1, scalar1=1e-6)
            ctx.op("dve", "reciprocal", out=r_t1, in_=r_t1)
            ctx.op("dve", "tensor_tensor", out=r_cos, in0=psf[2][0:1, 0:TT], in1=r_t1, op=ALU.mult)
            ctx.op("dve", "tensor_scalar", out=r_cos, in0=r_cos, scalar1=-1.0, scalar2=1.0,
                   op0=ALU.mult, op1=ALU.add)
            for mi2, (w1, w2col, bcol, outr, actf) in enumerate(
                    ((W1B, 0, 0, r_beta, AF.Sigmoid), (W1A, 1, 1, r_amp, AF.Identity))):
                for f, srow_ in enumerate((r_l2, r_l1, r_cos)):
                    ctx.op("pe", "matmul", out=psf[5][0:32, 0:TT], lhsT=w1[0:1, f * 32:(f + 1) * 32],
                           rhs=srow_, start=(f == 0), stop=(f == 2))
                ctx.op("act", "activation", out=H1B[:], in_=psf[5][0:32, 0:TT], func=AF.Silu,
                       bias=BB1[:, bcol:bcol + 1])
                ctx.op("pe", "matmul", out=psf[5][0:1, 0:TT], lhsT=MB2[:, w2col:w2col + 1], rhs=H1B[:],
                       start=True, stop=True)
                ctx.op("act", "activation", out=outr, in_=psf[5][0:1, 0:TT], func=actf,
                       bias=MLPR[0:1, 384 + w2col:385 + w2col])
            for cc2 in range(NCH):
                ctx.op("pe", "transpose", out=psf[5][:, 0:1], in_=r_beta[0:1, cc2 * 128:(cc2 + 1) * 128],
                       identity=c["ident_f"][0:1, 0:1])
                ctx.op("dve", "tensor_copy", out=BCOL[:, h * 4 + cc2:h * 4 + cc2 + 1], in_=psf[5][:, 0:1])
            ctx.op("act", "activation", out=SEXP[:], in_=KC[h][:], func=AF.Square)
            ctx.op("pe", "matmul", out=psf[0][0:1, 0:TT], lhsT=c["ones_col_f"][:], rhs=SEXP[:], start=True, stop=True)
            ctx.op("act", "activation", out=r_t1, in_=psf[0][0:1, 0:TT], func=AF.Sqrt)
            ctx.op("dve", "tensor_scalar_max", out=r_t1, in0=r_t1, scalar1=1e-12)
            ctx.op("dve", "reciprocal", out=r_t1, in_=r_t1)
            ctx.op("dve", "tensor_mul", out=r_t2, in0=r_t1, in1=r_amp)
            ctx.op("pe", "matmul", out=psf[1][:, 0:TT], lhsT=c["ones_row_f"][:], rhs=r_t2, start=True, stop=True)
            ctx.op("dve", "tensor_mul", out=G1[:], in0=KC[h][:], in1=psf[1][:, 0:TT])
            ctx.op("act", "activation", out=r_t3, in_=psf[0][0:1, 0:TT], func=AF.Sqrt, bias=c["cv"][0:1, 2:3])
            ctx.op("dve", "reciprocal", out=r_t3, in_=r_t3)
            ctx.op("pe", "matmul", out=psf[1][:, 0:TT], lhsT=c["ones_row_f"][:], rhs=r_t3, start=True, stop=True)
            ctx.op("dve", "tensor_mul", out=KC[h][:], in0=KC[h][:], in1=psf[1][:, 0:TT])
            ctx.op("act", "activation", out=SEXP[:], in_=QC[h][:], func=AF.Square)
            ctx.op("pe", "matmul", out=psf[0][0:1, 0:TT], lhsT=c["ones_col_f"][:], rhs=SEXP[:], start=True, stop=True)
            ctx.op("act", "activation", out=r_t3, in_=psf[0][0:1, 0:TT], func=AF.Sqrt, bias=c["cv"][0:1, 2:3])
            ctx.op("dve", "reciprocal", out=r_t3, in_=r_t3)
            ctx.op("dve", "tensor_scalar_mul", out=r_t3, in0=r_t3, scalar1=128.0 ** -0.5)
            ctx.op("pe", "matmul", out=psf[1][:, 0:TT], lhsT=c["ones_row_f"][:], rhs=r_t3, start=True, stop=True)
            ctx.op("dve", "tensor_mul", out=QC[h][:], in0=QC[h][:], in1=psf[1][:, 0:TT])
            ctx.op("dve", "tensor_scalar_add", out=G1[:], in0=G1[:], scalar1=DTBC[:, h:h + 1])
            ctx.op("act", "activation", out=SEXP[:], in_=G1[:], func=AF.Relu)
            ctx.op("act", "activation", out=G1[:], in_=G1[:], func=AF.Abs)
            ctx.op("act", "activation", out=G1[:], in_=G1[:], func=AF.Exp, scale=-1.0)
            ctx.op("act", "activation", out=G1[:], in_=G1[:], func=AF.Ln, bias=c["cv"][:, 6:7])
            ctx.op("dve", "tensor_add", out=G1[:], in0=G1[:], in1=SEXP[:])
            ctx.op("dve", "tensor_scalar_mul", out=G1[:], in0=G1[:], scalar1=NEAC[:, h:h + 1])
            emit_cumsum128(ctx, c, psf[2], psf[5], G1, BQ, 128, TT, R1)
            bq4 = BQ[:].rearrange("p (c x) -> p c x", c=NCH)
            ctx.op("dve", "memset", MBALL[:].rearrange("p (c x) -> p c x", c=NCH)[:, :, 0:16], 0.0)
            bq4i = BQ[:].rearrange("p (c i t) -> p c i t", c=NCH, i=8)
            ctx.op("dve", "tensor_copy",
                   out=MBALL[:].rearrange("p (c i t) -> p c i t", c=NCH, i=8)[:, :, 1:8, :],
                   in_=bq4i[:, :, 0:7, 15:16].broadcast_to([128, NCH, 7, 16]))
            ctx.op("dve", "tensor_sub", out=SEXP[:], in0=BQ[:], in1=MBALL[:])
            ctx.op("act", "activation", out=SEXP[:], in_=SEXP[:], func=AF.Exp)
            ctx.op("dve", "tensor_mul", out=UQ[:, 0:TT], in0=SEXP[:], in1=KC[h][:])
            ctx.op("dve", "tensor_mul", out=UQ[:, TT:2 * TT], in0=SEXP[:], in1=QC[h][:])
            ctx.op("act", "activation", out=SEXP[:], in_=BQ[:], func=AF.Exp)
            ctx.op("dve", "tensor_mul", out=UC[:], in0=SEXP[:], in1=KC[h][:])
            ctx.op("dve", "tensor_mul", out=QCC[:], in0=SEXP[:], in1=QC[h][:])
            ctx.op("dve", "tensor_tensor", out=SEXP[:],
                   in0=bq4[:, :, 127:128].broadcast_to([128, NCH, 128]), in1=BQ[:].rearrange("p (c x) -> p c x", c=NCH),
                   op=ALU.subtract)
            ctx.op("act", "activation", out=SEXP[:], in_=SEXP[:], func=AF.Exp)
            ctx.op("dve", "tensor_mul", out=WP[:], in0=SEXP[:], in1=KC[h][:])
            tpp = [psf[4][:, 0:256], psf[4][:, 256:512], psf[3][:, 0:256], psf[3][:, 256:512]]
            ctx.op("dve", "memset", psf[4][:, 0:512], 0.0)
            ctx.op("dve", "memset", psf[3][:, 0:512], 0.0)
            for I in range(1, 8):
                w = 16 * I
                ctx.op("dve", "tensor_tensor", out=sc["tmp"][0][:, 0:NCH * w].rearrange("p (c x) -> p c x", c=NCH),
                       in0=bq4[:, :, w - 1:w].broadcast_to([128, NCH, w]), in1=bq4[:, :, 0:w],
                       op=ALU.subtract)
                ctx.op("act", "activation", out=sc["tmp"][0][:, 0:NCH * w], in_=sc["tmp"][0][:, 0:NCH * w], func=AF.Exp)
                ctx.op("dve", "tensor_tensor", out=WBL[0][:, 0:NCH * w].rearrange("p (c x) -> p c x", c=NCH),
                       in0=sc["tmp"][0][:, 0:NCH * w].rearrange("p (c x) -> p c x", c=NCH),
                       in1=KC[h][:].rearrange("p (c x) -> p c x", c=NCH)[:, :, 0:w], op=ALU.mult)
                for cc in range(NCH):
                    ctx.op("pe", "matmul", out=tpp[cc][0:w, I * 32:I * 32 + 32],
                           lhsT=WBL[0][:, 0:NCH * w].rearrange("p (c x) -> p c x", c=NCH)[:, cc, :],
                           rhs=UQ[:].rearrange("p (u x) -> p u x", u=2)[:, :, cc * 128 + w:cc * 128 + w + 16],
                           start=True, stop=True)
            if dbg and it == 0 and h == 0:
                ctx.op("sp", "dma_start", out=dBQ[:, tsl], in_=BQ[:])
                ctx.op("pool", "dma_start", out=dKN[:], in_=KC[h][:])
                ctx.op("pool", "dma_start", out=dQN[:], in_=QC[h][:])
            for cc in range(NCH):
                cs = slice(cc * 128, (cc + 1) * 128)
                bqc = BQ[:, cs].rearrange("p (i t) -> p i t", i=8)
                kcc = KC[h][:, cs].rearrange("p (i t) -> p i t", i=8)
                qcc2 = QC[h][:, cs].rearrange("p (i t) -> p i t", i=8)
                ctx.op("dve", "tensor_tensor", out=DPAIR[0][:].rearrange("p (i s t) -> p i s t", i=8, s=16),
                       in0=bqc.unsqueeze(2).broadcast_to([128, 8, 16, 16]),
                       in1=bqc.unsqueeze(3).broadcast_to([128, 8, 16, 16]), op=ALU.subtract)
                ctx.op("dve", "tensor_scalar_min", out=DPAIR[0][:], in0=DPAIR[0][:], scalar1=0.0)
                ctx.op("act", "activation", out=DPAIR[0][:], in_=DPAIR[0][:], func=AF.Exp)
                ctx.op("dve", "tensor_tensor", out=KS1[:].rearrange("p (i s t) -> p i s t", i=8, s=16),
                       in0=DPAIR[0][:].rearrange("p (i s t) -> p i s t", i=8, s=16),
                       in1=kcc.unsqueeze(3).broadcast_to([128, 8, 16, 16]), op=ALU.mult)
                ctx.op("dve", "tensor_tensor", out=TPB[:].rearrange("p (i s t) -> p i s t", i=8, s=16),
                       in0=KS1[:].rearrange("p (i s t) -> p i s t", i=8, s=16),
                       in1=kcc.unsqueeze(2).broadcast_to([128, 8, 16, 16]), op=ALU.mult)
                ctx.op("dve", "tensor_tensor", out=PPB[:].rearrange("p (i s t) -> p i s t", i=8, s=16),
                       in0=KS1[:].rearrange("p (i s t) -> p i s t", i=8, s=16),
                       in1=qcc2.unsqueeze(2).broadcast_to([128, 8, 16, 16]), op=ALU.mult)
                for src_, dst_ in ((TPB, TD), (PPB, PD)):
                    for seg in range(4):
                        ctx.op("pe", "matmul", out=psf[seg % 2][0:1, 0:512], lhsT=c["ones_col_f"][:],
                               rhs=src_[:, seg * 512:(seg + 1) * 512], start=True, stop=True)
                        ctx.op("dve", "tensor_copy", out=TROW[0:1, seg * 512:(seg + 1) * 512],
                               in_=psf[seg % 2][0:1, 0:512])
                    ctx.op("act", "dma_start", out=dst_[:], in_=TROW[:])
                tdm = TD[:].unsqueeze(1).broadcast_to([128, 8, 16])
                pdm = PD[:].unsqueeze(1).broadcast_to([128, 8, 16])
                ctx.op("dve", "tensor_mul", out=R1[:].rearrange("p (i t) -> p i t", i=8),
                       in0=tdm, in1=c["tblk"][:].rearrange("p (i t) -> p i t", i=8))
                ctx.op("dve", "tensor_tensor", out=TS[:].rearrange("p (i t) -> p i t", i=8),
                       in0=R1[:].rearrange("p (i t) -> p i t", i=8),
                       in1=tpp[cc].rearrange("p (i u t) -> p i u t", i=8, u=2)[:, :, 0, :],
                       op=ALU.add)
                ctx.op("dve", "tensor_mul", out=R1[:].rearrange("p (i t) -> p i t", i=8),
                       in0=pdm, in1=c["pblk"][:].rearrange("p (i t) -> p i t", i=8))
                ctx.op("dve", "tensor_tensor", out=PS_[:].rearrange("p (i t) -> p i t", i=8),
                       in0=R1[:].rearrange("p (i t) -> p i t", i=8),
                       in1=tpp[cc].rearrange("p (i u t) -> p i u t", i=8, u=2)[:, :, 1, :],
                       op=ALU.add)
                if dbg and it == 0 and h == 0 and cc == 0:
                    ctx.op("sp", "dma_start", out=dTS[:], in_=TS[:])
                    ctx.op("sp", "dma_start", out=dPS[:], in_=PS_[:])
                if dbg and it == 0 and h == 0:
                    ctx.op("sp", "dma_start", out=dTS4[:, cc * 128:(cc + 1) * 128], in_=TS[:])
                ctx.op("pe", "transpose", out=psf[2][:, 0:128], in_=TS[:], identity=c["ident_f"][:])
                ctx.op("dve", "tensor_scalar", out=AB_[:], in0=psf[2][:, 0:128],
                       scalar1=BCOL[:, h * 4 + cc:h * 4 + cc + 1], scalar2=-1.0,
                       op0=ALU.mult, op1=ALU.mult)
                ctx.op("pe", "transpose", out=psf[2][:, 128:256], in_=AB_[:], identity=c["ident_f"][:])
                ctx.op("dve", "tensor_copy", out=ABT[:], in_=psf[2][:, 128:256])
                ctx.op("pe", "matmul", out=psf[5][:, 0:128], lhsT=UC[:, cs],
                       rhs=S[:, h * 128:(h + 1) * 128], start=True, stop=True)
                ctx.op("pe", "transpose", out=psb[0][:, 256:384], in_=VC[h][:, cs], identity=c["ident_b"][:])
                ctx.op("dve", "tensor_copy", out=VTS[:], in_=psb[0][:, 256:384])
                ctx.op("dve", "tensor_sub", out=R1[:], in0=VTS[:], in1=psf[5][:, 0:128])
                ctx.op("dve", "tensor_scalar_mul", out=XDF[:], in0=R1[:],
                       scalar1=BCOL[:, h * 4 + cc:h * 4 + cc + 1])
                if dbg and it == 0 and h == 0:
                    ctx.op("sp", "dma_start", out=dRH[:, cc * 128:(cc + 1) * 128], in_=XDF[:])
                apw, apwt = AB_, ABT
                for lev in range(7):
                    ctx.op("pe", "matmul", out=psf[5][:, 128:256], lhsT=apwt[:], rhs=XDF[:],
                           start=True, stop=True)
                    ctx.op("dve", "tensor_add", out=XDF[:], in0=XDF[:], in1=psf[5][:, 128:256])
                    if lev < 6:
                        nb = APW[lev % 2]
                        nbt = APWT[lev % 2]
                        ctx.op("pe", "matmul", out=psf[5][:, 256:384], lhsT=apwt[:], rhs=apw[:],
                               start=True, stop=True)
                        ctx.op("dve", "tensor_copy", out=nb[:], in_=psf[5][:, 256:384])
                        ctx.op("pe", "transpose", out=psf[2][:, 256:384], in_=nb[:], identity=c["ident_f"][:])
                        ctx.op("dve", "tensor_copy", out=nbt[:], in_=psf[2][:, 256:384])
                        apw, apwt = nb, nbt
                if dbg and it == 0 and h == 0 and cc == 0:
                    ctx.op("sp", "dma_start", out=dXD[:], in_=XDF[:])
                ctx.op("pe", "matmul", out=psf[5][:, 0:128], lhsT=QCC[:, cs],
                       rhs=S[:, h * 128:(h + 1) * 128], start=True, stop=False)
                ctx.op("pe", "matmul", out=psf[5][:, 0:128], lhsT=PS_[:], rhs=XDF[:],
                       start=False, stop=True)
                if dbg and it == 0 and h == 0:
                    ctx.op("dve", "tensor_copy", out=ONF[:], in_=psf[5][:, 0:128])
                    ctx.op("sp", "dma_start", out=dO4[:, cc * 128:(cc + 1) * 128], in_=ONF[:])
                ctx.op("act", "activation", out=ONB[:], in_=psf[5][:, 0:128], func=AF.Square)
                ctx.op("dve", "tensor_reduce", out=OCOL[:, 0:1], in_=ONB[:], axis=AX.X, op=ALU.add)
                ctx.op("act", "activation", out=OCOL[:, 1:2], in_=OCOL[:, 0:1], func=AF.Sqrt,
                       scale=1.0 / 128, bias=c["cv"][:, 0:1])
                ctx.op("dve", "reciprocal", out=OCOL[:, 1:2], in_=OCOL[:, 1:2])
                ctx.op("dve", "tensor_scalar_mul", out=ONF[:], in0=psf[5][:, 0:128],
                       scalar1=OCOL[:, 1:2])
                ctx.op("dve", "tensor_mul", out=ONB[:], in0=ONF[:], in1=NW[:])
                ctx.op("pe", "transpose", out=psb[1][:, 256:384], in_=ONB[:], identity=c["ident_b"][:])
                ctx.op("dve", "tensor_copy", out=OD[h][:, cs], in_=psb[1][:, 256:384])
                ctx.op("pe", "transpose", out=psf[2][:, 384:512], in_=WP[:, cs], identity=c["ident_f"][:])
                ctx.op("dve", "tensor_copy", out=VTS[:], in_=psf[2][:, 384:512])
                ctx.op("pe", "matmul", out=psf[5][:, 128:256], lhsT=VTS[:], rhs=XDF[:],
                       start=True, stop=True)
                ctx.op("act", "activation", out=OCOL[:, 2:3], in_=BQ[:, cc * 128 + 127:cc * 128 + 128],
                       func=AF.Exp)
                ctx.op("dve", "scalar_tensor_tensor", out=S[:, h * 128:(h + 1) * 128],
                       in0=S[:, h * 128:(h + 1) * 128], scalar=OCOL[:, 2:3],
                       in1=psf[5][:, 128:256], op0=ALU.mult, op1=ALU.add)
                if dbg and it == 0 and h == 0:
                    ctx.op("sp", "dma_start", out=dSS[:, cc * 128:(cc + 1) * 128],
                           in_=S[:, 0:128])
                    ctx.op("sp", "dma_start", out=dXD4[:, cc * 128:(cc + 1) * 128], in_=XDF[:])

        if dbg and it == 0:
            ctx.op("sp", "dma_start", out=dBC[:], in_=BCOL[:])
            for ch in range(16):
                ctx.op("pool", "dma_start", out=dOD[ch * 128:(ch + 1) * 128, tsl], in_=OD[ch][:])
        for j in range(4):
            wt = WQ[j % 2]
            ctx.op("pool", "dma_start", out=wt[:], in_=wg2T[:, j * 512:(j + 1) * 512])
            for oc in range(4):
                ch = j * 4 + oc
                ctx.op("pe", "matmul", out=psf[1 + (ch % 2)][:, 0:TT], lhsT=wt[:, oc * 128:(oc + 1) * 128],
                       rhs=G1GATE[:], start=True, stop=True)
                ctx.op("act", "activation", out=SGT[:], in_=psf[1 + (ch % 2)][:, 0:TT],
                       func=AF.Sigmoid, bias=BG2C[:, ch:ch + 1])
                ctx.op("dve", "tensor_mul", out=OD[ch][:], in0=OD[ch][:], in1=SGT[:])

        for j in range(4):
            emit_slab_mm(ctx, psf, WQ, woT, j * 512, OD, TT, queue="pool")
            for oc in range(4):
                i = j * 4 + oc
                ctx.op("dve", "tensor_copy", out=FO[i][:], in_=psf[oc][:, 0:TT])

        emit_ln_stream_add_f32(ctx, c, sc, psf[4], psf[5], psf[3], FO, [SEXP, G1], sc["tmp"], rx, ox, tsl, TT)
        t0 += TTt

    ctx.prefix = ""
    ctx.paranoid_flag = False
    ctx.pop_scope()
    if standalone:
        ctx.finish()
    return nc


def emit_ln_stream_add_f32(ctx, c, sc, ps_stat, ps_b0, ps_b1, FO, FSQ, XR2, rx, ox, tsl, TT):
    row = sc["row"]
    m_row = row[0:1, 0:TT]
    r_row = row[0:1, TT:2 * TT]
    t_row = row[0:1, 2 * TT:3 * TT]
    for i in range(NC):
        b = i % 2
        ctx.op("act", "activation", out=FSQ[b][:], in_=FO[i][:], func=AF.Square)
        ctx.op("pe", "matmul", out=ps_stat[0:1, 0:TT], lhsT=c["ones_col_f"][:], rhs=FO[i][:],
               start=(i == 0), stop=(i == NC - 1))
        ctx.op("pe", "matmul", out=ps_b0[0:1, 0:TT], lhsT=c["ones_col_f"][:], rhs=FSQ[b][:],
               start=(i == 0), stop=(i == NC - 1))
    ctx.op("dve", "tensor_scalar_mul", out=m_row, in0=ps_stat[0:1, 0:TT], scalar1=1.0 / D)
    ctx.op("act", "activation", out=t_row, in_=m_row, func=AF.Square)
    ctx.op("dve", "scalar_tensor_tensor", out=r_row, in0=ps_b0[0:1, 0:TT], scalar=1.0 / D,
           in1=t_row, op0=ALU.mult, op1=ALU.subtract)
    ctx.op("act", "activation", out=r_row, in_=r_row, func=AF.Sqrt, bias=c["cv"][0:1, 0:1])
    ctx.op("dve", "reciprocal", out=r_row, in_=r_row)
    ctx.op("pe", "matmul", out=ps_b0[:, 0:TT], lhsT=c["ones_row_f"][:], rhs=m_row, start=True, stop=True)
    ctx.op("pe", "matmul", out=ps_b1[:, 0:TT], lhsT=c["ones_row_f"][:], rhs=r_row, start=True, stop=True)
    for i in range(NC):
        b = i % 2
        ctx.op("sp", "dma_start", out=XR2[b][:], in_=rx[i * 128:(i + 1) * 128, tsl])
        ctx.op("dve", "tensor_sub", out=FO[i][:], in0=FO[i][:], in1=ps_b0[:, 0:TT])
        ctx.op("dve", "tensor_mul", out=FO[i][:], in0=FO[i][:], in1=ps_b1[:, 0:TT])
        ctx.op("dve", "tensor_add", out=FO[i][:], in0=FO[i][:], in1=XR2[b][:])
        ctx.op("sp", "dma_start", out=ox[i * 128:(i + 1) * 128, tsl], in_=FO[i][:])


def build_full(TT, tiles):
    nc = bass.Bass("TRN2", target_bir_lowering=False, debug=False)
    T = sum(tiles)
    rx = nc.dram_tensor("rx", [D, T], F32, kind="ExternalInput").ap()
    r1 = nc.dram_tensor("resid1", [D, T], F32, kind="Internal").ap()
    r2 = nc.dram_tensor("resid2", [D, T], F32, kind="Internal").ap()
    r3 = nc.dram_tensor("resid3", [D, T], F32, kind="Internal").ap()
    ox = nc.dram_tensor("ox", [D, T], F32, kind="ExternalOutput").ap()
    ctx = Ctx(nc)
    c = make_consts(ctx)
    build_l0attn(TT, tiles, nc=nc, ctx=ctx, c=c, rx=rx, ox_=r1)
    build_l0ffn(TT, tiles, nc=nc, ctx=ctx, c=c, rx=r1, ox_=r2)
    build_l1attn(TT, tiles, nc=nc, ctx=ctx, c=c, rx=r2, ox_=r3)
    build_l1ffn(TT, tiles, nc=nc, ctx=ctx, c=c, rx=r3, ox_=ox)
    ctx.finish()
    return nc



def _dmajor(x):
    return np.ascontiguousarray(x.T)


_PROGS = {}


def _progs(TT, tiles):
    key = (TT, tuple(tiles))
    if key not in _PROGS:
        _PROGS[key] = build_full(TT, tiles)
    return _PROGS[key]


def kernel(**inputs):
    TT = 512
    tiles = [TT] * 4
    ncf = _progs(TT, tiles)
    cons = consts_host()
    ct = lambda a: np.ascontiguousarray(a.astype(np.float32))
    g = lambda k: inputs[k]

    w0 = {"W_inT": ct(g("L0_attn_W_in").T),
          "W_outT": ct(g("L0_attn_W_out").T),
          "conv_w": ct(g("L0_attn_conv_w")[:, 0, :].reshape(34, 128, 4).transpose(1, 0, 2).reshape(128, 136)),
          "conv_b": ct(g("L0_attn_conv_b").reshape(34, 128).T),
          "dt_bias": ct(g("L0_attn_dt_bias")[:, None]),
          "A_log": ct(g("L0_attn_A_log")[:, None]),
          "D_vec": ct(np.repeat(g("L0_attn_D"), 64).reshape(32, 128).T),
          "norm_w": ct(g("L0_attn_norm_w").reshape(32, 128).T),
          "consts_pack": cons}
    w1 = {"w_guT": ct(g("L0_ffn_w_gate_up").T),
          "w_pT": ct(g("L0_ffn_w_proj").T),
          "consts_pack": cons}
    mlp = np.zeros((1, 464), np.float32)
    mlp[0, 0:96] = g("L1_attn_Wb1").T.reshape(-1)
    mlp[0, 96:192] = g("L1_attn_Wa1").T.reshape(-1)
    mlp[0, 192:224] = g("L1_attn_Wb2")[0]
    mlp[0, 224:256] = g("L1_attn_Wa2")[0]
    mlp[0, 256:288] = g("L1_attn_bb1")
    mlp[0, 288:320] = g("L1_attn_ba1")
    mlp[0, 384] = g("L1_attn_bb2")[0]
    mlp[0, 385] = g("L1_attn_ba2")[0]
    w2 = {"WqT": ct(g("L1_attn_Wq").T), "WkT": ct(g("L1_attn_Wk").T),
          "WvT": ct(g("L1_attn_Wv").T), "WoT": ct(g("L1_attn_Wo").T),
          "Wg1T": ct(g("L1_attn_Wg1").T), "Wg2T": ct(g("L1_attn_Wg2").T),
          "WsvT": ct(g("L1_attn_Wsv").T),
          "conv_q": ct(g("L1_attn_conv_q").reshape(16, 128, 4).transpose(1, 0, 2).reshape(128, 64)),
          "conv_k": ct(g("L1_attn_conv_k").reshape(16, 128, 4).transpose(1, 0, 2).reshape(128, 64)),
          "conv_v": ct(g("L1_attn_conv_v").reshape(16, 128, 4).transpose(1, 0, 2).reshape(128, 64)),
          "A_exp": ct(np.repeat(g("L1_attn_A_log"), 128).reshape(16, 128).T),
          "dt_bias1": ct(g("L1_attn_dt_bias").reshape(16, 128).T),
          "bg2": ct(g("L1_attn_bg2").reshape(16, 128).T),
          "norm_w_row": ct(g("L1_attn_norm_w")[None, :]),
          "mlp_w": mlp,
          "mb2_w": ct(np.stack([g("L1_attn_Wb2")[0], g("L1_attn_Wa2")[0]], 1)),
          "bb1_w": ct(np.stack([g("L1_attn_bb1"), g("L1_attn_ba1")], 1)),
          "consts_pack": cons}
    w3 = {"W_upT": ct(g("L1_ffn_W_up").T), "W_gateT": ct(g("L1_ffn_W_gate").T),
          "W_downT": ct(g("L1_ffn_W_down").T), "consts_pack": cons}

    x = g("L0_attn_u")
    wall = {}
    for wm in (w0, w1, w2, w3):
        wall.update(wm)
    in_maps = [dict(wall, rx=_dmajor(x[i])) for i in range(NCORES)]
    res = run_bass_kernel_spmd(ncf, in_maps, core_ids=list(range(NCORES)))
    return np.stack([res.results[i]["ox"].T for i in range(NCORES)]).astype(np.float32)


# version 07
# speedup vs baseline: 1.4500x; 1.1483x over previous
import numpy as np
from contextlib import ExitStack
import concourse.bass as bass
import concourse.mybir as mybir
from concourse.bass_utils import run_bass_kernel_spmd

F32 = mybir.dt.float32
BF16 = mybir.dt.bfloat16
AF = mybir.ActivationFunctionType
ALU = mybir.AluOpType
AX = mybir.AxisListType

D = 2048
NCORES = 8
NC = D // 128
LN_EPS = 1e-5

ENGS = ("pe", "dve", "act", "pool", "sp")
WRITE_KEYS = ("out", "accum_out", "out_max", "out_indices")
DMA_SEMS = {"sp": ["spd%d" % i for i in range(8)],
            "pool": ["pld%d" % i for i in range(4)],
            "act": ["acd%d" % i for i in range(2)],
            "dve": ["dvd%d" % i for i in range(2)]}


_SEQ = [0]


class _Op:
    __slots__ = ("eng", "meth", "args", "kwargs", "deps", "need_inc", "sem", "cnt",
                 "is_dma", "seq", "paranoid")

    def __init__(self, eng, meth, args, kwargs):
        _SEQ[0] += 1
        self.seq = _SEQ[0]
        self.paranoid = False
        self.eng = eng
        self.meth = meth
        self.args = args
        self.kwargs = kwargs
        self.deps = []
        self.need_inc = False
        self.sem = None
        self.cnt = None
        self.is_dma = meth.startswith("dma")


def _aps_of(obj):
    if isinstance(obj, bass.AP):
        return [obj]
    return []


class Ctx:

    def __init__(self, nc):
        self.nc = nc
        self.es = ExitStack()
        self.q = {e: [] for e in ENGS}
        self.sem_names = ["pe", "dve", "act", "pool"] + sum(DMA_SEMS.values(), [])
        self.sems = {s: self.es.enter_context(nc.semaphore("sem_" + s))
                     for s in self.sem_names}
        self.mem = {}
        self._dma_rr = {e: 0 for e in DMA_SEMS}
        self._dma_last = {s: None for s in sum(DMA_SEMS.values(), [])}
        self._names = set()
        self._barrier_ops = []
        self._scopes = [self.es]
        self.prefix = ""
        self.paranoid_flag = False

    def sb(self, name, shape, dt):
        name = self.prefix + name
        assert name not in self._names, name
        self._names.add(name)
        return self._scopes[-1].enter_context(self.nc.sbuf_tensor(name, shape, dt))

    def psum(self, name, shape, dt=F32):
        name = self.prefix + name
        assert name not in self._names, name
        self._names.add(name)
        return self._scopes[-1].enter_context(self.nc.psum_tensor(name, shape, dt))

    def push_scope(self):
        s = ExitStack()
        self._scopes.append(s)
        return s

    def pop_scope(self):
        self.hard_barrier()
        self._scopes.pop().close()

    def scope(self):
        return ExitStack()

    def sb_s(self, scope, name, shape, dt):
        name = self.prefix + name
        assert name not in self._names, name
        self._names.add(name)
        return scope.enter_context(self.nc.sbuf_tensor(name, shape, dt))

    def close_scope(self, scope):
        self.hard_barrier()
        scope.close()

    def hard_barrier(self):
        self._barrier_ops = [q[-1] for q in self.q.values() if q]

    def _slot(self, name):
        s = self.mem.get(name)
        if s is None:
            s = self.mem[name] = {"w": None, "r": {}}
        return s

    def op(self, eng, meth, *args, **kwargs):
        if meth == "transpose" and eng == "pe":
            meth = "matmul"
            kwargs = {"out": kwargs["out"], "lhsT": kwargs["in_"],
                      "rhs": kwargs["identity"], "start": True, "stop": True}
        op = _Op(eng, meth, args, kwargs)
        op.paranoid = self.paranoid_flag
        if op.is_dma:
            pool = DMA_SEMS[eng]
            idx = self._dma_rr[eng] % len(pool)
            self._dma_rr[eng] += 1
            op.sem = pool[idx]
            prev = self._dma_last[op.sem]
            if prev is not None:
                op.deps.append(prev)
            self._dma_last[op.sem] = op
        reads, writes = [], []
        for k, v in kwargs.items():
            for ap in _aps_of(v):
                (writes if k in WRITE_KEYS else reads).append(ap)
        for i, a in enumerate(args):
            if i == 0 and meth in ("memset", "memzero"):
                writes.extend(_aps_of(a))
            else:
                reads.extend(_aps_of(a))
        if op.meth == "matmul" and not kwargs.get("start", True):
            reads.extend(_aps_of(kwargs.get("out")))
        rd_names = {a.tensor.name for a in reads}
        wr_names = {a.tensor.name for a in writes}
        for b in self._barrier_ops:
            self._dep(op, b)
        for n in rd_names:
            s = self._slot(n)
            if s["w"] is not None:
                self._dep(op, s["w"])
        for n in wr_names:
            s = self._slot(n)
            if s["w"] is not None:
                self._dep(op, s["w"])
            for r in s["r"].values():
                self._dep(op, r)
        for n in rd_names:
            s = self._slot(n)
            key = op.sem if op.is_dma else op.eng
            s["r"][key] = op
        for n in wr_names:
            self.mem[n] = {"w": op, "r": {}}
        self.q[eng].append(op)
        return op

    def _dep(self, op, prod):
        if prod is op:
            return
        if prod.eng == "pe" and op.eng == "pe" and not prod.is_dma and not op.is_dma:
            return
        op.deps.append(prod)

    def finish(self, paranoid=False):
        import bisect
        any_par = paranoid or any(op.paranoid for e in ENGS for op in self.q[e])
        if any_par:
            for e in ENGS:
                for op in self.q[e]:
                    op.need_inc = True
        for e in ENGS:
            for op in self.q[e]:
                for d in op.deps:
                    d.need_inc = True
        counts = {s: 0 for s in self.sem_names}
        for e in ENGS:
            for op in self.q[e]:
                if op.is_dma:
                    counts[op.sem] += 16
                    op.cnt = counts[op.sem]
                elif op.need_inc:
                    op.sem = op.eng
                    counts[op.sem] += 1
                    op.cnt = counts[op.sem]
        self.final_counts = counts
        pred = {}
        if any_par:
            allops = sorted((op.seq, op) for e in ENGS for op in self.q[e])
            last = None
            for seq, op in allops:
                pred[seq] = last
                if op.sem is not None:
                    last = op
        with self.nc.Block() as block:
            for eng, attr in (("pe", "tensor"), ("dve", "vector"), ("act", "scalar"),
                              ("pool", "gpsimd"), ("sp", "sync")):
                queue = self.q[eng]

                def body(E, queue=queue, eng=eng):
                    hw = {}
                    for op in queue:
                        want = {}
                        for d in op.deps:
                            if d.sem is None:
                                continue
                            want[d.sem] = max(want.get(d.sem, 0), d.cnt)
                        if paranoid or op.paranoid:
                            p = pred.get(op.seq)
                            if p is not None and p.eng != op.eng:
                                want[p.sem] = max(want.get(p.sem, 0), p.cnt)
                            if op.sem is not None:
                                cap = op.cnt - (16 if op.is_dma else 1)
                                if want.get(op.sem, 0) > cap:
                                    want[op.sem] = cap
                        for s, v in sorted(want.items()):
                            if hw.get(s, 0) < v:
                                E.wait_ge(self.sems[s], v)
                                hw[s] = v
                        instr = getattr(E, op.meth)(*op.args, **op.kwargs)
                        if op.sem is not None and (op.is_dma or op.need_inc):
                            instr.then_inc(self.sems[op.sem], 16 if op.is_dma else 1)
                    for s in DMA_SEMS.get(eng, []):
                        if counts[s] > 0:
                            E.wait_ge(self.sems[s], counts[s])
                getattr(block, attr)(body)
        self.es.close()



def consts_host():
    pack = np.zeros((128, 1024), np.float32)
    pack[:, 0:128] = np.eye(128, dtype=np.float32)
    s = np.arange(128)[:, None]
    t = np.arange(128)[None, :]
    pack[:, 128:256] = (s <= t).astype(np.float32)
    pack[:, 256:384] = (s < t).astype(np.float32)
    pack[:, 512] = 1.0
    pack[:64, 515] = 1.0
    pack[64:, 516] = 1.0
    pack[:, 520] = 1e-5
    pack[:, 521] = 1.1920929e-7
    pack[:, 522] = 1e-6
    pack[:, 523] = 128.0 ** -0.5
    pack[:, 524] = 1e-12
    pack[:, 525] = 0.5
    pack[:, 526] = 1.0
    p16 = np.arange(128)[:, None] // 16
    c16 = np.arange(128)[None, :] // 16
    po = np.arange(128)[:, None] % 16
    co = np.arange(128)[None, :] % 16
    pack[:, 384:512] = ((p16 == c16) & (po < co)).astype(np.float32)
    pack[:, 896:1024] = ((p16 == c16) & (po <= co)).astype(np.float32)
    pack[0, 640:704] = 1.0
    pack[1, 704:768] = 1.0
    pack[0, 768:896] = 1.0
    return pack


def make_consts(ctx, lite=False, blk=False):
    nc = ctx.nc
    c = {}
    c["ident_f"] = ctx.sb("c_ident_f", [128, 128], F32)
    c["ident_b"] = ctx.sb("c_ident_b", [128, 128], BF16)
    c["ones_col_b"] = ctx.sb("c_ones_col_b", [128, 1], BF16)
    c["ones_col_f"] = ctx.sb("c_ones_col_f", [128, 1], F32)
    c["ones_row_f"] = ctx.sb("c_ones_row_f", [1, 128], F32)
    c["ones_row_b"] = ctx.sb("c_ones_row_b", [1, 128], BF16)
    c["mask_incl_f"] = ctx.sb("c_mask_incl_f", [128, 128], F32)
    c["cv"] = ctx.sb("c_cv", [128, 16], F32)
    dr = getattr(ctx, "_consts_dram", None)
    if dr is None:
        dr = nc.dram_tensor("consts_pack", [128, 1024], F32, kind="ExternalInput").ap()
    ctx.op("sp", "dma_start", out=c["ident_f"][:], in_=dr[:, 0:128])
    ctx.op("sp", "dma_start", out=c["mask_incl_f"][:], in_=dr[:, 128:256])
    ctx.op("sp", "dma_start", out=c["ones_col_f"][:], in_=dr[:, 512:513],
           allow_slow_non_contiguous=True)
    ctx.op("sp", "dma_start", out=c["ones_row_f"][:], in_=dr[0:1, 768:896])
    ctx.op("sp", "dma_start", out=c["cv"][:], in_=dr[:, 520:536])
    ctx.op("pool", "dma_start", out=c["ident_b"][:], in_=dr[:, 0:128])
    ctx.op("pool", "dma_start", out=c["ones_col_b"][:], in_=dr[:, 512:513],
           allow_slow_non_contiguous=True)
    ctx.op("pool", "dma_start", out=c["ones_row_b"][:], in_=dr[0:1, 768:896])
    c["_dr"] = dr
    return c


def load_ffnq_consts(ctx, c):
    dr = c["_dr"]
    g2 = ctx.sb("c_g2_b", [128, 2], BF16)
    sel = ctx.sb("c_sel_half_b", [2, 128], BF16)
    ctx.op("pool", "dma_start", out=g2[:], in_=dr[:, 515:517],
           allow_slow_non_contiguous=True)
    ctx.op("pool", "dma_start", out=sel[:], in_=dr[0:2, 640:768])
    return dict(c, g2_b=g2, sel_half_b=sel)


def load_blk_consts(ctx, c):
    dr = c["_dr"]
    tb = ctx.sb("c_tblk", [128, 128], F32)
    pb_ = ctx.sb("c_pblk", [128, 128], F32)
    ctx.op("sp", "dma_start", out=tb[:], in_=dr[:, 384:512])
    ctx.op("sp", "dma_start", out=pb_[:], in_=dr[:, 896:1024])
    return dict(c, tblk=tb, pblk=pb_)



def emit_ln(ctx, c, sc, ps_stat, ps_b0, ps_b1, X, XBF, TT,
            out_bf=None, resid=None, out_f32=None):
    sq = sc["sq"]
    row = sc["row"]
    tmp = sc["tmp"]
    m_row = row[0:1, 0:TT]
    r_row = row[0:1, TT:2 * TT]
    t_row = row[0:1, 2 * TT:3 * TT]
    for i in range(NC):
        b = i % 2
        ctx.op("act", "activation", out=sq[b][:], in_=X[i][:], func=AF.Square)
        ctx.op("pe", "matmul", out=ps_stat[0:1, 0:TT], lhsT=c["ones_col_b"][:], rhs=XBF[i][:],
               start=(i == 0), stop=(i == NC - 1))
        ctx.op("pe", "matmul", out=ps_b0[0:1, 0:TT], lhsT=c["ones_col_b"][:], rhs=sq[b][:],
               start=(i == 0), stop=(i == NC - 1))
    ctx.op("dve", "tensor_scalar_mul", out=m_row, in0=ps_stat[0:1, 0:TT], scalar1=1.0 / D)
    ctx.op("act", "activation", out=t_row, in_=m_row, func=AF.Square)
    ctx.op("dve", "scalar_tensor_tensor", out=r_row, in0=ps_b0[0:1, 0:TT], scalar=1.0 / D,
           in1=t_row, op0=ALU.mult, op1=ALU.subtract)
    ctx.op("act", "activation", out=r_row, in_=r_row, func=AF.Sqrt, bias=c["cv"][0:1, 0:1])
    ctx.op("dve", "reciprocal", out=r_row, in_=r_row)
    ctx.op("pe", "matmul", out=ps_b0[:, 0:TT], lhsT=c["ones_row_f"][:], rhs=m_row, start=True, stop=True)
    ctx.op("pe", "matmul", out=ps_b1[:, 0:TT], lhsT=c["ones_row_f"][:], rhs=r_row, start=True, stop=True)
    for i in range(NC):
        b = i % 2
        ctx.op("dve", "tensor_sub", out=tmp[b][:], in0=X[i][:], in1=ps_b0[:, 0:TT])
        if out_bf is not None:
            ctx.op("dve", "tensor_mul", out=out_bf[i][:], in0=tmp[b][:], in1=ps_b1[:, 0:TT])
        else:
            ctx.op("dve", "tensor_mul", out=tmp[b][:], in0=tmp[b][:], in1=ps_b1[:, 0:TT])
            ctx.op("dve", "tensor_add", out=out_f32[i][:], in0=resid[i][:], in1=tmp[b][:])


def emit_ln_stream(ctx, c, sc, ps_stat, ps_b0, ps_b1, src, tsl, XB2, out_bf, TT):
    sq = sc["sq"]
    row = sc["row"]
    m_row = row[0:1, 0:TT]
    r_row = row[0:1, TT:2 * TT]
    t_row = row[0:1, 2 * TT:3 * TT]
    for i in range(NC):
        b = i % 2
        ctx.op("pool", "dma_start", out=XB2[b][:], in_=src[i * 128:(i + 1) * 128, tsl])
        ctx.op("act", "activation", out=sq[b][:], in_=XB2[b][:], func=AF.Square)
        ctx.op("pe", "matmul", out=ps_stat[0:1, 0:TT], lhsT=c["ones_col_b"][:], rhs=XB2[b][:],
               start=(i == 0), stop=(i == NC - 1))
        ctx.op("pe", "matmul", out=ps_b0[0:1, 0:TT], lhsT=c["ones_col_b"][:], rhs=sq[b][:],
               start=(i == 0), stop=(i == NC - 1))
    ctx.op("dve", "tensor_scalar_mul", out=m_row, in0=ps_stat[0:1, 0:TT], scalar1=1.0 / D)
    ctx.op("act", "activation", out=t_row, in_=m_row, func=AF.Square)
    ctx.op("dve", "scalar_tensor_tensor", out=r_row, in0=ps_b0[0:1, 0:TT], scalar=1.0 / D,
           in1=t_row, op0=ALU.mult, op1=ALU.subtract)
    ctx.op("act", "activation", out=r_row, in_=r_row, func=AF.Sqrt, bias=c["cv"][0:1, 0:1])
    ctx.op("dve", "reciprocal", out=r_row, in_=r_row)
    ctx.op("pe", "matmul", out=ps_b0[:, 0:TT], lhsT=c["ones_row_f"][:], rhs=m_row, start=True, stop=True)
    ctx.op("pe", "matmul", out=ps_b1[:, 0:TT], lhsT=c["ones_row_f"][:], rhs=r_row, start=True, stop=True)
    for i in range(NC):
        b = i % 2
        ctx.op("pool", "dma_start", out=XB2[b][:], in_=src[i * 128:(i + 1) * 128, tsl])
        ctx.op("dve", "tensor_sub", out=sc["tmp"][b][:], in0=XB2[b][:], in1=ps_b0[:, 0:TT])
        ctx.op("dve", "tensor_mul", out=out_bf[i][:], in0=sc["tmp"][b][:], in1=ps_b1[:, 0:TT])


def emit_tile_load(ctx, X, XBF, src, tsl):
    for i in range(NC):
        ctx.op("sp", "dma_start", out=X[i][:], in_=src[i * 128:(i + 1) * 128, tsl])
        ctx.op("pool", "dma_start", out=XBF[i][:], in_=src[i * 128:(i + 1) * 128, tsl])


def emit_slab_mm(ctx, ps, WQ, wsrc, col0, rhs, TT, n_oc=4, ncols=None, queue="sp"):
    nk = len(rhs)
    if ncols is None:
        ncols = n_oc * 128
    for kk in range(nk):
        b = kk % 2
        ctx.op(queue, "dma_start", out=WQ[b][:, 0:ncols],
               in_=wsrc[kk * 128:(kk + 1) * 128, col0:col0 + ncols])
        for oc in range(n_oc):
            w = min(128, ncols - oc * 128)
            ctx.op("pe", "matmul", out=ps[oc][0:w if w < 128 else 128, 0:TT],
                   lhsT=WQ[b][:, oc * 128:oc * 128 + w], rhs=rhs[kk][:],
                   start=(kk == 0), stop=(kk == nk - 1))



def build_l1ffn(TT, tiles, nc=None, ctx=None, c=None, rx=None, ox_=None):
    T = sum(tiles)
    standalone = nc is None
    if standalone:
        nc = bass.Bass("TRN2", target_bir_lowering=False, debug=False)
    if rx is None:
        rx = nc.dram_tensor("rx", [D, T], F32, kind="ExternalInput").ap()
    wupT = nc.dram_tensor("W_upT", [2048, 8192], F32, kind="ExternalInput").ap()
    wgateT = nc.dram_tensor("W_gateT", [2048, 8192], F32, kind="ExternalInput").ap()
    wdownT = nc.dram_tensor("W_downT", [8192, 2048], F32, kind="ExternalInput").ap()
    if ox_ is None:
        ox = nc.dram_tensor("ox", [D, T], F32, kind="ExternalOutput").ap()
    else:
        ox = ox_
    qup = nc.dram_tensor("qup", [2048, 8192], BF16, kind="Internal").ap()
    qgate = nc.dram_tensor("qgate", [2048, 8192], BF16, kind="Internal").ap()
    qdown = nc.dram_tensor("qdown", [8192, 2048], BF16, kind="Internal").ap()
    if ctx is None:
        ctx = Ctx(nc)
        c = make_consts(ctx)
    ctx.push_scope()
    ctx.prefix = "l1f_"
    ps = [ctx.psum("ps%d" % i, [128, 512], F32) for i in range(8)]

    srow = ctx.sb("srow", [1, 8], F32)
    grow = ctx.sb("grow", [1, 16], F32)
    scol = ctx.sb("scol", [128, 8], F32)
    WQ = [ctx.sb("wq%d" % i, [128, 512], BF16) for i in range(2)]
    qsc = ctx.scope()
    WF = [ctx.sb_s(qsc, "wf%d" % i, [128, 512], F32) for i in range(2)]
    AB = [ctx.sb_s(qsc, "ab%d" % i, [128, 512], BF16) for i in range(2)]
    SG = [ctx.sb_s(qsc, "sgn%d" % i, [128, 512], F32) for i in range(2)]
    MK = [ctx.sb_s(qsc, "msk%d" % i, [128, 512], F32) for i in range(2)]

    mats = [(wupT, qup, 2048, 8192, 0), (wgateT, qgate, 2048, 8192, 1),
            (wdownT, qdown, 8192, 2048, 2)]
    sbank = [ps[7], ps[6], ps[5]]

    for (w, q, K, O, mi) in mats:
        nk, no = K // 128, O // 512
        ntile = nk * no
        ti = 0
        for ko in range(no):
            for kk in range(nk):
                b = ti % 2
                ctx.op("sp", "dma_start", out=WF[b][:],
                       in_=w[kk * 128:(kk + 1) * 128, ko * 512:(ko + 1) * 512])
                ctx.op("act", "activation", out=AB[b][:], in_=WF[b][:], func=AF.Abs)
                ctx.op("pe", "matmul", out=sbank[mi][0:1, 0:512], lhsT=c["ones_col_b"][:],
                       rhs=AB[b][:], start=(ti == 0), stop=(ti == ntile - 1))
                ti += 1
    for (w, q, K, O, mi) in mats:
        ctx.op("dve", "tensor_reduce", out=grow[0:1, mi:mi + 1], in_=sbank[mi][0:1, 0:512],
               axis=AX.X, op=ALU.add)
        ctx.op("dve", "tensor_scalar", out=srow[0:1, mi:mi + 1], in0=grow[0:1, mi:mi + 1],
               scalar1=1.0 / (K * O), scalar2=1e-8, op0=ALU.mult, op1=ALU.add)
        ctx.op("dve", "tensor_scalar_mul", out=srow[0:1, 3 + mi:4 + mi], in0=srow[0:1, mi:mi + 1],
               scalar1=0.5)
    ctx.op("pe", "matmul", out=ps[4][:, 0:6], lhsT=c["ones_row_f"][:], rhs=srow[0:1, 0:6],
           start=True, stop=True)
    ctx.op("dve", "tensor_copy", out=scol[:, 0:6], in_=ps[4][:, 0:6])

    for (w, q, K, O, mi) in mats:
        nk, no = K // 128, O // 512
        for ko in range(no):
            for kk in range(nk):
                b = kk % 2
                ctx.op("sp", "dma_start", out=WF[b][:],
                       in_=w[kk * 128:(kk + 1) * 128, ko * 512:(ko + 1) * 512])
                ctx.op("act", "activation", out=SG[b][:], in_=WF[b][:], func=AF.Sign)
                ctx.op("act", "activation", out=MK[b][:], in_=WF[b][:], func=AF.Abs)
                ctx.op("dve", "tensor_single_scalar", out=MK[b][:], in_=MK[b][:],
                       scalar=scol[:, 3 + mi:4 + mi], op=ALU.is_gt)
                ctx.op("dve", "tensor_mul", out=WQ[b][:], in0=MK[b][:], in1=SG[b][:])
                ctx.op("sp", "dma_start",
                       out=q[kk * 128:(kk + 1) * 128, ko * 512:(ko + 1) * 512],
                       in_=WQ[b][:])

    ctx.close_scope(qsc)
    X = [ctx.sb("x%d" % i, [128, TT], F32) for i in range(NC)]
    XBF = [ctx.sb("xb%d" % i, [128, TT], BF16) for i in range(NC)]
    XN = [ctx.sb("xn%d" % i, [128, TT], BF16) for i in range(NC)]
    FO = [ctx.sb("fo%d" % i, [128, TT], F32) for i in range(NC)]
    FOB = [ctx.sb("fob%d" % i, [128, TT], BF16) for i in range(NC)]
    H = [ctx.sb("h%d" % i, [128, TT], BF16) for i in range(64)]
    SU = [ctx.sb("su%d" % i, [128, TT], BF16) for i in range(4)]
    sc = {"sq": [ctx.sb("ln_sq0", [128, TT], BF16), ctx.sb("ln_sq1", [128, TT], BF16)],
          "row": ctx.sb("ln_row", [1, 3 * TT], F32),
          "tmp": [ctx.sb("ln_tmp0", [128, TT], F32), ctx.sb("ln_tmp1", [128, TT], F32)]}

    t0 = 0
    for it, TTt in enumerate(tiles):
        assert TTt == TT
        tsl = slice(t0, t0 + TTt)
        emit_tile_load(ctx, X, XBF, rx, tsl)
        emit_ln(ctx, c, sc, ps[4], ps[5], ps[6], X, XBF, TT, out_bf=XN)

        for j in range(16):
            for phase in (0, 1):
                qsrc = qup if phase == 0 else qgate
                emit_slab_mm(ctx, ps, WQ, qsrc, j * 512, XN, TT)
                if phase == 0:
                    for oc in range(4):
                        ctx.op("act", "activation", out=SU[oc][:], in_=ps[oc][:, 0:TT],
                               func=AF.Silu, scale=scol[:, 0:1])
                else:
                    for oc in range(4):
                        ctx.op("dve", "scalar_tensor_tensor", out=H[j * 4 + oc][:],
                               in0=ps[oc][:, 0:TT], scalar=scol[:, 1:2], in1=SU[oc][:],
                               op0=ALU.mult, op1=ALU.mult)

        for j in range(4):
            emit_slab_mm(ctx, ps, WQ, qdown, j * 512, H, TT)
            for oc in range(4):
                i = j * 4 + oc
                ctx.op("dve", "tensor_scalar_mul", out=FO[i][:], in0=ps[oc][:, 0:TT],
                       scalar1=scol[:, 2:3])
                ctx.op("pool", "tensor_copy", out=FOB[i][:], in_=FO[i][:])

        emit_ln(ctx, c, sc, ps[4], ps[5], ps[6], FO, FOB, TT, resid=X, out_f32=X)
        for i in range(NC):
            ctx.op("sp", "dma_start", out=ox[i * 128:(i + 1) * 128, tsl], in_=X[i][:])
        t0 += TTt

    ctx.prefix = ""
    ctx.pop_scope()
    if standalone:
        ctx.finish()
    return nc





def emit_group_quant(ctx, c, ps_gs, ps_sct, WB, AB, AV, SGV, MK, SCB, RSC, wsrc, qdst, K, O):
    nk, no = K // 128, O // 512
    for ko in range(no):
        for kk in range(nk):
            b = (ko * nk + kk) % 2
            ctx.op("pool", "dma_start", out=WB[b][:],
                   in_=wsrc[kk * 128:(kk + 1) * 128, ko * 512:(ko + 1) * 512])
            ctx.op("act", "activation", out=AB[b][:], in_=WB[b][:], func=AF.Abs)
            ctx.op("pe", "matmul", out=ps_gs[0:2, 0:512], lhsT=c["g2_b"][:], rhs=AB[b][:],
                   start=True, stop=True)
            ctx.op("dve", "tensor_scalar", out=SCB[:], in0=ps_gs[0:2, 0:512],
                   scalar1=1.0 / 64, scalar2=1e-8, op0=ALU.mult, op1=ALU.max)
            ctx.op("pe", "matmul", out=ps_sct[:, 0:512], lhsT=c["sel_half_b"][:], rhs=SCB[:],
                   start=True, stop=True)
            ctx.op("dve", "reciprocal", out=RSC[:], in_=ps_sct[:, 0:512])
            ctx.op("dve", "tensor_mul", out=AV[b][:], in0=WB[b][:], in1=RSC[:])
            ctx.op("act", "activation", out=SGV[b][:], in_=AV[b][:], func=AF.Sign)
            ctx.op("act", "activation", out=AV[b][:], in_=AV[b][:], func=AF.Abs)
            ctx.op("dve", "tensor_single_scalar", out=MK[b][:], in_=AV[b][:],
                   scalar=0.5, op=ALU.is_gt)
            ctx.op("dve", "tensor_mul", out=MK[b][:], in0=MK[b][:], in1=SGV[b][:])
            ctx.op("dve", "tensor_tensor", out=MK[b][:], in0=MK[b][:], in1=ps_sct[:, 0:512],
                   op=ALU.mult)
            ctx.op("sp", "dma_start",
                   out=qdst[kk * 128:(kk + 1) * 128, ko * 512:(ko + 1) * 512], in_=MK[b][:])


def build_l0ffn(TT, tiles, nc=None, ctx=None, c=None, rx=None, ox_=None):
    T = sum(tiles)
    standalone = nc is None
    if standalone:
        nc = bass.Bass("TRN2", target_bir_lowering=False, debug=False)
    if rx is None:
        rx = nc.dram_tensor("rx", [D, T], F32, kind="ExternalInput").ap()
    wguT = nc.dram_tensor("w_guT", [2048, 16384], F32, kind="ExternalInput").ap()
    wpT = nc.dram_tensor("w_pT", [8192, 2048], F32, kind="ExternalInput").ap()
    if ox_ is None:
        ox = nc.dram_tensor("ox", [D, T], F32, kind="ExternalOutput").ap()
    else:
        ox = ox_
    qgu = nc.dram_tensor("qgu", [2048, 16384], BF16, kind="Internal").ap()
    qp = nc.dram_tensor("qp", [8192, 2048], BF16, kind="Internal").ap()
    if ctx is None:
        ctx = Ctx(nc)
        c = make_consts(ctx)
    ctx.push_scope()
    ctx.prefix = "l0f_"
    c = load_ffnq_consts(ctx, c)
    ps = [ctx.psum("ps%d" % i, [128, 512], F32) for i in range(8)]
    WQ = [ctx.sb("wq%d" % i, [128, 512], BF16) for i in range(2)]

    qsc = ctx.scope()
    WB = [ctx.sb_s(qsc, "wb%d" % i, [128, 512], BF16) for i in range(2)]
    AB = [ctx.sb_s(qsc, "ab%d" % i, [128, 512], BF16) for i in range(2)]
    AV = [ctx.sb_s(qsc, "av%d" % i, [128, 512], BF16) for i in range(2)]
    SGV = [ctx.sb_s(qsc, "sgv%d" % i, [128, 512], BF16) for i in range(2)]
    MK = [ctx.sb_s(qsc, "mk%d" % i, [128, 512], BF16) for i in range(2)]
    SCB = ctx.sb_s(qsc, "scb", [2, 512], BF16)
    RSC = ctx.sb_s(qsc, "rsc", [128, 512], F32)
    emit_group_quant(ctx, c, ps[6], ps[7], WB, AB, AV, SGV, MK, SCB, RSC, wguT, qgu, 2048, 16384)
    emit_group_quant(ctx, c, ps[6], ps[7], WB, AB, AV, SGV, MK, SCB, RSC, wpT, qp, 8192, 2048)
    ctx.close_scope(qsc)

    X = [ctx.sb("x%d" % i, [128, TT], F32) for i in range(NC)]
    XBF = [ctx.sb("xb%d" % i, [128, TT], BF16) for i in range(NC)]
    XN = [ctx.sb("xn%d" % i, [128, TT], BF16) for i in range(NC)]
    FO = [ctx.sb("fo%d" % i, [128, TT], F32) for i in range(NC)]
    FOB = [ctx.sb("fob%d" % i, [128, TT], BF16) for i in range(NC)]
    H = [ctx.sb("h%d" % i, [128, TT], BF16) for i in range(64)]
    SU = [ctx.sb("su%d" % i, [128, TT], BF16) for i in range(4)]
    sc = {"sq": [ctx.sb("ln_sq0", [128, TT], BF16), ctx.sb("ln_sq1", [128, TT], BF16)],
          "row": ctx.sb("ln_row", [1, 3 * TT], F32),
          "tmp": [ctx.sb("ln_tmp0", [128, TT], F32), ctx.sb("ln_tmp1", [128, TT], F32)]}
    rrow = ctx.sb("rms_row", [1, TT], F32)

    t0 = 0
    for it, TTt in enumerate(tiles):
        assert TTt == TT
        tsl = slice(t0, t0 + TTt)
        emit_tile_load(ctx, X, XBF, rx, tsl)
        emit_ln(ctx, c, sc, ps[4], ps[5], ps[6], X, XBF, TT, out_bf=XN)

        for j in range(16):
            for phase in (0, 1):
                col = j * 512 + (8192 if phase == 1 else 0)
                emit_slab_mm(ctx, ps, WQ, qgu, col, XN, TT)
                if phase == 0:
                    for oc in range(4):
                        ctx.op("act", "activation", out=SU[oc][:], in_=ps[oc][:, 0:TT],
                               func=AF.Silu)
                else:
                    for oc in range(4):
                        ctx.op("dve", "tensor_mul", out=H[j * 4 + oc][:],
                               in0=ps[oc][:, 0:TT], in1=SU[oc][:])

        for i in range(64):
            b = i % 2
            ctx.op("act", "activation", out=sc["sq"][b][:], in_=H[i][:], func=AF.Square)
            ctx.op("pe", "matmul", out=ps[4][0:1, 0:TT], lhsT=c["ones_col_b"][:],
                   rhs=sc["sq"][b][:], start=(i == 0), stop=(i == 63))
        ctx.op("dve", "tensor_scalar_mul", out=rrow[0:1, 0:TT], in0=ps[4][0:1, 0:TT],
               scalar1=1.0 / 8192)
        ctx.op("act", "activation", out=rrow[0:1, 0:TT], in_=rrow[0:1, 0:TT], func=AF.Sqrt,
               bias=c["cv"][0:1, 1:2])
        ctx.op("dve", "reciprocal", out=rrow[0:1, 0:TT], in_=rrow[0:1, 0:TT])
        ctx.op("pe", "matmul", out=ps[5][:, 0:TT], lhsT=c["ones_row_f"][:], rhs=rrow[0:1, 0:TT],
               start=True, stop=True)
        for i in range(64):
            ctx.op("dve", "tensor_mul", out=H[i][:], in0=H[i][:], in1=ps[5][:, 0:TT])

        for j in range(4):
            emit_slab_mm(ctx, ps, WQ, qp, j * 512, H, TT)
            for oc in range(4):
                i = j * 4 + oc
                ctx.op("dve", "tensor_copy", out=FO[i][:], in_=ps[oc][:, 0:TT])
                ctx.op("pool", "tensor_copy", out=FOB[i][:], in_=FO[i][:])

        emit_ln(ctx, c, sc, ps[4], ps[5], ps[6], FO, FOB, TT, resid=X, out_f32=X)
        for i in range(NC):
            ctx.op("sp", "dma_start", out=ox[i * 128:(i + 1) * 128, tsl], in_=X[i][:])
        t0 += TTt

    ctx.prefix = ""
    ctx.pop_scope()
    if standalone:
        ctx.finish()
    return nc





def emit_cumsum128(ctx, c, ps_a, ps_b, src, dst, P, ncols, tmp):
    for cc in range(ncols // 128):
        cs = slice(cc * 128, (cc + 1) * 128)
        ctx.op("pe", "transpose", out=ps_a[:, 0:P], in_=src[:, cs],
               identity=c["ident_f"][0:P, 0:P])
        ctx.op("dve", "tensor_copy", out=tmp[:, 0:P], in_=ps_a[:, 0:P])
        ctx.op("pe", "matmul", out=ps_b[:, 0:P], lhsT=c["mask_incl_f"][:], rhs=tmp[:, 0:P],
               start=True, stop=True)
        ctx.op("dve", "tensor_copy", out=tmp[:, 0:P], in_=ps_b[:, 0:P])
        ctx.op("pe", "transpose", out=ps_a[0:P, 128:256], in_=tmp[:, 0:P],
               identity=c["ident_f"][:])
        ctx.op("dve", "tensor_copy", out=dst[:, cs], in_=ps_a[0:P, 128:256])



def build_l0attn(TT, tiles, first_tile_state_zero=True, upto='full',
                 nc=None, ctx=None, c=None, rx=None, ox_=None):
    import os
    skip = set(os.environ.get("L0SKIP", "").split(","))
    T = sum(tiles)
    NCH = TT // 128
    standalone = nc is None
    if standalone:
        nc = bass.Bass("TRN2", target_bir_lowering=False, debug=False)
    if rx is None:
        rx = nc.dram_tensor("rx", [D, T], F32, kind="ExternalInput").ap()
    winT = nc.dram_tensor("W_inT", [2048, 8512], F32, kind="ExternalInput").ap()
    woutT = nc.dram_tensor("W_outT", [4096, 2048], F32, kind="ExternalInput").ap()
    cw = nc.dram_tensor("conv_w", [128, 136], F32, kind="ExternalInput").ap()
    cb = nc.dram_tensor("conv_b", [128, 34], F32, kind="ExternalInput").ap()
    dtb = nc.dram_tensor("dt_bias", [64, 1], F32, kind="ExternalInput").ap()
    alog = nc.dram_tensor("A_log", [64, 1], F32, kind="ExternalInput").ap()
    dvec = nc.dram_tensor("D_vec", [128, 32], F32, kind="ExternalInput").ap()
    nrmw = nc.dram_tensor("norm_w", [128, 32], F32, kind="ExternalInput").ap()
    if ox_ is None:
        ox = nc.dram_tensor("ox", [D, T], F32, kind="ExternalOutput").ap()
    else:
        ox = ox_
    if ctx is None:
        ctx = Ctx(nc)
        c = make_consts(ctx, lite=True)
    ctx.push_scope()
    ctx.prefix = "l0a_"
    ctx.paranoid_flag = True
    ps = [ctx.psum("ps%d" % i, [128, 512], F32) for i in range(6)]
    pb = [ctx.psum("pb%d" % i, [128, 512], F32) for i in range(2)]

    XB2 = [ctx.sb("xb2_%d" % i, [128, TT], BF16) for i in range(2)]
    XN = [ctx.sb("xn%d" % i, [128, TT], BF16) for i in range(NC)]
    XBCC = [ctx.sb("xbcc%d" % i, [128, TT], BF16) for i in range(34)]
    YG = [ctx.sb("yg%d" % i, [128, TT], BF16) for i in range(32)]
    FO = [ctx.sb("fo%d" % i, [128, TT], F32) for i in range(NC)]
    FOB = [ctx.sb("fob%d" % i, [128, TT], BF16) for i in range(NC)]
    XT = ctx.sb("xt", [128, 4096], BF16)
    YT = ctx.sb("yt", [128, 4096], BF16)
    S = ctx.sb("state", [128, 4096], F32)
    WQ = [ctx.sb("wq%d" % i, [128, 512], BF16) for i in range(2)]
    sc = {"sq": [ctx.sb("ln_sq0", [128, TT], BF16), ctx.sb("ln_sq1", [128, TT], BF16)],
          "row": ctx.sb("ln_row", [1, 3 * TT], F32),
          "tmp": [ctx.sb("ln_tmp0", [128, TT], F32), ctx.sb("ln_tmp1", [128, TT], F32)]}
    XRW = [ctx.sb("xrw%d" % i, [128, TT + 3], BF16) for i in range(2)]
    ACC = sc["tmp"]
    TAILS = ctx.sb("tails", [128, 34 * 3], BF16)
    XR2 = sc["tmp"]
    SZ = [ctx.sb("sz%d" % i, [128, TT], BF16) for i in range(2)]
    DT1 = ctx.sb("dt1", [64, TT], F32)
    DTS = ctx.sb("dts", [64, TT], F32)
    LDA = DT1
    BCum = ctx.sb("bcum", [64, TT], F32)
    T64 = [ctx.sb("t64_0", [64, TT], F32)]
    NBT = ctx.sb("nbt", [128, 64], F32)
    EBT = ctx.sb("ebt", [128, 64], F32)
    DTT = ctx.sb("dtt", [128, 64], F32)
    DTWT = ctx.sb("dtwt", [128, 64], F32)
    EBLB = ctx.sb("eblb", [128, 64], F32)
    EBLR = ctx.sb("eblr", [1, 64], F32)
    BROW = ctx.sb("brow", [1, 128], F32)
    CBM = ctx.sb("cbm", [128, 128], F32)
    BT = ctx.sb("btr", [128, 128], BF16)
    SCF = [ctx.sb("scf%d" % i, [128, 128], F32) for i in range(2)]
    SCB = [ctx.sb("scbuf%d" % i, [128, 128], BF16) for i in range(2)]
    BD = [ctx.sb("bd%d" % i, [128, 128], BF16) for i in range(2)]
    TY = [ctx.sb("ty%d" % i, [128, 64], F32) for i in range(2)]
    S0B = [ctx.sb("s0b%d" % i, [128, 64], BF16) for i in range(2)]
    CWS = ctx.sb("cws", [128, 34 * 4], F32)
    CBS = ctx.sb("cbs", [128, 34], F32)
    DTB = ctx.sb("dtbs", [64, 1], F32)
    NEA = ctx.sb("nea", [64, 1], F32)
    DVC = ctx.sb("dvc", [128, 32], F32)
    NRM = ctx.sb("nrm", [128, 32], F32)
    rrow = ctx.sb("rms_row", [1, TT], F32)

    ctx.op("sp", "dma_start", out=CWS[:], in_=cw)
    ctx.op("sp", "dma_start", out=CBS[:], in_=cb)
    ctx.op("sp", "dma_start", out=DTB[:], in_=dtb)
    ctx.op("sp", "dma_start", out=DVC[:], in_=dvec)
    ctx.op("sp", "dma_start", out=NRM[:], in_=nrmw)
    AL = ctx.sb("als", [64, 1], F32)
    ctx.op("sp", "dma_start", out=AL[:], in_=alog)
    ctx.op("act", "activation", out=NEA[:], in_=AL[:], func=AF.Exp)
    ctx.op("dve", "tensor_scalar_mul", out=NEA[:], in0=NEA[:], scalar1=-1.0)
    if first_tile_state_zero:
        ctx.op("pool", "memset", S[:], 0.0)
        ctx.op("dve", "memset", TAILS[:], 0.0)

    t0 = 0
    for it, TTt in enumerate(tiles):
        assert TTt == TT
        tsl = slice(t0, t0 + TTt)
        emit_ln_stream(ctx, c, sc, ps[4], ps[5], ps[3], rx, tsl, XB2, XN, TT)

        slabs = [] if upto == "u1" else [(4096 + 512 * j, 4) for j in range(8)] + [(8192, 2)]
        for (col0, noc) in slabs:
            emit_slab_mm(ctx, ps, WQ, winT, col0, XN, TT, n_oc=noc, queue="pool")
            for oc in range(noc):
                ch = (col0 - 4096) // 128 + oc
                b = ch % 2
                ctx.op("dve", "tensor_copy", out=XRW[b][:, 0:3], in_=TAILS[:, ch * 3:ch * 3 + 3])
                ctx.op("dve", "tensor_copy", out=XRW[b][:, 3:3 + TT], in_=ps[oc][:, 0:TT])
                ctx.op("dve", "tensor_copy", out=TAILS[:, ch * 3:ch * 3 + 3], in_=XRW[b][:, TT:TT + 3])
                ctx.op("dve", "tensor_scalar_mul", out=ACC[b][:], in0=XRW[b][:, 0:TT],
                       scalar1=CWS[:, ch * 4:ch * 4 + 1])
                for j in (1, 2, 3):
                    ctx.op("dve", "scalar_tensor_tensor", out=ACC[b][:], in0=XRW[b][:, j:j + TT],
                           scalar=CWS[:, ch * 4 + j:ch * 4 + j + 1], in1=ACC[b][:],
                           op0=ALU.mult, op1=ALU.add)
                ctx.op("act", "activation", out=XBCC[ch][:], in_=ACC[b][:], func=AF.Silu,
                       bias=CBS[:, ch:ch + 1])
        if upto == "u1":
            for i in range(NC):
                ctx.op("pool", "dma_start", out=ox[i * 128:(i + 1) * 128, tsl], in_=XN[i][:])
            t0 += TTt
            continue
        emit_slab_mm(ctx, ps, WQ, winT, 8448, XN, TT, n_oc=1, ncols=64, queue="pool")
        ctx.op("dve", "tensor_scalar_add", out=DT1[:], in0=ps[0][0:64, 0:TT], scalar1=DTB[:, 0:1])
        ctx.op("act", "activation", out=DTS[:], in_=DT1[:], func=AF.Relu)
        ctx.op("act", "activation", out=T64[0][:], in_=DT1[:], func=AF.Abs)
        ctx.op("act", "activation", out=T64[0][:], in_=T64[0][:], func=AF.Exp, scale=-1.0)
        ctx.op("act", "activation", out=T64[0][:], in_=T64[0][:], func=AF.Ln, bias=c["cv"][0:64, 6:7])
        ctx.op("dve", "tensor_add", out=DTS[:], in0=T64[0][:], in1=DTS[:])
        ctx.op("dve", "tensor_scalar_mul", out=LDA[:], in0=DTS[:], scalar1=NEA[:, 0:1])
        emit_cumsum128(ctx, c, ps[4], ps[5], LDA, BCum, 64, TT, CBM)

        if upto in ("u1", "u2"):
            for i in range(NC):
                ctx.op("pool", "dma_start", out=ox[i * 128:(i + 1) * 128, tsl], in_=XN[i][:])
            t0 += TTt
            continue
        for cc in range(NCH):
            cs = slice(cc * 128, (cc + 1) * 128)
            bL = BCum[:, cc * 128 + 127:cc * 128 + 128]
            if "t1" not in skip:
                ctx.op("pe", "transpose", out=ps[4][:, 0:64], in_=BCum[:, cs],
                       identity=c["ident_f"][0:64, 0:64])
                ctx.op("dve", "tensor_scalar_mul", out=NBT[:], in0=ps[4][:, 0:64], scalar1=-1.0)
                ctx.op("act", "activation", out=EBT[:], in_=ps[4][:, 0:64], func=AF.Exp)
                ctx.op("pe", "transpose", out=ps[4][:, 64:128], in_=DTS[:, cs],
                       identity=c["ident_f"][0:64, 0:64])
                ctx.op("dve", "tensor_copy", out=DTT[:], in_=ps[4][:, 64:128])
            if "t2" not in skip:
                ctx.op("dve", "tensor_scalar", out=T64[0][:, 0:128], in0=BCum[:, cs], scalar1=bL,
                       scalar2=-1.0, op0=ALU.subtract, op1=ALU.mult)
                ctx.op("act", "activation", out=T64[0][:, 0:128], in_=T64[0][:, 0:128], func=AF.Exp)
                ctx.op("dve", "tensor_mul", out=T64[0][:, 0:128], in0=T64[0][:, 0:128], in1=DTS[:, cs])
                ctx.op("pe", "transpose", out=ps[4][:, 128:192], in_=T64[0][:, 0:128],
                       identity=c["ident_f"][0:64, 0:64])
                ctx.op("dve", "tensor_copy", out=DTWT[:], in_=ps[4][:, 128:192])
            if "t3" not in skip:
                ctx.op("act", "activation", out=T64[0][:, 128:129], in_=bL, func=AF.Exp)
                ctx.op("pe", "matmul", out=ps[4][0:1, 192:256], lhsT=T64[0][:, 128:129],
                       rhs=c["ident_f"][0:64, 0:64], start=True, stop=True)
                ctx.op("dve", "tensor_copy", out=EBLR[:], in_=ps[4][0:1, 192:256])
                ctx.op("pe", "matmul", out=ps[4][:, 256:320], lhsT=c["ones_row_f"][:], rhs=EBLR[:],
                       start=True, stop=True)
                ctx.op("dve", "tensor_copy", out=EBLB[:], in_=ps[4][:, 256:320])
            if "t4" not in skip:
                ctx.op("pe", "matmul", out=ps[5][:, 0:128], lhsT=XBCC[32][:, cs], rhs=XBCC[33][:, cs],
                       start=True, stop=True)
                ctx.op("dve", "tensor_mul", out=CBM[:], in0=ps[5][:, 0:128], in1=c["mask_incl_f"][:])
            if "t5" not in skip:
                ctx.op("pe", "transpose", out=pb[0][:, 0:128], in_=XBCC[32][:, cs], identity=c["ident_b"][:])
                ctx.op("dve", "tensor_copy", out=BT[:], in_=pb[0][:, 0:128])
            if "t6" not in skip:
                for d in range(32):
                    bb = d % 2
                    ctx.op("pe", "transpose", out=pb[bb][:, 128:256], in_=XBCC[d][:, cs],
                           identity=c["ident_b"][:])
                    ctx.op("dve", "tensor_copy", out=XT[:, d * 128:(d + 1) * 128], in_=pb[bb][:, 128:256])
            if upto == "c0":
                continue
            for h in range(64):
                b = h % 2
                hs = slice(h * 64, (h + 1) * 64)
                psc = [ps[0], ps[2]][b]
                py = [ps[1], ps[3]][b]
                pct = [ps[4], ps[5]][b]
                ctx.op("pe", "matmul", out=psc[0:1, 128:256], lhsT=c["ident_f"][0:64, h:h + 1],
                       rhs=BCum[:, cs], start=True, stop=True)
                ctx.op("dve", "tensor_copy", out=BROW[:], in_=psc[0:1, 128:256])
                ctx.op("pe", "matmul", out=psc[:, 0:128], lhsT=c["ones_row_f"][:], rhs=BROW[:],
                       start=True, stop=True)
                ctx.op("dve", "tensor_scalar", out=SCF[b][:], in0=psc[:, 0:128],
                       scalar1=NBT[:, h:h + 1], scalar2=0.0, op0=ALU.add, op1=ALU.min)
                ctx.op("act", "activation", out=SCF[b][:], in_=SCF[b][:], func=AF.Exp)
                ctx.op("dve", "tensor_mul", out=SCF[b][:], in0=SCF[b][:], in1=CBM[:])
                ctx.op("dve", "tensor_scalar_mul", out=SCB[b][:], in0=SCF[b][:], scalar1=DTT[:, h:h + 1])
                ctx.op("dve", "tensor_copy", out=S0B[b][:], in_=S[:, hs])
                ctx.op("pe", "matmul", out=py[:, 0:64], lhsT=SCB[b][:], rhs=XT[:, h * 64:h * 64 + 64],
                       start=True, stop=True)
                ctx.op("pe", "matmul", out=py[:, 64:128], lhsT=XBCC[33][:, cs], rhs=S0B[b][:],
                       start=True, stop=True)
                ctx.op("dve", "tensor_scalar_mul", out=TY[b][:], in0=py[:, 64:128], scalar1=EBT[:, h:h + 1])
                ctx.op("dve", "tensor_add", out=YT[:, hs], in0=py[:, 0:64], in1=TY[b][:])
                ctx.op("dve", "tensor_scalar_mul", out=BD[b][:], in0=BT[:], scalar1=DTWT[:, h:h + 1])
                ctx.op("pe", "matmul", out=pct[:, 320:384], lhsT=BD[b][:], rhs=XT[:, h * 64:h * 64 + 64],
                       start=True, stop=True)
                ctx.op("dve", "scalar_tensor_tensor", out=S[:, hs], in0=S[:, hs],
                       scalar=EBLB[:, h:h + 1], in1=pct[:, 320:384], op0=ALU.mult, op1=ALU.add)
            if upto == "c1":
                continue
            for d in range(32):
                bb = d % 2
                ctx.op("pe", "transpose", out=pb[bb][:, 256:384], in_=YT[:, d * 128:(d + 1) * 128],
                       identity=c["ident_b"][:])
                ctx.op("dve", "scalar_tensor_tensor", out=YG[d][:, cs], in0=XBCC[d][:, cs],
                       scalar=DVC[:, d:d + 1], in1=pb[bb][:, 256:384], op0=ALU.mult, op1=ALU.add)

        if upto in ("u3", "c0", "c1"):
            for i in range(NC):
                ctx.op("pool", "dma_start", out=ox[i * 128:(i + 1) * 128, tsl], in_=XN[i][:])
            t0 += TTt
            continue
        for j in range(8):
            emit_slab_mm(ctx, ps, WQ, winT, j * 512, XN, TT, queue="pool")
            for oc in range(4):
                zi = j * 4 + oc
                ctx.op("act", "activation", out=SZ[oc % 2][:], in_=ps[oc][:, 0:TT], func=AF.Silu)
                ctx.op("dve", "tensor_mul", out=YG[zi][:], in0=YG[zi][:], in1=SZ[oc % 2][:])

        for i in range(32):
            b = i % 2
            ctx.op("act", "activation", out=sc["sq"][b][:], in_=YG[i][:], func=AF.Square)
            ctx.op("pe", "matmul", out=ps[4][0:1, 0:TT], lhsT=c["ones_col_b"][:],
                   rhs=sc["sq"][b][:], start=(i == 0), stop=(i == 31))
        ctx.op("dve", "tensor_scalar_mul", out=rrow[0:1, 0:TT], in0=ps[4][0:1, 0:TT],
               scalar1=1.0 / 4096)
        ctx.op("act", "activation", out=rrow[0:1, 0:TT], in_=rrow[0:1, 0:TT], func=AF.Sqrt,
               bias=c["cv"][0:1, 0:1])
        ctx.op("dve", "reciprocal", out=rrow[0:1, 0:TT], in_=rrow[0:1, 0:TT])
        ctx.op("pe", "matmul", out=ps[5][:, 0:TT], lhsT=c["ones_row_f"][:], rhs=rrow[0:1, 0:TT],
               start=True, stop=True)
        for i in range(32):
            ctx.op("dve", "scalar_tensor_tensor", out=YG[i][:], in0=YG[i][:],
                   scalar=NRM[:, i:i + 1], in1=ps[5][:, 0:TT], op0=ALU.mult, op1=ALU.mult)

        if upto == "u4":
            for i in range(NC):
                ctx.op("pool", "dma_start", out=ox[i * 128:(i + 1) * 128, tsl], in_=XN[i][:])
            t0 += TTt
            continue
        for j in range(4):
            emit_slab_mm(ctx, ps, WQ, woutT, j * 512, YG, TT, queue="pool")
            for oc in range(4):
                i = j * 4 + oc
                ctx.op("dve", "tensor_copy", out=FO[i][:], in_=ps[oc][:, 0:TT])
                ctx.op("pool", "tensor_copy", out=FOB[i][:], in_=FO[i][:])

        emit_ln_stream_add(ctx, c, sc, ps[4], ps[5], ps[3], FO, FOB, XR2, rx, ox, tsl, TT)
        t0 += TTt

    ctx.prefix = ""
    ctx.paranoid_flag = False
    ctx.pop_scope()
    if standalone:
        ctx.finish()
    return nc


def emit_ln_stream_add(ctx, c, sc, ps_stat, ps_b0, ps_b1, FO, FOB, XR2, rx, ox, tsl, TT):
    sq = sc["sq"]
    row = sc["row"]
    m_row = row[0:1, 0:TT]
    r_row = row[0:1, TT:2 * TT]
    t_row = row[0:1, 2 * TT:3 * TT]
    for i in range(NC):
        b = i % 2
        ctx.op("act", "activation", out=sq[b][:], in_=FO[i][:], func=AF.Square)
        ctx.op("pe", "matmul", out=ps_stat[0:1, 0:TT], lhsT=c["ones_col_b"][:], rhs=FOB[i][:],
               start=(i == 0), stop=(i == NC - 1))
        ctx.op("pe", "matmul", out=ps_b0[0:1, 0:TT], lhsT=c["ones_col_b"][:], rhs=sq[b][:],
               start=(i == 0), stop=(i == NC - 1))
    ctx.op("dve", "tensor_scalar_mul", out=m_row, in0=ps_stat[0:1, 0:TT], scalar1=1.0 / D)
    ctx.op("act", "activation", out=t_row, in_=m_row, func=AF.Square)
    ctx.op("dve", "scalar_tensor_tensor", out=r_row, in0=ps_b0[0:1, 0:TT], scalar=1.0 / D,
           in1=t_row, op0=ALU.mult, op1=ALU.subtract)
    ctx.op("act", "activation", out=r_row, in_=r_row, func=AF.Sqrt, bias=c["cv"][0:1, 0:1])
    ctx.op("dve", "reciprocal", out=r_row, in_=r_row)
    ctx.op("pe", "matmul", out=ps_b0[:, 0:TT], lhsT=c["ones_row_f"][:], rhs=m_row, start=True, stop=True)
    ctx.op("pe", "matmul", out=ps_b1[:, 0:TT], lhsT=c["ones_row_f"][:], rhs=r_row, start=True, stop=True)
    for i in range(NC):
        b = i % 2
        ctx.op("sp", "dma_start", out=XR2[b][:], in_=rx[i * 128:(i + 1) * 128, tsl])
        ctx.op("dve", "tensor_sub", out=FO[i][:], in0=FO[i][:], in1=ps_b0[:, 0:TT])
        ctx.op("dve", "tensor_mul", out=FO[i][:], in0=FO[i][:], in1=ps_b1[:, 0:TT])
        ctx.op("dve", "tensor_add", out=FO[i][:], in0=FO[i][:], in1=XR2[b][:])
        ctx.op("sp", "dma_start", out=ox[i * 128:(i + 1) * 128, tsl], in_=FO[i][:])





def build_l1attn(TT, tiles, dbg=False, nc=None, ctx=None, c=None, rx=None, ox_=None):
    T = sum(tiles)
    NCH = TT // 128
    standalone = nc is None
    if standalone:
        nc = bass.Bass("TRN2", target_bir_lowering=False, debug=False)
    if rx is None:
        rx = nc.dram_tensor("rx", [D, T], F32, kind="ExternalInput").ap()
    wqT = nc.dram_tensor("WqT", [2048, 2048], F32, kind="ExternalInput").ap()
    wkT = nc.dram_tensor("WkT", [2048, 2048], F32, kind="ExternalInput").ap()
    wvT = nc.dram_tensor("WvT", [2048, 2048], F32, kind="ExternalInput").ap()
    convq = nc.dram_tensor("conv_q", [128, 64], F32, kind="ExternalInput").ap()
    convk = nc.dram_tensor("conv_k", [128, 64], F32, kind="ExternalInput").ap()
    convv = nc.dram_tensor("conv_v", [128, 64], F32, kind="ExternalInput").ap()
    aexp = nc.dram_tensor("A_exp", [128, 16], F32, kind="ExternalInput").ap()
    dtbias = nc.dram_tensor("dt_bias1", [128, 16], F32, kind="ExternalInput").ap()
    wsvT = nc.dram_tensor("WsvT", [128, 128], F32, kind="ExternalInput").ap()
    mlpw = nc.dram_tensor("mlp_w", [1, 4 * 96 + 80], F32, kind="ExternalInput").ap()
    mb2w = nc.dram_tensor("mb2_w", [32, 2], F32, kind="ExternalInput").ap()
    bb1w = nc.dram_tensor("bb1_w", [32, 2], F32, kind="ExternalInput").ap()
    wg1T = nc.dram_tensor("Wg1T", [2048, 128], F32, kind="ExternalInput").ap()
    wg2T = nc.dram_tensor("Wg2T", [128, 2048], F32, kind="ExternalInput").ap()
    bg2 = nc.dram_tensor("bg2", [128, 16], F32, kind="ExternalInput").ap()
    nwrow = nc.dram_tensor("norm_w_row", [1, 128], F32, kind="ExternalInput").ap()
    woT = nc.dram_tensor("WoT", [2048, 2048], F32, kind="ExternalInput").ap()
    if ox_ is None:
        ox = nc.dram_tensor("ox", [D, T], F32, kind="ExternalOutput").ap()
    else:
        ox = ox_
    if dbg:
        dK = nc.dram_tensor("dK", [2048, T], F32, kind="ExternalOutput").ap()
        dQ = nc.dram_tensor("dQ", [2048, T], F32, kind="ExternalOutput").ap()
        dV = nc.dram_tensor("dV", [2048, T], F32, kind="ExternalOutput").ap()
        dG = nc.dram_tensor("dG", [2048, T], F32, kind="ExternalOutput").ap()
        dBC = nc.dram_tensor("dBC", [128, 64], F32, kind="ExternalOutput").ap()
        dBQ = nc.dram_tensor("dBQ", [128, T], F32, kind="ExternalOutput").ap()
        dTS = nc.dram_tensor("dTS", [128, 128], F32, kind="ExternalOutput").ap()
        dPS = nc.dram_tensor("dPS", [128, 128], F32, kind="ExternalOutput").ap()
        dXD = nc.dram_tensor("dXD", [128, 128], F32, kind="ExternalOutput").ap()
        dSS = nc.dram_tensor("dSS", [128, 512], F32, kind="ExternalOutput").ap()
        dXD4 = nc.dram_tensor("dXD4", [128, 512], F32, kind="ExternalOutput").ap()
        dRH = nc.dram_tensor("dRH", [128, 512], F32, kind="ExternalOutput").ap()
        dO4 = nc.dram_tensor("dO4", [128, 512], F32, kind="ExternalOutput").ap()
        dTS4 = nc.dram_tensor("dTS4", [128, 512], F32, kind="ExternalOutput").ap()
        dKN = nc.dram_tensor("dKN", [128, 512], F32, kind="ExternalOutput").ap()
        dQN = nc.dram_tensor("dQN", [128, 512], F32, kind="ExternalOutput").ap()
        dOD = nc.dram_tensor("dOD", [2048, T], F32, kind="ExternalOutput").ap()

    if ctx is None:
        ctx = Ctx(nc)
        c = make_consts(ctx, lite=True, blk=True)
    ctx.push_scope()
    ctx.prefix = "l1a_"
    ctx.paranoid_flag = True
    c = load_blk_consts(ctx, c)
    psf = [ctx.psum("psf%d" % i, [128, 512], F32) for i in range(6)]
    psb = [ctx.psum("psb%d" % i, [128, 512], F32) for i in range(2)]

    XB2 = [ctx.sb("xb2_%d" % i, [128, TT], BF16) for i in range(2)]
    XN = [ctx.sb("xn%d" % i, [128, TT], BF16) for i in range(NC)]
    QC = [ctx.sb("qc%d" % i, [128, TT], BF16) for i in range(16)]
    KC = [ctx.sb("kc%d" % i, [128, TT], BF16) for i in range(16)]
    VC = [ctx.sb("vc%d" % i, [128, TT], BF16) for i in range(16)]
    G1GATE = ctx.sb("g1gate", [128, TT], BF16)
    SGT = ctx.sb("sgt", [128, TT], BF16)
    FO = [ctx.sb("fo%d" % i, [128, TT], F32) for i in range(NC)]
    S = ctx.sb("state", [128, 2048], F32)
    OD = XN
    WQ = [ctx.sb("wq%d" % i, [128, 512], BF16) for i in range(2)]
    sc = {"sq": [ctx.sb("ln_sq0", [128, TT], BF16), ctx.sb("ln_sq1", [128, TT], BF16)],
          "row": ctx.sb("ln_row", [1, 3 * TT], F32),
          "tmp": [ctx.sb("ln_tmp0", [128, TT], F32), ctx.sb("ln_tmp1", [128, TT], F32)]}
    XRW = [ctx.sb("xrw%d" % i, [128, TT + 3], BF16) for i in range(2)]
    TLS = [ctx.sb("tls%d" % i, [128, 3 * 16], BF16) for i in range(3)]
    EHF = None
    G1 = ctx.sb("g1f", [128, TT], F32)
    MBALL = G1
    BQ = ctx.sb("bq", [128, TT], F32)
    SEXP = ctx.sb("sexp", [128, TT], F32)
    DPAIR = [ctx.sb("dpair%d" % i, [128, 2048], F32) for i in range(1)]
    TPB = ctx.sb("tpb", [128, 2048], F32)
    PPB = ctx.sb("ppb", [128, 2048], F32)
    KS1 = ctx.sb("ks1", [128, 2048], F32)
    UQ = ctx.sb("uqbuf", [128, 2 * TT], F32)
    UC = ctx.sb("ucbuf", [128, TT], F32)
    QCC = ctx.sb("qccbuf", [128, TT], F32)
    WP = ctx.sb("wpbuf", [128, TT], F32)
    WBL = [ctx.sb("wbl%d" % i, [128, 4 * 112], F32) for i in range(1)]
    TROW = ctx.sb("trow", [1, 2048], F32)
    TD = ctx.sb("tdbuf", [128, 16], F32)
    PD = ctx.sb("pdbuf", [128, 16], F32)
    TS = ctx.sb("tsbuf", [128, 128], F32)
    PS_ = ctx.sb("psbuf", [128, 128], F32)
    AB_ = ctx.sb("abuf", [128, 128], F32)
    ABT = ctx.sb("abtbuf", [128, 128], F32)
    APW = [ctx.sb("apw%d" % i, [128, 128], F32) for i in range(2)]
    APWT = [ctx.sb("apwt%d" % i, [128, 128], F32) for i in range(2)]
    XDF = ctx.sb("xdf", [128, 128], F32)
    VTS = ctx.sb("vts", [128, 128], F32)
    R1 = ctx.sb("r1f", [128, 128], F32)
    ONF = R1
    ONB = ctx.sb("onb", [128, 128], BF16)
    OCOL = ctx.sb("ocol", [128, 4], F32)
    BCOL = ctx.sb("bcol", [128, 64], F32)
    ROWS = ctx.sb("rows_s", [1, 5 * TT], F32)
    H1B = ctx.sb("h1b", [32, TT], BF16)
    NW = ctx.sb("nwb", [128, 128], F32)
    DTBC = ctx.sb("dtbc", [128, 16], F32)
    NEAC = ctx.sb("neac", [128, 16], F32)
    BG2C = ctx.sb("bg2c", [128, 16], F32)
    CWS = [ctx.sb("cws%d" % i, [128, 64], F32) for i in range(3)]
    MLPR = ctx.sb("mlpr", [1, 4 * 96 + 80], F32)
    WSV = ctx.sb("wsv", [128, 128], BF16)
    FSQ = None
    EHF = sc["tmp"][1]

    ctx.op("sp", "dma_start", out=MLPR[:], in_=mlpw)
    ctx.op("pool", "dma_start", out=WSV[:], in_=wsvT)
    for i, cv_ in enumerate((convq, convk, convv)):
        ctx.op("sp", "dma_start", out=CWS[i][:], in_=cv_)
    ctx.op("sp", "dma_start", out=DTBC[:], in_=dtbias)
    ctx.op("sp", "dma_start", out=NEAC[:], in_=aexp)
    ctx.op("sp", "dma_start", out=BG2C[:], in_=bg2)
    ctx.op("act", "activation", out=NEAC[:], in_=NEAC[:], func=AF.Exp)
    ctx.op("dve", "tensor_scalar_mul", out=NEAC[:], in0=NEAC[:], scalar1=-1.0)
    ctx.op("sp", "dma_start", out=ROWS[0:1, 0:128], in_=nwrow)
    ctx.op("pe", "matmul", out=psf[0][:, 0:128], lhsT=c["ones_row_f"][:], rhs=ROWS[0:1, 0:128],
           start=True, stop=True)
    ctx.op("dve", "tensor_copy", out=NW[:], in_=psf[0][:, 0:128])
    ctx.op("pool", "memset", S[:], 0.0)
    for i in range(3):
        ctx.op("dve", "memset", TLS[i][:], 0.0)
    W1B = MLPR[0:1, 0:96]
    W1A = MLPR[0:1, 96:192]
    MB2 = ctx.sb("mb2", [32, 2], BF16)
    ctx.op("pool", "dma_start", out=MB2[:], in_=mb2w)
    BB1 = ctx.sb("bb1c", [32, 2], F32)
    ctx.op("sp", "dma_start", out=BB1[:], in_=bb1w)

    t0 = 0
    for it, TTt in enumerate(tiles):
        assert TTt == TT
        tsl = slice(t0, t0 + TTt)
        emit_ln_stream(ctx, c, sc, psf[4], psf[5], psf[3], rx, tsl, XB2, XN, TT)

        for mi, (wT, dst) in enumerate(((wqT, QC), (wkT, KC), (wvT, VC))):
            for j in range(4):
                emit_slab_mm(ctx, psf, WQ, wT, j * 512, XN, TT, queue="pool")
                for oc in range(4):
                    ch = j * 4 + oc
                    b = ch % 2
                    ctx.op("dve", "tensor_copy", out=XRW[b][:, 0:3], in_=TLS[mi][:, ch * 3:ch * 3 + 3])
                    ctx.op("dve", "tensor_copy", out=XRW[b][:, 3:3 + TT], in_=psf[oc][:, 0:TT])
                    ctx.op("dve", "tensor_copy", out=TLS[mi][:, ch * 3:ch * 3 + 3], in_=XRW[b][:, TT:TT + 3])
                    ctx.op("dve", "tensor_scalar_mul", out=sc["tmp"][b][:], in0=XRW[b][:, 0:TT],
                           scalar1=CWS[mi][:, ch * 4:ch * 4 + 1])
                    for jj in (1, 2, 3):
                        ctx.op("dve", "scalar_tensor_tensor", out=sc["tmp"][b][:], in0=XRW[b][:, jj:jj + TT],
                               scalar=CWS[mi][:, ch * 4 + jj:ch * 4 + jj + 1], in1=sc["tmp"][b][:],
                               op0=ALU.mult, op1=ALU.add)
                    ctx.op("act", "activation", out=dst[ch][:], in_=sc["tmp"][b][:], func=AF.Silu)

        for kk in range(16):
            b = kk % 2
            ctx.op("pool", "dma_start", out=WQ[b][:, 0:128], in_=wg1T[kk * 128:(kk + 1) * 128, :])
            ctx.op("pe", "matmul", out=psf[0][:, 0:TT], lhsT=WQ[b][:, 0:128], rhs=XN[kk][:],
                   start=(kk == 0), stop=(kk == 15))
        ctx.op("dve", "tensor_copy", out=G1GATE[:], in_=psf[0][:, 0:TT])

        if dbg and it == 0:
            for ch in range(16):
                ctx.op("pool", "dma_start", out=dK[ch * 128:(ch + 1) * 128, tsl], in_=KC[ch][:])
                ctx.op("pool", "dma_start", out=dQ[ch * 128:(ch + 1) * 128, tsl], in_=QC[ch][:])
                ctx.op("pool", "dma_start", out=dV[ch * 128:(ch + 1) * 128, tsl], in_=VC[ch][:])
        for h in range(16):
            ehr = slice(0, TT)
            r_l2 = ROWS[0:1, 0:TT]
            r_l1 = ROWS[0:1, TT:2 * TT]
            r_cos = ROWS[0:1, 2 * TT:3 * TT]
            r_t1 = ROWS[0:1, 3 * TT:4 * TT]
            r_t2 = ROWS[0:1, 4 * TT:5 * TT]
            r_beta = r_t1
            r_amp = r_t2
            r_t3 = r_l2
            ctx.op("pe", "matmul", out=psf[5][:, 0:TT], lhsT=WSV[:], rhs=KC[h][:], start=True, stop=True)
            ctx.op("dve", "tensor_sub", out=EHF[:], in0=psf[5][:, 0:TT], in1=VC[h][:])
            ctx.op("act", "activation", out=SEXP[:], in_=EHF[:], func=AF.Square)
            ctx.op("pe", "matmul", out=psf[0][0:1, 0:TT], lhsT=c["ones_col_f"][:], rhs=SEXP[:], start=True, stop=True)
            ctx.op("act", "activation", out=SEXP[:], in_=EHF[:], func=AF.Abs)
            ctx.op("pe", "matmul", out=psf[1][0:1, 0:TT], lhsT=c["ones_col_f"][:], rhs=SEXP[:], start=True, stop=True)
            ctx.op("dve", "tensor_mul", out=SEXP[:], in0=psf[5][:, 0:TT], in1=VC[h][:])
            ctx.op("pe", "matmul", out=psf[2][0:1, 0:TT], lhsT=c["ones_col_f"][:], rhs=SEXP[:], start=True, stop=True)
            ctx.op("act", "activation", out=SEXP[:], in_=psf[5][:, 0:TT], func=AF.Square)
            ctx.op("pe", "matmul", out=psf[3][0:1, 0:TT], lhsT=c["ones_col_f"][:], rhs=SEXP[:], start=True, stop=True)
            ctx.op("act", "activation", out=SEXP[:], in_=VC[h][:], func=AF.Square)
            ctx.op("pe", "matmul", out=psf[4][0:1, 0:TT], lhsT=c["ones_col_f"][:], rhs=SEXP[:], start=True, stop=True)
            ctx.op("act", "activation", out=r_l2, in_=psf[0][0:1, 0:TT], func=AF.Sqrt, bias=c["cv"][0:1, 2:3])
            ctx.op("dve", "tensor_copy", out=r_l1, in_=psf[1][0:1, 0:TT])
            ctx.op("act", "activation", out=r_t1, in_=psf[3][0:1, 0:TT], func=AF.Sqrt, bias=c["cv"][0:1, 2:3])
            ctx.op("act", "activation", out=r_t2, in_=psf[4][0:1, 0:TT], func=AF.Sqrt, bias=c["cv"][0:1, 2:3])
            ctx.op("dve", "tensor_mul", out=r_t1, in0=r_t1, in1=r_t2)
            ctx.op("dve", "tensor_scalar_add", out=r_t1, in0=r_t1, scalar1=1e-6)
            ctx.op("dve", "reciprocal", out=r_t1, in_=r_t1)
            ctx.op("dve", "tensor_tensor", out=r_cos, in0=psf[2][0:1, 0:TT], in1=r_t1, op=ALU.mult)
            ctx.op("dve", "tensor_scalar", out=r_cos, in0=r_cos, scalar1=-1.0, scalar2=1.0,
                   op0=ALU.mult, op1=ALU.add)
            for mi2, (w1, w2col, bcol, outr, actf) in enumerate(
                    ((W1B, 0, 0, r_beta, AF.Sigmoid), (W1A, 1, 1, r_amp, AF.Identity))):
                for f, srow_ in enumerate((r_l2, r_l1, r_cos)):
                    ctx.op("pe", "matmul", out=psf[5][0:32, 0:TT], lhsT=w1[0:1, f * 32:(f + 1) * 32],
                           rhs=srow_, start=(f == 0), stop=(f == 2))
                ctx.op("act", "activation", out=H1B[:], in_=psf[5][0:32, 0:TT], func=AF.Silu,
                       bias=BB1[:, bcol:bcol + 1])
                ctx.op("pe", "matmul", out=psf[5][0:1, 0:TT], lhsT=MB2[:, w2col:w2col + 1], rhs=H1B[:],
                       start=True, stop=True)
                ctx.op("act", "activation", out=outr, in_=psf[5][0:1, 0:TT], func=actf,
                       bias=MLPR[0:1, 384 + w2col:385 + w2col])
            for cc2 in range(NCH):
                ctx.op("pe", "transpose", out=psf[5][:, 0:1], in_=r_beta[0:1, cc2 * 128:(cc2 + 1) * 128],
                       identity=c["ident_f"][0:1, 0:1])
                ctx.op("dve", "tensor_copy", out=BCOL[:, h * 4 + cc2:h * 4 + cc2 + 1], in_=psf[5][:, 0:1])
            ctx.op("act", "activation", out=SEXP[:], in_=KC[h][:], func=AF.Square)
            ctx.op("pe", "matmul", out=psf[0][0:1, 0:TT], lhsT=c["ones_col_f"][:], rhs=SEXP[:], start=True, stop=True)
            ctx.op("act", "activation", out=r_t1, in_=psf[0][0:1, 0:TT], func=AF.Sqrt)
            ctx.op("dve", "tensor_scalar_max", out=r_t1, in0=r_t1, scalar1=1e-12)
            ctx.op("dve", "reciprocal", out=r_t1, in_=r_t1)
            ctx.op("dve", "tensor_mul", out=r_t2, in0=r_t1, in1=r_amp)
            ctx.op("pe", "matmul", out=psf[1][:, 0:TT], lhsT=c["ones_row_f"][:], rhs=r_t2, start=True, stop=True)
            ctx.op("dve", "tensor_mul", out=G1[:], in0=KC[h][:], in1=psf[1][:, 0:TT])
            ctx.op("act", "activation", out=r_t3, in_=psf[0][0:1, 0:TT], func=AF.Sqrt, bias=c["cv"][0:1, 2:3])
            ctx.op("dve", "reciprocal", out=r_t3, in_=r_t3)
            ctx.op("pe", "matmul", out=psf[1][:, 0:TT], lhsT=c["ones_row_f"][:], rhs=r_t3, start=True, stop=True)
            ctx.op("dve", "tensor_mul", out=KC[h][:], in0=KC[h][:], in1=psf[1][:, 0:TT])
            ctx.op("act", "activation", out=SEXP[:], in_=QC[h][:], func=AF.Square)
            ctx.op("pe", "matmul", out=psf[0][0:1, 0:TT], lhsT=c["ones_col_f"][:], rhs=SEXP[:], start=True, stop=True)
            ctx.op("act", "activation", out=r_t3, in_=psf[0][0:1, 0:TT], func=AF.Sqrt, bias=c["cv"][0:1, 2:3])
            ctx.op("dve", "reciprocal", out=r_t3, in_=r_t3)
            ctx.op("dve", "tensor_scalar_mul", out=r_t3, in0=r_t3, scalar1=128.0 ** -0.5)
            ctx.op("pe", "matmul", out=psf[1][:, 0:TT], lhsT=c["ones_row_f"][:], rhs=r_t3, start=True, stop=True)
            ctx.op("dve", "tensor_mul", out=QC[h][:], in0=QC[h][:], in1=psf[1][:, 0:TT])
            ctx.op("dve", "tensor_scalar_add", out=G1[:], in0=G1[:], scalar1=DTBC[:, h:h + 1])
            ctx.op("act", "activation", out=SEXP[:], in_=G1[:], func=AF.Relu)
            ctx.op("act", "activation", out=G1[:], in_=G1[:], func=AF.Abs)
            ctx.op("act", "activation", out=G1[:], in_=G1[:], func=AF.Exp, scale=-1.0)
            ctx.op("act", "activation", out=G1[:], in_=G1[:], func=AF.Ln, bias=c["cv"][:, 6:7])
            ctx.op("dve", "tensor_add", out=G1[:], in0=G1[:], in1=SEXP[:])
            ctx.op("dve", "tensor_scalar_mul", out=G1[:], in0=G1[:], scalar1=NEAC[:, h:h + 1])
            emit_cumsum128(ctx, c, psf[2], psf[5], G1, BQ, 128, TT, R1)
            bq4 = BQ[:].rearrange("p (c x) -> p c x", c=NCH)
            ctx.op("dve", "memset", MBALL[:].rearrange("p (c x) -> p c x", c=NCH)[:, :, 0:16], 0.0)
            bq4i = BQ[:].rearrange("p (c i t) -> p c i t", c=NCH, i=8)
            ctx.op("dve", "tensor_copy",
                   out=MBALL[:].rearrange("p (c i t) -> p c i t", c=NCH, i=8)[:, :, 1:8, :],
                   in_=bq4i[:, :, 0:7, 15:16].broadcast_to([128, NCH, 7, 16]))
            ctx.op("dve", "tensor_sub", out=SEXP[:], in0=BQ[:], in1=MBALL[:])
            ctx.op("act", "activation", out=SEXP[:], in_=SEXP[:], func=AF.Exp)
            ctx.op("dve", "tensor_mul", out=UQ[:, 0:TT], in0=SEXP[:], in1=KC[h][:])
            ctx.op("dve", "tensor_mul", out=UQ[:, TT:2 * TT], in0=SEXP[:], in1=QC[h][:])
            ctx.op("act", "activation", out=SEXP[:], in_=BQ[:], func=AF.Exp)
            ctx.op("dve", "tensor_mul", out=UC[:], in0=SEXP[:], in1=KC[h][:])
            ctx.op("dve", "tensor_mul", out=QCC[:], in0=SEXP[:], in1=QC[h][:])
            ctx.op("dve", "tensor_tensor", out=SEXP[:],
                   in0=bq4[:, :, 127:128].broadcast_to([128, NCH, 128]), in1=BQ[:].rearrange("p (c x) -> p c x", c=NCH),
                   op=ALU.subtract)
            ctx.op("act", "activation", out=SEXP[:], in_=SEXP[:], func=AF.Exp)
            ctx.op("dve", "tensor_mul", out=WP[:], in0=SEXP[:], in1=KC[h][:])
            tpp = [psf[4][:, 0:256], psf[4][:, 256:512], psf[3][:, 0:256], psf[3][:, 256:512]]
            ctx.op("dve", "memset", psf[4][:, 0:512], 0.0)
            ctx.op("dve", "memset", psf[3][:, 0:512], 0.0)
            for I in range(1, 8):
                w = 16 * I
                ctx.op("dve", "tensor_tensor", out=sc["tmp"][0][:, 0:NCH * w].rearrange("p (c x) -> p c x", c=NCH),
                       in0=bq4[:, :, w - 1:w].broadcast_to([128, NCH, w]), in1=bq4[:, :, 0:w],
                       op=ALU.subtract)
                ctx.op("act", "activation", out=sc["tmp"][0][:, 0:NCH * w], in_=sc["tmp"][0][:, 0:NCH * w], func=AF.Exp)
                ctx.op("dve", "tensor_tensor", out=WBL[0][:, 0:NCH * w].rearrange("p (c x) -> p c x", c=NCH),
                       in0=sc["tmp"][0][:, 0:NCH * w].rearrange("p (c x) -> p c x", c=NCH),
                       in1=KC[h][:].rearrange("p (c x) -> p c x", c=NCH)[:, :, 0:w], op=ALU.mult)
                for cc in range(NCH):
                    ctx.op("pe", "matmul", out=tpp[cc][0:w, I * 32:I * 32 + 32],
                           lhsT=WBL[0][:, 0:NCH * w].rearrange("p (c x) -> p c x", c=NCH)[:, cc, :],
                           rhs=UQ[:].rearrange("p (u x) -> p u x", u=2)[:, :, cc * 128 + w:cc * 128 + w + 16],
                           start=True, stop=True)
            if dbg and it == 0 and h == 0:
                ctx.op("sp", "dma_start", out=dBQ[:, tsl], in_=BQ[:])
                ctx.op("pool", "dma_start", out=dKN[:], in_=KC[h][:])
                ctx.op("pool", "dma_start", out=dQN[:], in_=QC[h][:])
            for cc in range(NCH):
                cs = slice(cc * 128, (cc + 1) * 128)
                bqc = BQ[:, cs].rearrange("p (i t) -> p i t", i=8)
                kcc = KC[h][:, cs].rearrange("p (i t) -> p i t", i=8)
                qcc2 = QC[h][:, cs].rearrange("p (i t) -> p i t", i=8)
                ctx.op("dve", "tensor_tensor", out=DPAIR[0][:].rearrange("p (i s t) -> p i s t", i=8, s=16),
                       in0=bqc.unsqueeze(2).broadcast_to([128, 8, 16, 16]),
                       in1=bqc.unsqueeze(3).broadcast_to([128, 8, 16, 16]), op=ALU.subtract)
                ctx.op("dve", "tensor_scalar_min", out=DPAIR[0][:], in0=DPAIR[0][:], scalar1=0.0)
                ctx.op("act", "activation", out=DPAIR[0][:], in_=DPAIR[0][:], func=AF.Exp)
                ctx.op("dve", "tensor_tensor", out=KS1[:].rearrange("p (i s t) -> p i s t", i=8, s=16),
                       in0=DPAIR[0][:].rearrange("p (i s t) -> p i s t", i=8, s=16),
                       in1=kcc.unsqueeze(3).broadcast_to([128, 8, 16, 16]), op=ALU.mult)
                ctx.op("dve", "tensor_tensor", out=TPB[:].rearrange("p (i s t) -> p i s t", i=8, s=16),
                       in0=KS1[:].rearrange("p (i s t) -> p i s t", i=8, s=16),
                       in1=kcc.unsqueeze(2).broadcast_to([128, 8, 16, 16]), op=ALU.mult)
                ctx.op("dve", "tensor_tensor", out=PPB[:].rearrange("p (i s t) -> p i s t", i=8, s=16),
                       in0=KS1[:].rearrange("p (i s t) -> p i s t", i=8, s=16),
                       in1=qcc2.unsqueeze(2).broadcast_to([128, 8, 16, 16]), op=ALU.mult)
                for src_, dst_ in ((TPB, TD), (PPB, PD)):
                    for seg in range(4):
                        ctx.op("pe", "matmul", out=psf[seg % 2][0:1, 0:512], lhsT=c["ones_col_f"][:],
                               rhs=src_[:, seg * 512:(seg + 1) * 512], start=True, stop=True)
                        ctx.op("dve", "tensor_copy", out=TROW[0:1, seg * 512:(seg + 1) * 512],
                               in_=psf[seg % 2][0:1, 0:512])
                    ctx.op("act", "dma_start", out=dst_[:], in_=TROW[:])
                tdm = TD[:].unsqueeze(1).broadcast_to([128, 8, 16])
                pdm = PD[:].unsqueeze(1).broadcast_to([128, 8, 16])
                ctx.op("dve", "tensor_mul", out=R1[:].rearrange("p (i t) -> p i t", i=8),
                       in0=tdm, in1=c["tblk"][:].rearrange("p (i t) -> p i t", i=8))
                ctx.op("dve", "tensor_tensor", out=TS[:].rearrange("p (i t) -> p i t", i=8),
                       in0=R1[:].rearrange("p (i t) -> p i t", i=8),
                       in1=tpp[cc].rearrange("p (i u t) -> p i u t", i=8, u=2)[:, :, 0, :],
                       op=ALU.add)
                ctx.op("dve", "tensor_mul", out=R1[:].rearrange("p (i t) -> p i t", i=8),
                       in0=pdm, in1=c["pblk"][:].rearrange("p (i t) -> p i t", i=8))
                ctx.op("dve", "tensor_tensor", out=PS_[:].rearrange("p (i t) -> p i t", i=8),
                       in0=R1[:].rearrange("p (i t) -> p i t", i=8),
                       in1=tpp[cc].rearrange("p (i u t) -> p i u t", i=8, u=2)[:, :, 1, :],
                       op=ALU.add)
                if dbg and it == 0 and h == 0 and cc == 0:
                    ctx.op("sp", "dma_start", out=dTS[:], in_=TS[:])
                    ctx.op("sp", "dma_start", out=dPS[:], in_=PS_[:])
                if dbg and it == 0 and h == 0:
                    ctx.op("sp", "dma_start", out=dTS4[:, cc * 128:(cc + 1) * 128], in_=TS[:])
                ctx.op("pe", "transpose", out=psf[2][:, 0:128], in_=TS[:], identity=c["ident_f"][:])
                ctx.op("dve", "tensor_scalar", out=AB_[:], in0=psf[2][:, 0:128],
                       scalar1=BCOL[:, h * 4 + cc:h * 4 + cc + 1], scalar2=-1.0,
                       op0=ALU.mult, op1=ALU.mult)
                ctx.op("pe", "transpose", out=psf[2][:, 128:256], in_=AB_[:], identity=c["ident_f"][:])
                ctx.op("dve", "tensor_copy", out=ABT[:], in_=psf[2][:, 128:256])
                ctx.op("pe", "matmul", out=psf[5][:, 0:128], lhsT=UC[:, cs],
                       rhs=S[:, h * 128:(h + 1) * 128], start=True, stop=True)
                ctx.op("pe", "transpose", out=psb[0][:, 256:384], in_=VC[h][:, cs], identity=c["ident_b"][:])
                ctx.op("dve", "tensor_copy", out=VTS[:], in_=psb[0][:, 256:384])
                ctx.op("dve", "tensor_sub", out=R1[:], in0=VTS[:], in1=psf[5][:, 0:128])
                ctx.op("dve", "tensor_scalar_mul", out=XDF[:], in0=R1[:],
                       scalar1=BCOL[:, h * 4 + cc:h * 4 + cc + 1])
                if dbg and it == 0 and h == 0:
                    ctx.op("sp", "dma_start", out=dRH[:, cc * 128:(cc + 1) * 128], in_=XDF[:])
                apw, apwt = AB_, ABT
                for lev in range(7):
                    ctx.op("pe", "matmul", out=psf[5][:, 128:256], lhsT=apwt[:], rhs=XDF[:],
                           start=True, stop=True)
                    ctx.op("dve", "tensor_add", out=XDF[:], in0=XDF[:], in1=psf[5][:, 128:256])
                    if lev < 6:
                        nb = APW[lev % 2]
                        nbt = APWT[lev % 2]
                        ctx.op("pe", "matmul", out=psf[5][:, 256:384], lhsT=apwt[:], rhs=apw[:],
                               start=True, stop=True)
                        ctx.op("dve", "tensor_copy", out=nb[:], in_=psf[5][:, 256:384])
                        ctx.op("pe", "transpose", out=psf[2][:, 256:384], in_=nb[:], identity=c["ident_f"][:])
                        ctx.op("dve", "tensor_copy", out=nbt[:], in_=psf[2][:, 256:384])
                        apw, apwt = nb, nbt
                if dbg and it == 0 and h == 0 and cc == 0:
                    ctx.op("sp", "dma_start", out=dXD[:], in_=XDF[:])
                ctx.op("pe", "matmul", out=psf[5][:, 0:128], lhsT=QCC[:, cs],
                       rhs=S[:, h * 128:(h + 1) * 128], start=True, stop=False)
                ctx.op("pe", "matmul", out=psf[5][:, 0:128], lhsT=PS_[:], rhs=XDF[:],
                       start=False, stop=True)
                if dbg and it == 0 and h == 0:
                    ctx.op("dve", "tensor_copy", out=ONF[:], in_=psf[5][:, 0:128])
                    ctx.op("sp", "dma_start", out=dO4[:, cc * 128:(cc + 1) * 128], in_=ONF[:])
                ctx.op("act", "activation", out=ONB[:], in_=psf[5][:, 0:128], func=AF.Square)
                ctx.op("dve", "tensor_reduce", out=OCOL[:, 0:1], in_=ONB[:], axis=AX.X, op=ALU.add)
                ctx.op("act", "activation", out=OCOL[:, 1:2], in_=OCOL[:, 0:1], func=AF.Sqrt,
                       scale=1.0 / 128, bias=c["cv"][:, 0:1])
                ctx.op("dve", "reciprocal", out=OCOL[:, 1:2], in_=OCOL[:, 1:2])
                ctx.op("dve", "tensor_scalar_mul", out=ONF[:], in0=psf[5][:, 0:128],
                       scalar1=OCOL[:, 1:2])
                ctx.op("dve", "tensor_mul", out=ONB[:], in0=ONF[:], in1=NW[:])
                ctx.op("pe", "transpose", out=psb[1][:, 256:384], in_=ONB[:], identity=c["ident_b"][:])
                ctx.op("dve", "tensor_copy", out=OD[h][:, cs], in_=psb[1][:, 256:384])
                ctx.op("pe", "transpose", out=psf[2][:, 384:512], in_=WP[:, cs], identity=c["ident_f"][:])
                ctx.op("dve", "tensor_copy", out=VTS[:], in_=psf[2][:, 384:512])
                ctx.op("pe", "matmul", out=psf[5][:, 128:256], lhsT=VTS[:], rhs=XDF[:],
                       start=True, stop=True)
                ctx.op("act", "activation", out=OCOL[:, 2:3], in_=BQ[:, cc * 128 + 127:cc * 128 + 128],
                       func=AF.Exp)
                ctx.op("dve", "scalar_tensor_tensor", out=S[:, h * 128:(h + 1) * 128],
                       in0=S[:, h * 128:(h + 1) * 128], scalar=OCOL[:, 2:3],
                       in1=psf[5][:, 128:256], op0=ALU.mult, op1=ALU.add)
                if dbg and it == 0 and h == 0:
                    ctx.op("sp", "dma_start", out=dSS[:, cc * 128:(cc + 1) * 128],
                           in_=S[:, 0:128])
                    ctx.op("sp", "dma_start", out=dXD4[:, cc * 128:(cc + 1) * 128], in_=XDF[:])

        if dbg and it == 0:
            ctx.op("sp", "dma_start", out=dBC[:], in_=BCOL[:])
            for ch in range(16):
                ctx.op("pool", "dma_start", out=dOD[ch * 128:(ch + 1) * 128, tsl], in_=OD[ch][:])
        for j in range(4):
            wt = WQ[j % 2]
            ctx.op("pool", "dma_start", out=wt[:], in_=wg2T[:, j * 512:(j + 1) * 512])
            for oc in range(4):
                ch = j * 4 + oc
                ctx.op("pe", "matmul", out=psf[1 + (ch % 2)][:, 0:TT], lhsT=wt[:, oc * 128:(oc + 1) * 128],
                       rhs=G1GATE[:], start=True, stop=True)
                ctx.op("act", "activation", out=SGT[:], in_=psf[1 + (ch % 2)][:, 0:TT],
                       func=AF.Sigmoid, bias=BG2C[:, ch:ch + 1])
                ctx.op("dve", "tensor_mul", out=OD[ch][:], in0=OD[ch][:], in1=SGT[:])

        for j in range(4):
            emit_slab_mm(ctx, psf, WQ, woT, j * 512, OD, TT, queue="pool")
            for oc in range(4):
                i = j * 4 + oc
                ctx.op("dve", "tensor_copy", out=FO[i][:], in_=psf[oc][:, 0:TT])

        emit_ln_stream_add_f32(ctx, c, sc, psf[4], psf[5], psf[3], FO, [SEXP, G1], sc["tmp"], rx, ox, tsl, TT)
        t0 += TTt

    ctx.prefix = ""
    ctx.paranoid_flag = False
    ctx.pop_scope()
    if standalone:
        ctx.finish()
    return nc


def emit_ln_stream_add_f32(ctx, c, sc, ps_stat, ps_b0, ps_b1, FO, FSQ, XR2, rx, ox, tsl, TT):
    row = sc["row"]
    m_row = row[0:1, 0:TT]
    r_row = row[0:1, TT:2 * TT]
    t_row = row[0:1, 2 * TT:3 * TT]
    for i in range(NC):
        b = i % 2
        ctx.op("act", "activation", out=FSQ[b][:], in_=FO[i][:], func=AF.Square)
        ctx.op("pe", "matmul", out=ps_stat[0:1, 0:TT], lhsT=c["ones_col_f"][:], rhs=FO[i][:],
               start=(i == 0), stop=(i == NC - 1))
        ctx.op("pe", "matmul", out=ps_b0[0:1, 0:TT], lhsT=c["ones_col_f"][:], rhs=FSQ[b][:],
               start=(i == 0), stop=(i == NC - 1))
    ctx.op("dve", "tensor_scalar_mul", out=m_row, in0=ps_stat[0:1, 0:TT], scalar1=1.0 / D)
    ctx.op("act", "activation", out=t_row, in_=m_row, func=AF.Square)
    ctx.op("dve", "scalar_tensor_tensor", out=r_row, in0=ps_b0[0:1, 0:TT], scalar=1.0 / D,
           in1=t_row, op0=ALU.mult, op1=ALU.subtract)
    ctx.op("act", "activation", out=r_row, in_=r_row, func=AF.Sqrt, bias=c["cv"][0:1, 0:1])
    ctx.op("dve", "reciprocal", out=r_row, in_=r_row)
    ctx.op("pe", "matmul", out=ps_b0[:, 0:TT], lhsT=c["ones_row_f"][:], rhs=m_row, start=True, stop=True)
    ctx.op("pe", "matmul", out=ps_b1[:, 0:TT], lhsT=c["ones_row_f"][:], rhs=r_row, start=True, stop=True)
    for i in range(NC):
        b = i % 2
        ctx.op("sp", "dma_start", out=XR2[b][:], in_=rx[i * 128:(i + 1) * 128, tsl])
        ctx.op("dve", "tensor_sub", out=FO[i][:], in0=FO[i][:], in1=ps_b0[:, 0:TT])
        ctx.op("dve", "tensor_mul", out=FO[i][:], in0=FO[i][:], in1=ps_b1[:, 0:TT])
        ctx.op("dve", "tensor_add", out=FO[i][:], in0=FO[i][:], in1=XR2[b][:])
        ctx.op("sp", "dma_start", out=ox[i * 128:(i + 1) * 128, tsl], in_=FO[i][:])


def build_full(TT, tiles):
    nc = bass.Bass("TRN2", target_bir_lowering=False, debug=False)
    T = sum(tiles)
    rx = nc.dram_tensor("rx", [D, T], F32, kind="ExternalInput").ap()
    r1 = nc.dram_tensor("resid1", [D, T], F32, kind="Internal").ap()
    r2 = nc.dram_tensor("resid2", [D, T], F32, kind="Internal").ap()
    r3 = nc.dram_tensor("resid3", [D, T], F32, kind="Internal").ap()
    ox = nc.dram_tensor("ox", [D, T], F32, kind="ExternalOutput").ap()
    ctx = Ctx(nc)
    c = make_consts(ctx)
    build_l0attn(TT, tiles, nc=nc, ctx=ctx, c=c, rx=rx, ox_=r1)
    build_l0ffn(TT, tiles, nc=nc, ctx=ctx, c=c, rx=r1, ox_=r2)
    build_l1attn(TT, tiles, nc=nc, ctx=ctx, c=c, rx=r2, ox_=r3)
    build_l1ffn(TT, tiles, nc=nc, ctx=ctx, c=c, rx=r3, ox_=ox)
    ctx.finish()
    return nc



def _dmajor(x):
    return np.ascontiguousarray(x.T)


_PROGS = {}


def _progs(TT, tiles):
    key = (TT, tuple(tiles))
    if key not in _PROGS:
        _PROGS[key] = build_full(TT, tiles)
    return _PROGS[key]


def kernel(**inputs):
    TT = 512
    tiles = [TT] * 4
    ncf = _progs(TT, tiles)
    cons = consts_host()
    ct = lambda a: np.ascontiguousarray(a.astype(np.float32))
    g = lambda k: inputs[k]

    w0 = {"W_inT": ct(g("L0_attn_W_in").T),
          "W_outT": ct(g("L0_attn_W_out").T),
          "conv_w": ct(g("L0_attn_conv_w")[:, 0, :].reshape(34, 128, 4).transpose(1, 0, 2).reshape(128, 136)),
          "conv_b": ct(g("L0_attn_conv_b").reshape(34, 128).T),
          "dt_bias": ct(g("L0_attn_dt_bias")[:, None]),
          "A_log": ct(g("L0_attn_A_log")[:, None]),
          "D_vec": ct(np.repeat(g("L0_attn_D"), 64).reshape(32, 128).T),
          "norm_w": ct(g("L0_attn_norm_w").reshape(32, 128).T),
          "consts_pack": cons}
    w1 = {"w_guT": ct(g("L0_ffn_w_gate_up").T),
          "w_pT": ct(g("L0_ffn_w_proj").T),
          "consts_pack": cons}
    mlp = np.zeros((1, 464), np.float32)
    mlp[0, 0:96] = g("L1_attn_Wb1").T.reshape(-1)
    mlp[0, 96:192] = g("L1_attn_Wa1").T.reshape(-1)
    mlp[0, 192:224] = g("L1_attn_Wb2")[0]
    mlp[0, 224:256] = g("L1_attn_Wa2")[0]
    mlp[0, 256:288] = g("L1_attn_bb1")
    mlp[0, 288:320] = g("L1_attn_ba1")
    mlp[0, 384] = g("L1_attn_bb2")[0]
    mlp[0, 385] = g("L1_attn_ba2")[0]
    w2 = {"WqT": ct(g("L1_attn_Wq").T), "WkT": ct(g("L1_attn_Wk").T),
          "WvT": ct(g("L1_attn_Wv").T), "WoT": ct(g("L1_attn_Wo").T),
          "Wg1T": ct(g("L1_attn_Wg1").T), "Wg2T": ct(g("L1_attn_Wg2").T),
          "WsvT": ct(g("L1_attn_Wsv").T),
          "conv_q": ct(g("L1_attn_conv_q").reshape(16, 128, 4).transpose(1, 0, 2).reshape(128, 64)),
          "conv_k": ct(g("L1_attn_conv_k").reshape(16, 128, 4).transpose(1, 0, 2).reshape(128, 64)),
          "conv_v": ct(g("L1_attn_conv_v").reshape(16, 128, 4).transpose(1, 0, 2).reshape(128, 64)),
          "A_exp": ct(np.repeat(g("L1_attn_A_log"), 128).reshape(16, 128).T),
          "dt_bias1": ct(g("L1_attn_dt_bias").reshape(16, 128).T),
          "bg2": ct(g("L1_attn_bg2").reshape(16, 128).T),
          "norm_w_row": ct(g("L1_attn_norm_w")[None, :]),
          "mlp_w": mlp,
          "mb2_w": ct(np.stack([g("L1_attn_Wb2")[0], g("L1_attn_Wa2")[0]], 1)),
          "bb1_w": ct(np.stack([g("L1_attn_bb1"), g("L1_attn_ba1")], 1)),
          "consts_pack": cons}
    w3 = {"W_upT": ct(g("L1_ffn_W_up").T), "W_gateT": ct(g("L1_ffn_W_gate").T),
          "W_downT": ct(g("L1_ffn_W_down").T), "consts_pack": cons}

    x = g("L0_attn_u")
    wall = {}
    for wm in (w0, w1, w2, w3):
        wall.update(wm)
    in_maps = [dict(wall, rx=_dmajor(x[i])) for i in range(NCORES)]
    res = run_bass_kernel_spmd(ncf, in_maps, core_ids=list(range(NCORES)))
    return np.stack([res.results[i]["ox"].T for i in range(NCORES)]).astype(np.float32)
